# version 14
# baseline (speedup 1.0000x reference)
import numpy as np
from contextlib import ExitStack

import concourse.bass as bass
import concourse.mybir as mybir
from concourse.bass_utils import run_bass_kernel_spmd

F32 = mybir.dt.float32
F16 = mybir.dt.float16
I32 = mybir.dt.int32
AF = mybir.ActivationFunctionType
ALU = mybir.AluOpType
AX = mybir.AxisListType

B_SZ, T, TH, H = 4, 2048, 1024, 4096
DIN, DH, ND, KC = 8192, 4096, 32, 32
NST, R = 16, 256
EPS_ATTN, EPS_LN, RMS_EPS = 1e-6, 1e-5, 1.1920929e-07
SINK_ITERS, SINK_TAU = 50, 0.1
TS = 512
TSP = TS + 1

ENGS = ("sync", "vector", "scalar", "tensor", "gpsimd")
UPTO = 99
FFN_STAGE = 99
DEBUG = False


class Sem:
    def __init__(self, nc, name):
        self.h = nc.ctx.enter_context(nc.semaphore(name))
        self.n = 0


class Prog:
    def __init__(self):
        self.ops = {e: [] for e in ENGS}

    def emit(self, eng, fn, waits=(), inc=None, inc_k=1):
        w = tuple((s.h, v) for (s, v) in waits)

        def run(e, fn=fn, w=w, inc=inc, inc_k=inc_k):
            for sh, v in w:
                e.wait_ge(sh, v)
            ins = fn(e)
            if inc is not None:
                ins.then_inc(inc.h, inc_k)
        self.ops[eng].append(run)
        if inc is not None:
            inc.n += inc_k

    def wait(self, eng, *sems_vals):
        w = tuple((s.h, v) for (s, v) in sems_vals)

        def run(e, w=w):
            for sh, v in w:
                e.wait_ge(sh, v)
        self.ops[eng].append(run)



def _cp(e, out, in_):
    if hasattr(e, "tensor_copy"):
        return e.tensor_copy(out, in_)
    return e.copy(out, in_)

def build_nc():
    nc = bass.Bass("TRN2", target_bir_lowering=False)
    ctx = ExitStack()
    E = ctx.enter_context

    def din(name, shape, dt=F32):
        return nc.dram_tensor(name, list(shape), dt, kind="ExternalInput").ap()

    x_b = din("x_b", [T, H])
    x_mine = din("x_mine", [TH, H])
    wxz = din("wxz", [2 * ND, 128, KC, 128])
    wxpT = din("wxpT", [128, ND, R + 2 * NST])
    wdtT = din("wdtT", [128, 2, DH])
    woT = din("woT", [KC, 128, 2 * ND, 128])
    convw = din("convw", [128, ND * 4])
    convb = din("convb", [128, ND])
    dparam = din("dparam", [128, ND])
    dtbias = din("dtbias", [128, ND])
    alog = din("alog", [128, ND * NST])
    dtlnw = din("dtlnw", [128, 2])
    blnw = din("blnw", [16, 1])
    clnw = din("clnw", [16, 1])
    wresT = din("wresT", [128, KC, 24])
    wbr = din("wbr", [128, 8, 1024])
    hbases = din("hbases", [1, 24])
    gate = din("gate", [1, 1])
    eye4 = din("eye4", [1, 16])
    ident = din("ident", [128, 128], F16)
    ones128c = din("ones128c", [128, 1])
    ones16c = din("ones16c", [16, 1])
    ones1r = din("ones1r", [1, 128])
    oh16 = din("oh16", [16, 16 * 128], F16)
    flagj = din("flagj", [1, 2], I32)

    out = nc.dram_tensor("out", [TH, H], F32, kind="ExternalOutput").ap()
    dbg = {}
    if DEBUG:
        for nm, shape, dt in [("d_xc", [DH, T], F16), ("d_zs", [DH, T], F16),
                              ("d_ssm0", [R + 2 * NST, TH], F32),
                              ("d_ssm1", [R + 2 * NST, TH], F32),
                              ("d_yag0", [DIN, TH], F16), ("d_yag1", [DIN, TH], F16),
                              ("d_sd", [H, TH], F16), ("d_sgT", [TH, H], F16),
                              ("d_hres", [16, TH], F16), ("d_hpre", [16, TH], F16),
                              ("d_hpo", [4, TH], F16), ("d_xpre", [128, 8 * TH], F16),
                              ("d_ybr", [128, 8 * TH], F16), ("d_rs2", [128, 8], F32),
                              ("d_lg", [128, 192], F32), ("d_uu", [128, 32], F32),
                              ("d_vv", [128, 32], F32), ("d_h16", [128, 128], F16),
                              ("d_hpp", [128, 64], F16), ("d_zt", [128, 128], F32),
                              ("d_m1", [128, 32], F32), ("d_s1", [128, 32], F32),
                              ("d_e1", [128, 128], F32), ("d_sb", [128, 48], F32),
                              ("d_sp1", [128, 32], F32), ("d_sp2", [128, 32], F32),
                              ("d_smx", [128, 8], F32),
                              ("d_chi", [128, 512], F16), ("d_clo", [128, 512], F16),
                              ("d_chi0", [128, 512], F16), ("d_clo0", [128, 512], F16),
                              ("d_chi1", [128, 512], F16), ("d_clo1", [128, 512], F16),
                              ("d_da0", [128, 8 * 514], F32), ("d_dbx0", [128, 8 * 514], F16),
                              ("d_h0", [128, 8 * 514], F32),
                              ("d_bb", [128, 8192], F16), ("d_cb", [128, 8192], F16),
                              ("d_u", [128, 512], F16), ("d_da", [128, 8 * 514], F32),
                              ("d_h", [128, 8 * 514], F32), ("d_dbx", [128, 8 * 514], F16)]:
            dbg[nm] = nc.dram_tensor(nm, shape, dt, kind="ExternalOutput").ap()

    xc_d = nc.dram_tensor("xc_d", [DH, T], F16).ap()
    zs_d = nc.dram_tensor("zs_d", [DH, T], F16).ap()
    ssm_in = [nc.dram_tensor(f"ssm_in{t}", [R + 2 * NST, TH], F32).ap() for t in range(2)]
    ssm_out = [nc.dram_tensor(f"ssm_out{t}", [R + 2 * NST, TH], F32).ap() for t in range(2)]
    yag_in = [nc.dram_tensor(f"yag_in{t}", [DH, TH], F16).ap() for t in range(2)]
    yag_out = [nc.dram_tensor(f"yag_out{t}", [DIN, TH], F16).ap() for t in range(2)]
    s_d = nc.dram_tensor("s_d", [H, TH], F16).ap()
    sgT_d = nc.dram_tensor("sgT_d", [TH, H], F16).ap()

    RG = [[0, 1], [2, 3], [4, 5], [6, 7]]

    def sb(name, shape, dt):
        return E(nc.sbuf_tensor(name, list(shape), dt))

    R0 = sb("R0", [128, 33024], F16)
    v_xn = R0[:, 0:KC * TH]
    NH, LEN = 8, TS + 2
    v_da = R0.bitcast(F32)[:, 0:NH * LEN]
    v_dbx = R0[:, 8224:8224 + NH * LEN]
    v_h = R0.bitcast(F32)[:, 6168:6168 + NH * LEN]
    v_p = R0[:, 20560:20560 + NH * LEN]
    v_yres = R0[:, 0:2 * ND * 512]
    v_xnm = R0[:, 0:KC * TH]

    R2 = sb("R2", [128, 24576], F16)
    v_wxp = R2[:, 0:ND * (R + 2 * NST)]
    v_ssm = R2.bitcast(F32)[:, 4800:4800 + 3 * TH]
    v_xt = R2.bitcast(F32)[:, 7872:7872 + 4096]
    v_fin = v_xt
    v_bb = R2[:, 0:NST * TS]
    v_cb = R2[:, NST * TS:2 * NST * TS]
    v_wdt = R2[:, 2 * NST * TS:2 * NST * TS + 2 * DH]
    v_wo = [R2[:, i * (2 * ND * 128):(i + 1) * (2 * ND * 128)] for i in range(2)]
    v_xpre = R2[:, 0:8 * TH]
    v_ybr = R2[:, 8 * TH:16 * TH]
    v_wbr = R2[:, 16 * TH:16 * TH + 8 * 1024]

    R3 = sb("R3", [128, 8192], F16)
    v_w16 = [R3[:, i * (KC * 128):(i + 1) * (KC * 128)] for i in range(2)]
    v_hb = R3[:, 0:5 * TH]
    v_sev = R3[:, 5 * TH:5 * TH + 512]
    v_sgc = R3[:, 6 * TH:7 * TH]


    R5 = sb("R5", [128, 8192], F16)
    v_sq = R5[:, 0:H]
    v_dt2 = R5.bitcast(F32)[:, 0:2 * TH]
    v_nsq = R5.bitcast(F32)[:, 2 * TH:4 * TH]
    v_zt = R5.bitcast(F32)[:, 0:128]
    v_e1 = R5.bitcast(F32)[:, 128:256]
    hpp16 = R5[:, 256:320]
    v_uu = R5.bitcast(F32)[:, 256:288]
    v_vv = R5.bitcast(F32)[:, 288:320]
    v_m1 = R5.bitcast(F32)[:, 320:352]
    v_s1 = R5.bitcast(F32)[:, 352:384]
    v_eyeb = R5.bitcast(F32)[:, 384:400]
    v_baseb = R5.bitcast(F32)[:, 400:424]
    v_gcol = R5.bitcast(F32)[:, 424:425]
    v_omg = R5.bitcast(F32)[:, 425:426]
    v_mx = R5.bitcast(F32)[:, 432:440]
    v_lg = R5.bitcast(F32)[:, 512:704]
    v_hres16 = R5[:, 1704:1832]
    v_lgst16 = R5[:, 4096:4096 + TH]


    R7 = sb("R7", [128, 11992], F16)
    v_dtv = R7[:, 0:TS]
    v_u = R7[:, TS:2 * TS]
    v_sld = R7[:, 10936:10936 + TH]
    v_xcr = [R7[:, 1024 + i * TS:1024 + (i + 1) * TS] for i in range(2)]
    v_tev = R7[:, 1024:1024 + 512]
    v_zsr = [R7[:, 2048 + i * TS:2048 + (i + 1) * TS] for i in range(2)]
    v_ys = R7.bitcast(F32)[:, 1536:1536 + TS]
    v_yf = R7[:, 4096:4096 + TS]
    v_yf2 = R7[:, 4608:4608 + TH]
    v_chi = R7[:, 5632:5632 + ND * NST]
    v_clo = R7[:, 6144:6144 + ND * NST]
    v_spt = R7.bitcast(F32)[:, 3328:3328 + 512]
    v_spt2 = R7.bitcast(F32)[:, 3840:3840 + 512]
    v_xraw = R7[:, 8704:8704 + TH + 8]
    v_xcw = R7[:, 9736:9736 + TH]
    v_ctail = R7[:, 10760:10760 + ND * 4]
    v_stat = R7.bitcast(F32)[:, 5444:5460]
    v_rs2 = R7.bitcast(F32)[:, 5460:5468]
    v_lo32 = R7.bitcast(F32)[:, 5980:5996]
    v_sp1 = R7.bitcast(F32)[:, 3072:3104]
    v_sp2 = R7.bitcast(F32)[:, 3104:3136]
    v_smx = R7.bitcast(F32)[:, 3136:3144]

    R8 = sb("R8", [128, 8192], F16)
    v_braw = R8.bitcast(F32)[:, 0:TH]
    v_craw = R8.bitcast(F32)[:, TH:2 * TH]
    v_r1 = R8.bitcast(F32)[:, 2 * TH:3 * TH]
    v_rb = R8[:, 6 * TH:7 * TH]
    v_rb16 = R8[:, 7 * TH:8 * TH]

    _dtrn = sb("s_dtrn", [128, 2 * TH], F16)
    s_dtrn = [_dtrn, _dtrn]
    s_bn = sb("s_bn", [16, TH], F16)
    s_cn = sb("s_cn", [16, TH], F16)
    s_hpo = sb("s_hpo", [4, TH], F16)
    s_convw = sb("s_convw", [128, ND * 4], F32)
    s_convb = sb("s_convb", [128, ND], F32)
    s_dpar = sb("s_dpar", [128, ND], F32)
    s_dtb = sb("s_dtb", [128, ND], F32)
    s_aneg = sb("s_aneg", [128, ND * NST], F32)
    s_dtln = sb("s_dtln", [128, 2], F32)
    s_bln = sb("s_bln", [16, 1], F32)
    s_cln = sb("s_cln", [16, 1], F32)
    s_wres = sb("s_wres", [128, KC * 24], F16)
    s_id = sb("s_id", [128, 128], F16)
    s_o128c = sb("s_o128c", [128, 1], F32)
    s_o16c = sb("s_o16c", [16, 1], F32)
    s_o1r = sb("s_o1r", [1, 128], F32)
    s_oh16 = sb("s_oh16", [16, 16 * 128], F16)
    s_small = sb("s_small", [1, 64], F32)
    s_flag = sb("s_flag", [1, 2], I32)

    p_x = E(nc.psum_tensor("p_x", [128, 512], F32))
    p_z = E(nc.psum_tensor("p_z", [128, 512], F32))
    p_ssm = E(nc.psum_tensor("p_ssm", [128, 3 * 512], F32))
    p_tp = E(nc.psum_tensor("p_tp", [128, 1024], F16))
    p_dt = E(nc.psum_tensor("p_dt", [128, 512], F32))
    p_bc = E(nc.psum_tensor("p_bc", [128, 512], F32))
    p_o = [p_x, p_z]

    sm = {k: Sem(nc, k) for k in [
        "ldc", "setup", "xld", "lnv", "lna", "tpd", "xnf",
        "wld", "pxz", "evx", "cvd", "sil", "xcd", "psd", "ssa", "ssmd", "cc",
        "nrm", "bcb", "dtm", "spd", "dad", "dbd", "scn", "yrd", "xcr",
        "wod", "yrl", "omm", "oev",
        "ffw", "ffv", "ffa", "ffp", "fgd", "outd", "pl"]}

    P = Prog()

    def dma(dst, src):
        return lambda e: e.dma_start(out=dst, in_=src)

    ld = sm["ldc"]
    for dst, src in [
        (s_convw[:], convw), (s_convb[:], convb), (s_dpar[:], dparam),
        (s_dtb[:], dtbias), (s_aneg[:], alog), (s_dtln[:], dtlnw),
        (s_bln[:], blnw), (s_cln[:], clnw), (s_id[:], ident),
        (s_o128c[:], ones128c), (s_o16c[:], ones16c), (s_o1r[:], ones1r),
        (s_oh16[:], oh16), (s_flag[:], flagj),
        (s_small[0:1, 0:24], hbases), (s_small[0:1, 24:25], gate),
        (s_small[0:1, 32:48], eye4),
    ]:
        P.emit("sync", dma(dst, src), inc=ld, inc_k=16)
    P.emit("gpsimd", dma(v_wxp.rearrange("p (n e) -> p n e", n=ND), wxpT), inc=ld, inc_k=16)
    P.emit("gpsimd", dma(s_wres[:].rearrange("p (n e) -> p n e", n=KC), wresT), inc=ld, inc_k=16)
    NCONST = ld.n

    P.wait("scalar", (ld, NCONST))
    P.emit("scalar", lambda e: e.activation(s_aneg[:], s_aneg[:], AF.Exp), inc=sm["setup"])
    P.wait("vector", (ld, NCONST))
    P.emit("vector", lambda e: e.tensor_scalar_mul(s_aneg[:], s_aneg[:], -1.0),
           waits=[(sm["setup"], 1)], inc=sm["setup"])
    P.emit("vector", lambda e: e.memset(v_chi[:], 0.0), inc=sm["setup"])
    P.emit("vector", lambda e: e.memset(v_clo[:], 0.0), inc=sm["setup"])
    P.emit("vector", lambda e: e.memset(v_ctail[:], 0.0), inc=sm["setup"])
    SETUP_N = sm["setup"].n
    P.wait("tensor", (ld, NCONST))
    P.wait("gpsimd", (ld, NCONST))
    P.wait("sync", (ld, NCONST))

    def phase_LN(th, src_dram, ffn=False):
        if ffn:
            sv, sa, sp = sm["ffv"], sm["ffa"], sm["ffp"]
        else:
            sv, sa, sp = sm["lnv"], sm["lna"], sm["tpd"]
        r0_guard = (sm["pxz"], sm["pxz"].n) if not ffn else (sm["omm"], sm["omm"].n)
        for i in range(8):
            if ffn:
                srcap = src_dram[i * 128:(i + 1) * 128, :]
            else:
                srcap = src_dram[th * TH + i * 128:th * TH + (i + 1) * 128, :]
            w = [(sm["xnf"], sm["xnf"].n), (sm["fgd"], sm["fgd"].n)] if (i > 0 or th > 0 or ffn) else []
            P.emit("sync", dma(v_xt, srcap), waits=w, inc=sm["xld"], inc_k=16)
            xv = sm["xld"].n
            P.emit("vector", lambda e: e.tensor_scalar(
                v_sq, v_xt, 1.0, 0.0, ALU.mult, ALU.add, accum_out=v_stat[:, 0:1]),
                waits=[(sm["xld"], xv), (sa, sa.n)], inc=sv)
            P.emit("scalar", lambda e: e.activation(
                v_sq, v_xt, AF.Square, accum_out=v_stat[:, 1:2]),
                waits=[(sm["xld"], xv), (sv, sv.n)], inc=sa)

            def stats(e):
                e.tensor_scalar_mul(v_stat[:, 0:1], v_stat[:, 0:1], 1.0 / H)
                e.tensor_scalar_mul(v_stat[:, 1:2], v_stat[:, 1:2], 1.0 / H)
                e.tensor_tensor(v_stat[:, 8:9], v_stat[:, 0:1], v_stat[:, 0:1], ALU.mult)
                e.tensor_sub(v_stat[:, 1:2], v_stat[:, 1:2], v_stat[:, 8:9])
                return e.tensor_scalar(v_stat[:, 9:10], v_stat[:, 1:2], 1.0, EPS_LN,
                                       ALU.mult, ALU.add)
            P.emit("vector", stats, waits=[(sv, sv.n), (sa, sa.n)], inc=sv)
            P.emit("scalar", lambda e: e.activation(v_stat[:, 10:11], v_stat[:, 9:10], AF.Sqrt),
                   waits=[(sv, sv.n)], inc=sa)

            def fin_a(e, ffn=ffn):
                ins = e.reciprocal(v_stat[:, 11:12], v_stat[:, 10:11])
                if ffn:
                    e.tensor_tensor(v_stat[:, 12:13], v_stat[:, 1:2], v_stat[:, 11:12], ALU.mult)
                    e.tensor_tensor(v_stat[:, 12:13], v_stat[:, 12:13], v_stat[:, 11:12], ALU.mult)
                    ins = e.tensor_scalar(v_stat[:, 12:13], v_stat[:, 12:13], 1.0, RMS_EPS,
                                          ALU.mult, ALU.add)
                return ins
            P.emit("vector", fin_a, waits=[(sa, sa.n)], inc=sv)
            P.emit("vector", lambda e: e.tensor_scalar(
                v_sq, v_xt, v_stat[:, 0:1], v_stat[:, 11:12], ALU.subtract, ALU.mult),
                waits=[(sv, sv.n)], inc=sv)
            if ffn:
                P.emit("scalar", lambda e: e.activation(v_stat[:, 13:14], v_stat[:, 12:13],
                                                        AF.Sqrt), waits=[(sv, sv.n)], inc=sa)
                P.emit("vector", lambda e, i=i: e.reciprocal(v_rs2[:, i:i + 1], v_stat[:, 13:14]),
                       waits=[(sa, sa.n)], inc=sv)
            lv = sv.n
            for g in range(8):
                def tp(e, g=g):
                    for jj in range(4):
                        kk = g * 4 + jj
                        ins = e.transpose(p_tp[:, jj * 128:(jj + 1) * 128],
                                          v_sq[:, kk * 128:(kk + 1) * 128], s_id[:])
                    return ins
                P.emit("tensor", tp, waits=[(sv, lv), (sm["xnf"], sm["xnf"].n)], inc=sp)
                dst = (v_xnm if ffn else v_xn).rearrange("p (k t) -> p k t", k=KC)[
                    :, g * 4:(g + 1) * 4, i * 128:(i + 1) * 128]
                P.emit("scalar", lambda e, dst=dst: _cp(e, 
                    dst, p_tp[:, 0:512].rearrange("p (k t) -> p k t", k=4)),
                    waits=[(sp, sp.n), r0_guard], inc=sm["xnf"])

    pxz_after_d = {}

    def phase_A(th):
        P.emit("vector", lambda e: e.memset(v_ssm, 0.0),
               waits=[(sm["ssmd"], sm["ssmd"].n)], inc=sm["ssa"])
        ssm_clear = sm["ssa"].n
        xn_ready = sm["xnf"].n
        for d in range(ND):
            wb = v_w16[d % 2]
            key = (th, d - 2) if d >= 2 else ((th - 1, ND - 2 + d) if th > 0 else None)
            wfree = [(sm["pxz"], pxz_after_d[key])] if key in pxz_after_d else []
            P.emit("gpsimd", dma(wb.rearrange("p (k c) -> p k c", k=KC), wxz[d]),
                   waits=wfree + [(sm["xnf"], xn_ready)], inc=sm["wld"], inc_k=16)
            wv = sm["wld"].n
            for ts in range(2):
                def mmx(e, ts=ts, wb=wb):
                    for k in range(KC):
                        ins = e.matmul(p_x[:], wb[:, k * 128:(k + 1) * 128],
                                       v_xn[:, k * TH + ts * 512:k * TH + (ts + 1) * 512],
                                       start=(k == 0), stop=(k == KC - 1))
                    return ins
                P.emit("tensor", mmx, waits=[(sm["wld"], wv), (sm["evx"], sm["evx"].n)],
                       inc=sm["pxz"])
                P.emit("scalar", lambda e, ts=ts: _cp(e, 
                    v_xraw[:, 3 + ts * 512:3 + (ts + 1) * 512], p_x[:]),
                    waits=[(sm["pxz"], sm["pxz"].n), (sm["xcd"], sm["xcd"].n),
                           (sm["cvd"], sm["cvd"].n)], inc=sm["evx"])
            evv = sm["evx"].n
            if th == 0:
                P.emit("vector", lambda e: e.memset(v_xraw[:, 0:3], 0.0),
                       waits=[(sm["evx"], evv), (sm["setup"], SETUP_N)], inc=sm["cvd"])
            else:
                P.emit("vector", lambda e, d=d: _cp(e, 
                    v_xraw[:, 0:3], v_ctail[:, d * 4:d * 4 + 3]),
                    waits=[(sm["evx"], evv)], inc=sm["cvd"])
            P.emit("vector", lambda e, d=d: _cp(e, 
                v_ctail[:, d * 4:d * 4 + 3], v_xraw[:, TH:TH + 3]), inc=sm["cvd"])

            def conv(e, d=d):
                e.tensor_scalar(v_xcw, v_xraw[:, 3:3 + TH],
                                s_convw[:, d * 4 + 3:d * 4 + 4],
                                s_convb[:, d:d + 1], ALU.mult, ALU.add)
                for k in range(3):
                    ins = e.scalar_tensor_tensor(
                        v_xcw, v_xraw[:, k:k + TH], s_convw[:, d * 4 + k:d * 4 + k + 1],
                        v_xcw, ALU.mult, ALU.add)
                return ins
            P.emit("vector", conv, waits=[(sm["xcd"], sm["xcd"].n),
                                          (sm["cvd"], sm["cvd"].n)], inc=sm["cvd"])
            cvv = sm["cvd"].n
            P.emit("scalar", lambda e: e.activation(v_xcw, v_xcw, AF.Silu),
                   waits=[(sm["cvd"], cvv)], inc=sm["sil"])
            silv = sm["sil"].n
            P.emit("sync", lambda e, d=d, th=th: e.dma_start(
                out=xc_d[d * 128:(d + 1) * 128, th * TH:(th + 1) * TH], in_=v_xcw),
                waits=[(sm["sil"], silv)], inc=sm["xcd"], inc_k=16)
            for ts in range(2):
                def mmssm(e, d=d, ts=ts):
                    sl = v_xcw[:, ts * 512:(ts + 1) * 512]
                    e.matmul(p_ssm[:, 0:512], v_wxp[:, d * 288:d * 288 + 128], sl,
                             start=True, stop=True)
                    e.matmul(p_ssm[:, 512:1024], v_wxp[:, d * 288 + 128:d * 288 + 256], sl,
                             start=True, stop=True)
                    return e.matmul(p_ssm[0:32, 1024:1536],
                                    v_wxp[:, d * 288 + 256:d * 288 + 288],
                                    sl, start=True, stop=True)
                P.emit("tensor", mmssm, waits=[(sm["sil"], silv), (sm["ssa"], sm["ssa"].n)],
                       inc=sm["psd"])

                def ssmacc(e, ts=ts):
                    dst = v_ssm.rearrange("p (g t) -> p g t", g=3)[:, :, ts * 512:(ts + 1) * 512]
                    src = p_ssm[:].rearrange("p (g t) -> p g t", g=3)
                    return e.scalar_tensor_tensor(dst, src, 1.0, dst, ALU.mult, ALU.add)
                P.emit("vector", ssmacc, waits=[(sm["psd"], sm["psd"].n),
                                                (sm["ssa"], ssm_clear)], inc=sm["ssa"])
            P.emit("gpsimd", dma(wb.rearrange("p (k c) -> p k c", k=KC), wxz[ND + d]),
                   waits=[(sm["pxz"], sm["pxz"].n)], inc=sm["wld"], inc_k=16)
            wv2 = sm["wld"].n
            for ts in range(2):
                def mmz(e, ts=ts, wb=wb):
                    for k in range(KC):
                        ins = e.matmul(p_z[:], wb[:, k * 128:(k + 1) * 128],
                                       v_xn[:, k * TH + ts * 512:k * TH + (ts + 1) * 512],
                                       start=(k == 0), stop=(k == KC - 1))
                    return ins
                P.emit("tensor", mmz, waits=[(sm["wld"], wv2), (sm["sil"], sm["sil"].n)],
                       inc=sm["pxz"])
                P.emit("scalar", lambda e, ts=ts: e.activation(
                    v_xraw[:, 3 + ts * 512:3 + (ts + 1) * 512], p_z[:], AF.Silu),
                    waits=[(sm["pxz"], sm["pxz"].n), (sm["cvd"], cvv),
                           (sm["xcd"], sm["xcd"].n)], inc=sm["sil"])
            pxz_after_d[(th, d)] = sm["pxz"].n
            P.emit("sync", lambda e, d=d, th=th: e.dma_start(
                out=zs_d[d * 128:(d + 1) * 128, th * TH:(th + 1) * TH],
                in_=v_xraw[:, 3:3 + TH]),
                waits=[(sm["sil"], sm["sil"].n)], inc=sm["xcd"], inc_k=16)
        P.wait("sync", (sm["ssa"], sm["ssa"].n))
        P.emit("sync", lambda e, th=th: e.dma_start(out=ssm_in[th][0:128, :], in_=v_ssm[:, 0:TH]),
               inc=sm["ssmd"], inc_k=16)
        P.emit("sync", lambda e, th=th: e.dma_start(out=ssm_in[th][128:256, :],
                                                    in_=v_ssm[:, TH:2 * TH]),
               inc=sm["ssmd"], inc_k=16)
        P.emit("sync", lambda e, th=th: e.dma_start(out=ssm_in[th][256:288, :],
                                                    in_=v_ssm[0:32, 2 * TH:3 * TH]),
               inc=sm["ssmd"], inc_k=16)
        P.emit("gpsimd", lambda e, th=th: e.collective_compute(
            "AllReduce", ALU.add, replica_groups=RG,
            ins=[ssm_in[th].opt()], outs=[ssm_out[th].opt()]),
            waits=[(sm["ssmd"], sm["ssmd"].n)], inc=sm["cc"])

    def phase_N(th, ccv):
        nr = sm["nrm"]
        P.emit("sync", lambda e, th=th: e.dma_start(
            out=v_dt2[:, 0:TH], in_=ssm_out[th][0:128, :]),
            waits=[(sm["cc"], ccv), (sm["xnf"], sm["xnf"].n)], inc=nr, inc_k=16)
        P.emit("sync", lambda e, th=th: e.dma_start(
            out=v_dt2[:, TH:2 * TH], in_=ssm_out[th][128:256, :]),
            waits=[(sm["cc"], ccv)], inc=nr, inc_k=16)
        P.emit("sync", lambda e, th=th: e.dma_start(out=v_braw[0:16, :],
                                                    in_=ssm_out[th][256:272, :]),
               waits=[(sm["cc"], ccv)], inc=nr, inc_k=16)
        P.emit("sync", lambda e, th=th: e.dma_start(out=v_craw[0:16, :],
                                                    in_=ssm_out[th][272:288, :]),
               waits=[(sm["cc"], ccv)], inc=nr, inc_k=16)
        base = nr.n
        P.emit("scalar", lambda e: e.activation(v_nsq[:, 0:2 * TH], v_dt2, AF.Square),
               waits=[(nr, base)], inc=nr)

        def msum(e):
            last = None
            for c in range(2):
                pp = p_dt if c == 0 else p_bc
                for k in range(2):
                    last = e.matmul(pp[0:1, :], s_o128c[:],
                                    v_nsq[:, k * TH + c * 512:k * TH + (c + 1) * 512],
                                    start=(k == 0), stop=(k == 1))
            return last
        P.emit("tensor", msum, waits=[(nr, nr.n)], inc=nr)

        def rfun(e):
            e.tensor_scalar(v_r1[0:1, 0:512], p_dt[0:1, :], 1.0 / R, EPS_ATTN, ALU.mult, ALU.add)
            return e.tensor_scalar(v_r1[0:1, 512:1024], p_bc[0:1, :], 1.0 / R, EPS_ATTN,
                                   ALU.mult, ALU.add)
        P.emit("vector", rfun, waits=[(nr, nr.n)], inc=nr)
        P.emit("scalar", lambda e: e.activation(v_r1[0:1, :], v_r1[0:1, :], AF.Sqrt),
               waits=[(nr, nr.n)], inc=nr)
        P.emit("vector", lambda e: e.reciprocal(v_r1[0:1, :], v_r1[0:1, :]),
               waits=[(nr, nr.n)], inc=nr)
        rv = nr.n
        for c in range(2):
            P.emit("tensor", lambda e, c=c: e.matmul(
                p_bc[:] if c else p_dt[:], s_o1r[:], v_r1[0:1, c * 512:(c + 1) * 512],
                start=True, stop=True), waits=[(nr, rv)], inc=nr)
            P.emit("scalar", lambda e, c=c: _cp(e, 
                v_rb[:, c * 512:(c + 1) * 512], p_bc[:] if c else p_dt[:]),
                waits=[(nr, nr.n)], inc=nr)
        bv = nr.n

        def dtrn(e, th=th):
            last = None
            for k in range(2):
                last = e.scalar_tensor_tensor(
                    s_dtrn[th][:, k * TH:(k + 1) * TH], v_dt2[:, k * TH:(k + 1) * TH],
                    s_dtln[:, k:k + 1], v_rb, ALU.mult, ALU.mult)
            return last
        P.emit("vector", dtrn, waits=[(nr, bv), (sm["dtm"], sm["dtm"].n)], inc=nr)
        for (raw, lnw, dstn) in ((v_braw, s_bln, s_bn), (v_craw, s_cln, s_cn)):
            P.emit("scalar", lambda e, raw=raw: e.activation(
                v_nsq[0:16, 0:TH], raw[0:16, :], AF.Square), waits=[(nr, nr.n)], inc=nr)

            def bsum(e):
                e.matmul(p_dt[0:1, :], s_o16c[:], v_nsq[0:16, 0:512], start=True, stop=True)
                return e.matmul(p_bc[0:1, :], s_o16c[:], v_nsq[0:16, 512:1024],
                                start=True, stop=True)
            P.emit("tensor", bsum, waits=[(nr, nr.n)], inc=nr)

            def brs(e):
                e.tensor_scalar(v_r1[0:1, 0:512], p_dt[0:1, :], 1.0 / NST, EPS_ATTN,
                                ALU.mult, ALU.add)
                return e.tensor_scalar(v_r1[0:1, 512:1024], p_bc[0:1, :], 1.0 / NST, EPS_ATTN,
                                       ALU.mult, ALU.add)
            P.emit("vector", brs, waits=[(nr, nr.n)], inc=nr)
            P.emit("scalar", lambda e: e.activation(v_r1[0:1, :], v_r1[0:1, :], AF.Sqrt),
                   waits=[(nr, nr.n)], inc=nr)
            P.emit("vector", lambda e: e.reciprocal(v_r1[0:1, :], v_r1[0:1, :]),
                   waits=[(nr, nr.n)], inc=nr)

            def bc16(e):
                e.matmul(p_dt[0:16, :], s_o1r[0:1, 0:16], v_r1[0:1, 0:512],
                         start=True, stop=True)
                return e.matmul(p_bc[0:16, :], s_o1r[0:1, 0:16], v_r1[0:1, 512:1024],
                                start=True, stop=True)
            P.emit("tensor", bc16, waits=[(nr, nr.n)], inc=nr)

            def bevac(e):
                _cp(e, v_rb16[0:16, 0:512], p_dt[0:16, :])
                return _cp(e, v_rb16[0:16, 512:1024], p_bc[0:16, :])
            P.emit("scalar", bevac, waits=[(nr, nr.n)], inc=nr)
            P.emit("vector", lambda e, raw=raw, lnw=lnw, dstn=dstn: e.scalar_tensor_tensor(
                dstn[:], raw[0:16, :], lnw[:], v_rb16[0:16, :], ALU.mult, ALU.mult),
                waits=[(nr, nr.n)], inc=nr)

    xcr_hist = {}
    xcr_marks = []
    cs_pend = [None]
    pend_y = [None]
    cs_slot = 0

    def phase_B(th):
        nonlocal cs_slot
        if th == 0:
            P.emit("gpsimd", dma(v_wdt.rearrange("p (n e) -> p n e", n=2), wdtT),
                   waits=[(sm["pxz"], sm["pxz"].n)], inc=sm["wld"], inc_k=16)
            P.emit("vector", lambda e: e.memset(
                v_da.rearrange("p (n t) -> p n t", n=NH)[:, :, 0:1], 0.0),
                waits=[(sm["pxz"], sm["pxz"].n), (sm["xnf"], sm["xnf"].n)], inc=sm["dbd"])
            P.emit("vector", lambda e: e.memset(
                v_da.rearrange("p (n t) -> p n t", n=NH)[:, :, 1:2], 1.0), inc=sm["dbd"])
        sep_done = sm["dbd"].n
        wdt_v = sm["wld"].n
        for ts in range(2):
            t0 = ts * TS
            for (src, dstb) in ((s_bn, v_bb), (s_cn, v_cb)):
                for n in range(NST):
                    def bcast(e, n=n, src=src, t0=t0):
                        return e.matmul(p_bc[:], s_oh16[:, n * 128:(n + 1) * 128],
                                        src[:, t0:t0 + TS], start=True, stop=True)
                    P.emit("tensor", bcast, waits=[(sm["nrm"], sm["nrm"].n),
                                                   (sm["bcb"], sm["bcb"].n)], inc=sm["bcb"])
                    P.emit("scalar", lambda e, n=n, dstb=dstb: _cp(e, 
                        dstb[:, n * TS:(n + 1) * TS], p_bc[:]),
                        waits=[(sm["bcb"], sm["bcb"].n), (sm["dbd"], sm["dbd"].n),
                               (sm["scn"], sm["scn"].n)], inc=sm["bcb"])
            bcv = sm["bcb"].n
            for d in range(ND):
                wpre = [(sm["scn"], xcr_marks[-1])] if xcr_marks else []
                P.emit("sync", lambda e, d=d, th=th, t0=t0: e.dma_start(
                    out=v_xcr[d % 2],
                    in_=xc_d[d * 128:(d + 1) * 128, th * TH + t0:th * TH + t0 + TS]),
                    waits=wpre, inc=sm["xcr"], inc_k=16)
                P.emit("sync", lambda e, d=d, th=th, t0=t0: e.dma_start(
                    out=v_zsr[d % 2],
                    in_=zs_d[d * 128:(d + 1) * 128, th * TH + t0:th * TH + t0 + TS]),
                    inc=sm["xcr"], inc_k=16)
                xcrv = sm["xcr"].n

                def mmdt(e, d=d, th=th, t0=t0):
                    for k in range(2):
                        ins = e.matmul(
                            p_dt[:], v_wdt[:, k * DH + d * 128:k * DH + (d + 1) * 128],
                            s_dtrn[th][:, k * TH + t0:k * TH + t0 + TS],
                            start=(k == 0), stop=(k == 1))
                    return ins
                P.emit("tensor", mmdt, waits=[(sm["wld"], wdt_v), (sm["nrm"], sm["nrm"].n),
                                              (sm["spd"], sm["spd"].n), (sm["bcb"], bcv)],
                       inc=sm["dtm"])
                dtv_ = sm["dtm"].n

                def sp(e, d=d):
                    e.activation(v_spt[:, 0:512], p_dt[:], AF.Abs, bias=s_dtb[:, d:d + 1])
                    e.activation(v_spt[:, 0:512], v_spt[:, 0:512], AF.Exp, scale=-1.0)
                    e.activation(v_spt[:, 0:512], v_spt[:, 0:512], AF.Ln, bias=1.0)
                    return e.activation(v_spt2[:, 0:512], p_dt[:], AF.Relu,
                                        bias=s_dtb[:, d:d + 1])
                P.emit("scalar", sp, waits=[(sm["dtm"], dtv_), (sm["dad"], sm["dad"].n)],
                       inc=sm["spd"])
                P.emit("vector", lambda e: e.tensor_add(v_dtv, v_spt2[:, 0:512], v_spt[:, 0:512]),
                       waits=[(sm["spd"], sm["spd"].n)], inc=sm["spd"])
                spv = sm["spd"].n
                P.emit("vector", lambda e, d=d: e.tensor_mul(v_u, v_dtv, v_xcr[d % 2]),
                       waits=[(sm["spd"], spv), (sm["xcr"], xcrv)], inc=sm["dbd"])

                def drain(last=False):
                    if pend_y[0] is None:
                        return
                    pd, pnh, plv = pend_y[0]
                    pend_y[0] = None

                    def yredB(e, pd=pd, pnh=pnh):
                        p3 = v_p.rearrange("p (n t) -> p n t", n=NH)
                        e.tensor_add(p3[:, 0:2, 2:LEN], p3[:, 0:2, 2:LEN], p3[:, 2:4, 2:LEN])
                        e.tensor_add(p3[:, 0:1, 2:LEN], p3[:, 0:1, 2:LEN], p3[:, 1:2, 2:LEN])
                        if pnh == 0:
                            return e.tensor_copy(v_ys, p3[:, 0, 2:LEN])
                        return e.tensor_add(v_ys, v_ys, p3[:, 0, 2:LEN])
                    P.emit("vector", yredB, waits=[(sm["pl"], plv)], inc=sm["scn"])
                    if pnh == 1:
                        def yfin(e, pd=pd):
                            e.scalar_tensor_tensor(v_ys, v_xcr[pd % 2], s_dpar[:, pd:pd + 1],
                                                   v_ys, ALU.mult, ALU.add)
                            return e.tensor_mul(v_yf, v_ys, v_zsr[pd % 2])
                        P.emit("vector", yfin, waits=[(sm["yrd"], sm["yrd"].n)], inc=sm["scn"])
                        xcr_marks.append(sm["scn"].n)
                        P.emit("sync", lambda e, pd=pd, th=th, t0=t0: e.dma_start(
                            out=yag_in[th][pd * 128:(pd + 1) * 128, t0:t0 + TS], in_=v_yf),
                            waits=[(sm["scn"], sm["scn"].n)], inc=sm["yrd"], inc_k=16)

                for nh in range(2):
                    def mkda(e, d=d, nh=nh):
                        for nn in range(NH):
                            n = nh * NH + nn
                            ins = e.activation(
                                v_da.rearrange("p (n t) -> p n t", n=NH)[:, nn, 2:LEN],
                                v_dtv, AF.Exp, scale=s_aneg[:, d * NST + n:d * NST + n + 1])
                        return ins
                    P.emit("scalar", mkda, waits=[(sm["spd"], spv), (sm["scn"], sm["scn"].n),
                                                  (sm["setup"], SETUP_N)], inc=sm["dad"])
                    drain()

                    def mkdbx(e, d=d, nh=nh):
                        dst = v_dbx.rearrange("p (n t) -> p n t", n=NH)[:, :, 2:LEN]
                        ub = v_u.unsqueeze(1).broadcast_to([128, NH, TS])
                        bb = v_bb.rearrange("p (n t) -> p n t", n=NST)[:, nh * NH:(nh + 1) * NH, :]
                        e.tensor_tensor(dst, ub, bb, ALU.mult)
                        e.tensor_copy(
                            v_dbx.rearrange("p (n t) -> p n t", n=NH)[:, :, 0],
                            v_chi[:, d * NST + nh * NH:d * NST + nh * NH + NH])
                        return e.tensor_copy(
                            v_dbx.rearrange("p (n t) -> p n t", n=NH)[:, :, 1],
                            v_clo[:, d * NST + nh * NH:d * NST + nh * NH + NH])
                    P.emit("vector", mkdbx, waits=[(sm["bcb"], bcv), (sm["scn"], sm["scn"].n),
                                                   (sm["setup"], SETUP_N),
                                                   (sm["dbd"], sm["dbd"].n)], inc=sm["dbd"])
                    P.emit("vector", lambda e: e.tensor_tensor_scan(
                        v_h, v_da, v_dbx, 0.0, ALU.mult, ALU.add),
                        waits=[(sm["dad"], sm["dad"].n), (sm["dbd"], sep_done),
                               (sm["dbd"], sm["dbd"].n)], inc=sm["scn"])

                    def csaveA(e, d=d, nh=nh, pend=cs_pend[0]):
                        sl = slice(d * NST + nh * NH, d * NST + nh * NH + NH)
                        hl = v_h.rearrange("p (n t) -> p n t", n=NH)[:, :, LEN - 1]
                        ins = e.tensor_copy(v_chi[:, sl], hl)
                        if pend is not None:
                            psl, pslot = pend
                            ins = e.tensor_copy(v_clo[:, psl], v_lo32[:, pslot * NH:(pslot + 1) * NH])
                        return ins
                    P.emit("vector", csaveA, waits=[(sm["scn"], sm["scn"].n)], inc=sm["scn"])
                    cs_slot = (cs_slot + 1) % 2

                    def csaveB(e, d=d, nh=nh, slot=cs_slot):
                        sl = slice(d * NST + nh * NH, d * NST + nh * NH + NH)
                        hl = v_h.rearrange("p (n t) -> p n t", n=NH)[:, :, LEN - 1]
                        return e.tensor_tensor(v_lo32[:, slot * NH:(slot + 1) * NH],
                                               hl, v_chi[:, sl], ALU.subtract)
                    P.emit("vector", csaveB, waits=[(sm["scn"], sm["scn"].n)], inc=sm["scn"])
                    cs_pend[0] = (slice(d * NST + nh * NH, d * NST + nh * NH + NH), cs_slot)

                    def ymulP(e, nh=nh):
                        hv = v_h.rearrange("p (n t) -> p n t", n=NH)[:, :, 2:LEN]
                        p3 = v_p.rearrange("p (n t) -> p n t", n=NH)
                        cb = v_cb.rearrange("p (n t) -> p n t", n=NST)[:, nh * NH:(nh + 1) * NH, :]
                        e.tensor_tensor(p3[:, :, 2:LEN], hv, cb, ALU.mult)
                        return e.tensor_add(p3[:, 0:4, 2:LEN], p3[:, 0:4, 2:LEN],
                                            p3[:, 4:8, 2:LEN])
                    P.emit("gpsimd", ymulP, waits=[(sm["scn"], sm["scn"].n)], inc=sm["pl"])
                    pend_y[0] = (d, nh, sm["pl"].n)
            drain(last=True)
            def csflush(e, pend=cs_pend[0]):
                psl, pslot = pend
                return e.tensor_copy(v_clo[:, psl], v_lo32[:, pslot * NH:(pslot + 1) * NH])
            P.emit("vector", csflush, waits=[(sm["scn"], sm["scn"].n)], inc=sm["scn"])
            cs_pend[0] = None
            if DEBUG and th == 0:
                mids = [(f"d_chi{ts}", v_chi), (f"d_clo{ts}", v_clo)]
                if ts == 0:
                    mids += [("d_da0", v_da), ("d_dbx0", v_dbx), ("d_h0", v_h)]
                for k, (nm, srcT) in enumerate(mids):
                    P.emit("gpsimd", lambda e, nm=nm, srcT=srcT: e.dma_start(
                        out=dbg[nm], in_=srcT),
                        waits=[(sm["scn"], sm["scn"].n)] if k == 0 else [],
                        inc=sm["outd"], inc_k=16)
                P.emit("vector", lambda e: e.tensor_copy(v_dtv, v_dtv),
                       waits=[(sm["outd"], sm["outd"].n)], inc=sm["scn"])
        P.emit("gpsimd", lambda e, th=th: e.collective_compute(
            "AllGather", ALU.bypass, replica_groups=RG,
            ins=[yag_in[th].opt()], outs=[yag_out[th].opt()]),
            waits=[(sm["yrd"], sm["yrd"].n)], inc=sm["cc"])

    omm_hist = {}

    def phase_C(ccv):
        for tb in range(2):
            def ldy(e, tb=tb):
                v = e.value_load(s_flag[0:1, 0:1])
                for half in range(2):
                    src = yag_out[half][:, tb * 512:(tb + 1) * 512].rearrange(
                        "(k p) t -> p k t", p=128)
                    e.dma_start(out=v_yres[:].rearrange("p (k t) -> p k t", k=2 * ND),
                                in_=src, cond=(v == half)).then_inc(sm["yrl"].h, 16)
            w = [(sm["cc"], ccv), (sm["scn"], sm["scn"].n)]
            if tb > 0:
                w.append((sm["omm"], omm_hist[(tb - 1, KC - 1)]))
            wt = tuple((s.h, v) for (s, v) in w)

            def ldy_run(e, ldy=ldy, wt=wt):
                for sh, v in wt:
                    e.wait_ge(sh, v)
                ldy(e)
            P.ops["sync"].append(ldy_run)
            sm["yrl"].n += 16 * 2
            ylv = sm["yrl"].n
            for hch in range(KC):
                wo = v_wo[hch % 2]
                key = (tb, hch - 2) if hch >= 2 else ((tb - 1, KC - 2 + hch) if tb > 0 else None)
                wfree = [(sm["omm"], omm_hist[key])] if key in omm_hist else [
                    (sm["scn"], sm["scn"].n)]
                P.emit("gpsimd", dma(wo.rearrange("p (k c) -> p k c", k=2 * ND), woT[hch]),
                       waits=wfree, inc=sm["wod"], inc_k=16)
                wv = sm["wod"].n

                def mmo(e, wo=wo, hch=hch):
                    pp = p_o[hch % 2]
                    for k in range(2 * ND):
                        ins = e.matmul(pp[:], wo[:, k * 128:(k + 1) * 128],
                                       v_yres[:, k * 512:(k + 1) * 512],
                                       start=(k == 0), stop=(k == 2 * ND - 1))
                    return ins
                P.emit("tensor", mmo, waits=[(sm["wod"], wv), (sm["yrl"], ylv),
                                             (sm["oev"], sm["oev"].n)], inc=sm["omm"])
                omm_hist[(tb, hch)] = sm["omm"].n
                P.emit("scalar", lambda e, hch=hch: _cp(e, v_sev, p_o[hch % 2][:]),
                       waits=[(sm["omm"], sm["omm"].n), (sm["oev"], sm["oev"].n)],
                       inc=sm["oev"])
                P.emit("sync", lambda e, hch=hch, tb=tb: e.dma_start(
                    out=s_d[hch * 128:(hch + 1) * 128, tb * 512:(tb + 1) * 512], in_=v_sev),
                    waits=[(sm["oev"], sm["oev"].n)], inc=sm["oev"], inc_k=16)

    def phase_FFN():
        sv, sa, sp = sm["ffv"], sm["ffa"], sm["ffp"]
        phase_LN(0, x_mine, ffn=True)
        xnv = sm["xnf"].n
        if FFN_STAGE < 2:
            return
        for c in range(2):
            def mml(e, c=c):
                for k in range(KC):
                    ins = e.matmul(p_dt[0:24, :], s_wres[:, k * 24:(k + 1) * 24],
                                   v_xnm[:, k * TH + c * 512:k * TH + (c + 1) * 512],
                                   start=(k == 0), stop=(k == KC - 1))
                return ins
            P.emit("tensor", mml, waits=[(sm["xnf"], xnv), (sa, sa.n),
                                         (sm["nrm"], sm["nrm"].n)], inc=sp)
            P.emit("scalar", lambda e, c=c: _cp(e, 
                v_lgst16[0:24, c * 512:(c + 1) * 512], p_dt[0:24, :]),
                waits=[(sp, sp.n)], inc=sa)
        lgv = sa.n
        for i in range(8):
            P.emit("tensor", lambda e, i=i: e.transpose(
                p_tp[:, 0:24], v_lgst16[0:24, i * 128:(i + 1) * 128], s_id[0:24, 0:24]),
                waits=[(sa, lgv), (sa, sa.n)], inc=sp)
            P.emit("scalar", lambda e, i=i: _cp(e, 
                v_lg[:, i * 24:(i + 1) * 24], p_tp[:, 0:24]),
                waits=[(sp, sp.n)], inc=sa)
        if FFN_STAGE < 3:
            return
        P.emit("scalar", lambda e: e.activation(s_small[0:1, 25:26], s_small[0:1, 24:25],
                                                AF.Sigmoid), inc=sa)

        for (dst, srcsl) in ((v_baseb[:, 0:24], s_small[0:1, 0:24]),
                             (v_eyeb[:, 0:16], s_small[0:1, 32:48]),
                             (v_gcol[:, 0:1], s_small[0:1, 25:26])):
            P.emit("tensor", lambda e, srcsl=srcsl: e.matmul(
                p_bc[0:128, 0:srcsl.shape[1]], s_o1r[:], srcsl, start=True, stop=True),
                waits=[(sa, sa.n)], inc=sp)
            P.emit("scalar", lambda e, dst=dst, srcsl=srcsl: _cp(
                e, dst, p_bc[0:128, 0:srcsl.shape[1]]),
                waits=[(sp, sp.n)], inc=sa)
        bsv = sa.n

        def mkz(e):
            lg3 = v_lg.rearrange("p (i e) -> p i e", i=8)
            for i in range(8):
                e.tensor_scalar_mul(lg3[:, i, :], lg3[:, i, :], v_rs2[:, i:i + 1])
            bb = v_baseb[:, 0:24].unsqueeze(1).broadcast_to([128, 8, 24])
            e.tensor_tensor(lg3, lg3, bb, ALU.add)
            return e.tensor_scalar_mul(
                v_zt.rearrange("p (i g) -> p i g", i=8), lg3[:, :, 0:16], 1.0 / SINK_TAU)
        P.emit("vector", mkz, waits=[(sa, bsv), (sv, sv.n)], inc=sv)
        if FFN_STAGE < 4:
            return
        P.emit("vector", lambda e: e.memset(v_uu[:, 0:32], 0.0), inc=sv)
        P.emit("vector", lambda e: e.memset(v_vv[:, 0:32], 0.0), inc=sv)
        z4 = v_zt.rearrange("p (i j k) -> p i j k", i=8, j=4)
        e4 = v_e1.rearrange("p (i j k) -> p i j k", i=8, j=4)
        e4T = v_e1.rearrange("p (i j k) -> p i k j", i=8, j=4)
        u4 = v_uu[:, 0:32].rearrange("p (i j) -> p i j", i=8)
        v4 = v_vv[:, 0:32].rearrange("p (i j) -> p i j", i=8)
        m4 = v_m1[:, 0:32].rearrange("p (i j) -> p i j", i=8)
        s4 = v_s1[:, 0:32].rearrange("p (i j) -> p i j", i=8)
        SINK_HALF = globals().get('SINK_HALF', False)
        for it in range(globals().get('SINK_IT_OVR', SINK_ITERS)):
            def ustep_a(e):
                vb = v4.unsqueeze(2).broadcast_to([128, 8, 4, 4])
                e.tensor_tensor(e4, z4, vb, ALU.add)
                return e.reduce_max(m4, e4, axis=AX.X)
            P.emit("vector", ustep_a, waits=[(sa, sa.n), (sv, sv.n)], inc=sv)

            def ustep_b(e):
                mb = m4.unsqueeze(3).broadcast_to([128, 8, 4, 4])
                return e.tensor_tensor(e4, e4, mb, ALU.subtract)
            P.emit("vector", ustep_b, waits=[(sv, sv.n)], inc=sv)
            P.emit("scalar", lambda e: e.activation(v_e1[:, 0:128], v_e1[:, 0:128], AF.Exp),
                   waits=[(sv, sv.n)], inc=sa)
            P.emit("vector", lambda e: e.reduce_sum(s4, e4, axis=AX.X),
                   waits=[(sa, sa.n)], inc=sv)
            P.emit("scalar", lambda e: e.activation(v_s1[:, 0:32], v_s1[:, 0:32], AF.Ln),
                   waits=[(sv, sv.n)], inc=sa)
            P.emit("vector", lambda e: e.scalar_tensor_tensor(
                v_uu[:, 0:32], v_m1[:, 0:32], -1.0, v_s1[:, 0:32], ALU.mult, ALU.subtract),
                waits=[(sa, sa.n)], inc=sv)

            if SINK_HALF:
                continue

            def vstep_a(e):
                ub = u4.unsqueeze(3).broadcast_to([128, 8, 4, 4])
                e.tensor_tensor(e4, z4, ub, ALU.add)
                return e.reduce_max(m4, e4T, axis=AX.X)
            P.emit("vector", vstep_a, waits=[(sv, sv.n)], inc=sv)

            def vstep_b(e):
                mb = m4.unsqueeze(3).broadcast_to([128, 8, 4, 4])
                return e.tensor_tensor(e4T, e4T, mb, ALU.subtract)
            P.emit("vector", vstep_b, waits=[(sv, sv.n)], inc=sv)
            P.emit("scalar", lambda e: e.activation(v_e1[:, 0:128], v_e1[:, 0:128], AF.Exp),
                   waits=[(sv, sv.n)], inc=sa)
            P.emit("vector", lambda e: e.reduce_sum(s4, e4T, axis=AX.X),
                   waits=[(sa, sa.n)], inc=sv)
            P.emit("scalar", lambda e: e.activation(v_s1[:, 0:32], v_s1[:, 0:32], AF.Ln),
                   waits=[(sv, sv.n)], inc=sa)
            P.emit("vector", lambda e: e.scalar_tensor_tensor(
                v_vv[:, 0:32], v_m1[:, 0:32], -1.0, v_s1[:, 0:32], ALU.mult, ALU.subtract),
                waits=[(sa, sa.n)], inc=sv)

        if FFN_STAGE < 5:
            return

        def hfin(e):
            ub = u4.unsqueeze(3).broadcast_to([128, 8, 4, 4])
            vb = v4.unsqueeze(2).broadcast_to([128, 8, 4, 4])
            e.tensor_tensor(e4, z4, ub, ALU.add)
            return e.tensor_tensor(e4, e4, vb, ALU.add)
        P.emit("vector", hfin, waits=[(sv, sv.n)], inc=sv)
        P.emit("scalar", lambda e: e.activation(v_e1[:, 0:128], v_e1[:, 0:128], AF.Exp),
               waits=[(sv, sv.n)], inc=sa)

        def blend1(e):
            e.tensor_scalar_mul(v_e1[:, 0:128], v_e1[:, 0:128], v_gcol[:, 0:1])
            return e.tensor_scalar(v_omg[:, 0:1], v_gcol[:, 0:1], -1.0, 1.0, ALU.mult, ALU.add)
        P.emit("vector", blend1, waits=[(sa, sa.n)], inc=sv)

        def blend2(e):
            return e.scalar_tensor_tensor(
                v_hres16[:, 0:128].rearrange("p (i e) -> p i e", i=8),
                v_eyeb[:, 0:16].unsqueeze(1).broadcast_to([128, 8, 16]),
                v_omg[:, 0:1],
                v_e1[:, 0:128].rearrange("p (i e) -> p i e", i=8), ALU.mult, ALU.add)
        P.emit("vector", blend2, waits=[(sv, sv.n)], inc=sv)

        lg3s = v_lg.rearrange("p (i e) -> p i e", i=8)
        for (c0, off) in ((16, 0), (20, 32)):
            P.emit("vector", lambda e, c0=c0: e.tensor_copy(
                v_sp1[:, 0:32].rearrange("p (i j) -> p i j", i=8),
                lg3s[:, :, c0:c0 + 4]),
                waits=[(sv, sv.n), (sa, sa.n)], inc=sv)
            P.emit("vector", lambda e: e.reduce_max(
                v_smx[:, 0:8], v_sp1[:, 0:32].rearrange("p (i j) -> p i j", i=8),
                axis=AX.X), waits=[(sv, sv.n)], inc=sv)
            P.emit("vector", lambda e: e.tensor_tensor(
                v_sp1[:, 0:32].rearrange("p (i j) -> p i j", i=8),
                v_sp1[:, 0:32].rearrange("p (i j) -> p i j", i=8),
                v_smx[:, 0:8].unsqueeze(2).broadcast_to([128, 8, 4]),
                ALU.subtract), waits=[(sv, sv.n)], inc=sv)
            P.emit("scalar", lambda e: e.activation(v_sp2[:, 0:32], v_sp1[:, 0:32], AF.Exp),
                   waits=[(sv, sv.n)], inc=sa)

            P.emit("vector", lambda e: e.reduce_sum(
                v_smx[:, 0:8], v_sp2[:, 0:32].rearrange("p (i j) -> p i j", i=8),
                axis=AX.X), waits=[(sa, sa.n)], inc=sv)
            P.emit("vector", lambda e: e.reciprocal(v_smx[:, 0:8], v_smx[:, 0:8]),
                   waits=[(sv, sv.n)], inc=sv)
            P.emit("vector", lambda e, off=off: e.tensor_tensor(
                hpp16[:, off:off + 32].rearrange("p (i j) -> p i j", i=8),
                v_sp2[:, 0:32].rearrange("p (i j) -> p i j", i=8),
                v_smx[:, 0:8].unsqueeze(2).broadcast_to([128, 8, 4]),
                ALU.mult), waits=[(sv, sv.n)], inc=sv)
        hv = sv.n
        if FFN_STAGE < 6:
            return
        for i in range(8):
            def tph(e, i=i):
                e.transpose(p_tp[0:16, 0:128], v_hres16[:, i * 16:(i + 1) * 16], s_id[:])
                e.transpose(p_tp[0:4, 128:256], hpp16[:, i * 4:(i + 1) * 4], s_id[:])
                return e.transpose(p_tp[0:4, 256:384], hpp16[:, 32 + i * 4:32 + (i + 1) * 4],
                                   s_id[:])
            P.emit("tensor", tph, waits=[(sv, hv), (sa, sa.n), (sm["scn"], sm["scn"].n)],
                   inc=sp)

            def evh(e, i=i):
                _cp(e, s_bn[:, i * 128:(i + 1) * 128], p_tp[0:16, 0:128])
                _cp(e, s_cn[0:4, i * 128:(i + 1) * 128], p_tp[0:4, 128:256])
                return _cp(e, s_hpo[0:4, i * 128:(i + 1) * 128], p_tp[0:4, 256:384])
            P.emit("scalar", evh, waits=[(sp, sp.n)], inc=sa)
        hrv = sa.n
        P.emit("gpsimd", dma(v_wbr.rearrange("p (n e) -> p n e", n=8), wbr),
               waits=[(sm["omm"], sm["omm"].n), (sm["xnf"], sm["xnf"].n),
                      (sm["tpd"], sm["tpd"].n), (sm["ffp"], sm["ffp"].n)],
               inc=sm["ffw"], inc_k=16)

        def emit_bcast_row(src, r, dst_off, extra=(), kdim=16):
            def mm2(e, src=src, r=r, kdim=kdim):
                e.matmul(p_bc[:], s_oh16[0:kdim, r * 128:(r + 1) * 128],
                         src[0:kdim, 0:512], start=True, stop=True)
                return e.matmul(p_dt[:], s_oh16[0:kdim, r * 128:(r + 1) * 128],
                                src[0:kdim, 512:1024], start=True, stop=True)
            P.emit("tensor", mm2, waits=[(sa, hrv), (sa, sa.n)] + list(extra), inc=sp)

            def ev2(e, dst_off=dst_off):
                _cp(e, v_hb[:, dst_off:dst_off + 512], p_bc[:])
                return _cp(e, v_hb[:, dst_off + 512:dst_off + TH], p_dt[:])
            P.emit("scalar", ev2, waits=[(sp, sp.n), (sv, sv.n)], inc=sa)

        if FFN_STAGE < 7:
            return
        for j in range(4):
            emit_bcast_row(s_cn, j, j * TH)
        bpv = sa.n
        if FFN_STAGE < 7.5:
            return
        for ci in range(8):
            def xpre(e, ci=ci):
                acc = v_xpre[:, ci * TH:(ci + 1) * TH]
                e.tensor_tensor(acc, v_xnm[:, ci * TH:(ci + 1) * TH],
                                v_hb[:, 0:TH], ALU.mult)
                for j in range(1, 4):
                    e.tensor_tensor(v_yf2, v_xnm[:, (j * 8 + ci) * TH:(j * 8 + ci + 1) * TH],
                                    v_hb[:, j * TH:(j + 1) * TH], ALU.mult)
                    ins = e.tensor_add(acc, acc, v_yf2)
                return ins
            P.emit("vector", xpre, waits=[(sa, bpv), (sm["wod"], sm["wod"].n)], inc=sv)
        xpv = sv.n
        if FFN_STAGE < 8:
            return
        for co in range(8):
            for tc in range(2):
                def mmbr(e, co=co, tc=tc):
                    pp = p_o[0]
                    for k in range(8):
                        ins = e.matmul(pp[:],
                                       v_wbr[:, k * 1024 + co * 128:k * 1024 + (co + 1) * 128],
                                       v_xpre[:, k * TH + tc * 512:k * TH + (tc + 1) * 512],
                                       start=(k == 0), stop=(k == 7))
                    return ins
                P.emit("tensor", mmbr, waits=[(sv, xpv), (sm["ffw"], sm["ffw"].n),
                                              (sa, sa.n), (sm["oev"], sm["oev"].n)], inc=sp)
                P.emit("scalar", lambda e, co=co, tc=tc: _cp(e, 
                    v_ybr[:, co * TH + tc * 512:co * TH + (tc + 1) * 512],
                    p_o[0][:]), waits=[(sp, sp.n)], inc=sa)
        ybv = sa.n
        if FFN_STAGE < 9:
            return
        for i in range(4):
            for j in range(4):
                emit_bcast_row(s_bn, i * 4 + j, j * TH, extra=[(sv, sv.n), (sa, sa.n)])
            emit_bcast_row(s_hpo, i, 4 * TH, extra=[(sa, sa.n)], kdim=4)
            brv = sa.n
            for ci in range(8):
                fc = i * 8 + ci
                P.emit("sync", lambda e, fc=fc: e.dma_start(
                    out=v_sld, in_=s_d[fc * 128:(fc + 1) * 128, :]),
                    waits=[(sm["oev"], sm["oev"].n), (sm["fgd"], sm["fgd"].n)],
                    inc=sm["fgd"], inc_k=16)
                sldv = sm["fgd"].n

                def mix(e, ci=ci):
                    acc = v_sgc
                    e.tensor_tensor(acc, v_xnm[:, ci * TH:(ci + 1) * TH],
                                    v_hb[:, 0:TH], ALU.mult)
                    for j in range(1, 4):
                        e.tensor_tensor(v_yf2, v_xnm[:, (j * 8 + ci) * TH:(j * 8 + ci + 1) * TH],
                                        v_hb[:, j * TH:(j + 1) * TH], ALU.mult)
                        e.tensor_add(acc, acc, v_yf2)
                    e.tensor_tensor(v_yf2, v_ybr[:, ci * TH:(ci + 1) * TH],
                                    v_hb[:, 4 * TH:5 * TH], ALU.mult)
                    return e.tensor_add(acc, acc, v_yf2)
                P.emit("vector", mix, waits=[(sa, brv), (sm["ffp"], sm["ffp"].n)], inc=sv)
                P.emit("scalar", lambda e: e.activation(v_sgc, v_sgc, AF.Silu),
                       waits=[(sv, sv.n)], inc=sa)
                P.emit("vector", lambda e: e.tensor_mul(v_sgc, v_sgc, v_sld),
                       waits=[(sa, sa.n), (sm["fgd"], sldv)], inc=sv)
                sgv = sv.n
                for g in range(2):
                    def tps(e, g=g):
                        for jj in range(4):
                            tt = g * 4 + jj
                            ins = e.transpose(p_tp[:, jj * 128:(jj + 1) * 128],
                                              v_sgc[:, tt * 128:(tt + 1) * 128], s_id[:])
                        return ins
                    P.emit("tensor", tps, waits=[(sv, sgv), (sa, sa.n),
                                                 (sm["fgd"], sm["fgd"].n)], inc=sp)
                    P.emit("scalar", lambda e: _cp(e, v_tev[:, 0:512], p_tp[:, 0:512]),
                           waits=[(sp, sp.n), (sm["fgd"], sm["fgd"].n)], inc=sa)
                    P.emit("sync", lambda e, fc=fc, g=g: e.dma_start(
                        out=sgT_d[g * 512:(g + 1) * 512, fc * 128:(fc + 1) * 128]
                        .rearrange("(a p) c -> p a c", p=128),
                        in_=v_tev[:, 0:512].rearrange("p (a c) -> p a c", a=4)),
                        waits=[(sa, sa.n)], inc=sm["fgd"], inc_k=16)

    def phase_FIN():
        fgv = sm["fgd"].n
        for i in range(8):
            P.emit("sync", lambda e, i=i: e.dma_start(
                out=v_fin, in_=x_mine[i * 128:(i + 1) * 128, :]),
                waits=[(sm["fgd"], fgv), (sm["outd"], sm["outd"].n)],
                inc=sm["outd"], inc_k=16)
            P.emit("sync", lambda e, i=i: e.dma_start(
                out=v_sq, in_=sgT_d[i * 128:(i + 1) * 128, :]),
                waits=[(sm["ffv"], sm["ffv"].n), (sm["ffa"], sm["ffa"].n)],
                inc=sm["outd"], inc_k=16)
            fv = sm["outd"].n
            P.emit("vector", lambda e: e.tensor_add(v_fin, v_fin, v_sq),
                   waits=[(sm["outd"], fv)], inc=sm["fgd"])
            P.emit("sync", lambda e, i=i: e.dma_start(
                out=out[i * 128:(i + 1) * 128, :], in_=v_fin),
                waits=[(sm["fgd"], sm["fgd"].n)], inc=sm["outd"], inc_k=16)
        P.wait("sync", (sm["outd"], sm["outd"].n))
        P.wait("gpsimd", (sm["outd"], sm["outd"].n))

    phase_LN(0, x_b)
    phase_A(0)
    cc_a0 = sm["cc"].n
    if UPTO >= 2:
        phase_LN(1, x_b)
        phase_A(1)
    cc_a1 = sm["cc"].n
    if UPTO >= 3:
        phase_N(0, cc_a0)
        phase_B(0)
    if UPTO >= 4:
        phase_N(1, cc_a1)
        phase_B(1)
    cc_y = sm["cc"].n
    if UPTO >= 5:
        phase_C(cc_y)
    if UPTO >= 6:
        phase_FFN()
    if UPTO >= 7:
        phase_FIN()
    else:
        allw = [(s, s.n) for s in sm.values()]
        for i in range(8):
            P.emit("sync", lambda e, i=i: e.dma_start(
                out=out[i * 128:(i + 1) * 128, :], in_=v_sq.bitcast(F32)[:, 0:H // 2]
                if False else R5.bitcast(F32)[:, 0:H]),
                waits=allw if i == 0 else [], inc=sm["outd"], inc_k=16)
        P.wait("sync", (sm["outd"], sm["outd"].n))
        P.wait("gpsimd", (sm["outd"], sm["outd"].n))

    if DEBUG:
        allw = [(s, s.n) for s in sm.values()]
        pairs = [("d_xc", xc_d), ("d_zs", zs_d), ("d_ssm0", ssm_out[0]),
                 ("d_ssm1", ssm_out[1]), ("d_yag0", yag_out[0]), ("d_yag1", yag_out[1]),
                 ("d_sd", s_d), ("d_sgT", sgT_d), ("d_hres", s_bn[:]),
                 ("d_hpre", s_cn[:]), ("d_hpo", s_hpo[:]), ("d_xpre", v_xpre),
                 ("d_ybr", v_ybr), ("d_rs2", v_rs2), ("d_lg", v_lg),
                 ("d_uu", v_uu[:, 0:32]), ("d_vv", v_vv[:, 0:32]),
                 ("d_h16", v_hres16[:, 0:128]), ("d_hpp", hpp16[:, 0:64]),
                 ("d_zt", v_zt), ("d_m1", v_m1[:, 0:32]), ("d_s1", v_s1[:, 0:32]),
                 ("d_e1", v_e1),
                 ("d_sb", R5.bitcast(F32)[:, 384:432]),
                 ("d_sp1", v_sp1[:, 0:32]), ("d_sp2", v_sp2[:, 0:32]),
                 ("d_smx", v_smx[:, 0:8]),
                 ("d_chi", v_chi), ("d_clo", v_clo),
                 ("d_bb", v_bb), ("d_cb", v_cb), ("d_u", v_u),
                 ("d_da", v_da), ("d_h", v_h), ("d_dbx", v_dbx)]
        for k, (nm, srcT) in enumerate(pairs):
            P.emit("gpsimd", lambda e, nm=nm, srcT=srcT: e.dma_start(
                out=dbg[nm], in_=srcT), waits=allw if k == 0 else [],
                inc=sm["outd"], inc_k=16)
        P.wait("gpsimd", (sm["outd"], sm["outd"].n))

    with nc.Block() as block:
        for eng in ENGS:
            def make(eng):
                def body(e):
                    for f in P.ops[eng]:
                        f(e)
                return body
            getattr(block, eng)(make(eng))

    ctx.close()
    return nc


_NC_CACHE = {}


def _stage(inputs, j):
    ip = inputs
    W = ip["L0_attn_in_proj_w"]
    Wx = W[:, j * DH:(j + 1) * DH]
    Wz = W[:, DIN + j * DH:DIN + (j + 1) * DH]

    def blk(Wp):
        return np.ascontiguousarray(Wp.reshape(KC, 128, ND, 128).transpose(2, 1, 0, 3))
    wxz = np.concatenate([blk(Wx), blk(Wz)], axis=0)
    wxpT = np.ascontiguousarray(
        ip["L0_attn_x_proj_w"].T[j * DH:(j + 1) * DH, :]
        .reshape(ND, 128, R + 2 * NST).transpose(1, 0, 2))
    wdtT = np.ascontiguousarray(
        ip["L0_attn_dt_proj_w"][j * DH:(j + 1) * DH, :].T
        .reshape(2, 128, DH).transpose(1, 0, 2))
    woT = np.ascontiguousarray(
        ip["L0_attn_out_proj_w"].T.reshape(2 * ND, 128, KC, 128).transpose(2, 1, 0, 3))

    def cols(v):
        return np.ascontiguousarray(v[j * DH:(j + 1) * DH].reshape(ND, 128).T)

    convw = np.ascontiguousarray(
        ip["L0_attn_conv_w"][j * DH:(j + 1) * DH].reshape(ND, 128, 4)
        .transpose(1, 0, 2).reshape(128, ND * 4))
    alog = np.ascontiguousarray(
        ip["L0_attn_A_log"][j * DH:(j + 1) * DH].reshape(ND, 128, NST)
        .transpose(1, 0, 2).reshape(128, ND * NST))
    wres_cat = np.concatenate([ip["L0_ffn_W_res"], ip["L0_ffn_W_pre"],
                               ip["L0_ffn_W_post"]], axis=0)
    wresT = np.ascontiguousarray(wres_cat.T.reshape(KC, 128, 24).transpose(1, 0, 2))
    wbr = np.ascontiguousarray(
        ip["L0_ffn_W_branch"].reshape(8, 128, 1024).transpose(1, 0, 2))
    oh = np.zeros((16, 16, 128), np.float16)
    for n in range(16):
        oh[n, n, :] = 1.0
    hb = np.concatenate([ip["L0_ffn_H_res_base"].ravel(),
                         ip["L0_ffn_H_pre_base"].ravel(),
                         ip["L0_ffn_H_post_base"].ravel()]).reshape(1, 24)
    return dict(
        wxz=wxz, wxpT=wxpT, wdtT=wdtT, woT=woT, convw=convw,
        convb=cols(ip["L0_attn_conv_b"]).copy(),
        dparam=cols(ip["L0_attn_D_param"]).copy(),
        dtbias=cols(ip["L0_attn_dt_bias"]).copy(),
        alog=alog,
        dtlnw=np.ascontiguousarray(ip["L0_attn_dt_ln_w"].reshape(2, 128).T),
        blnw=np.ascontiguousarray(ip["L0_attn_B_ln_w"].reshape(16, 1)),
        clnw=np.ascontiguousarray(ip["L0_attn_C_ln_w"].reshape(16, 1)),
        wresT=wresT, wbr=wbr,
        hbases=np.ascontiguousarray(hb, dtype=np.float32),
        gate=np.ascontiguousarray(ip["L0_ffn_gate"].reshape(1, 1)),
        eye4=np.ascontiguousarray(np.eye(4, dtype=np.float32).reshape(1, 16)),
        ident=np.eye(128, dtype=np.float16),
        ones128c=np.ones((128, 1), np.float32),
        ones16c=np.ones((16, 1), np.float32),
        ones1r=np.ones((1, 128), np.float32),
        oh16=np.ascontiguousarray(oh.reshape(16, 16 * 128)),
        flagj=np.array([[j, 0]], np.int32),
    )


def kernel(**inputs):
    inputs = {k: np.asarray(v) for k, v in inputs.items()}
    x = np.asarray(inputs["L0_attn_hidden_states"], dtype=np.float32)
    if "nc" not in _NC_CACHE:
        _NC_CACHE["nc"] = build_nc()
    nc = _NC_CACHE["nc"]
    staged = [_stage(inputs, j) for j in range(2)]
    in_maps = []
    for c in range(8):
        b, j = c // 2, c % 2
        m = dict(staged[j])
        m["x_b"] = np.ascontiguousarray(x[b])
        m["x_mine"] = np.ascontiguousarray(x[b, j * TH:(j + 1) * TH])
        in_maps.append(m)
    res = run_bass_kernel_spmd(nc, in_maps, core_ids=list(range(8)))
    outp = np.empty((B_SZ, T, H), np.float32)
    for c in range(8):
        b, j = c // 2, c % 2
        outp[b, j * TH:(j + 1) * TH] = np.asarray(res.results[c]["out"])
    if DEBUG:
        kernel.debug_results = res.results
    return outp


if __name__ == "__main__":
    nc = build_nc()
    print("built ok")


# version 15
# speedup vs baseline: 1.0123x; 1.0123x over previous
import numpy as np
from contextlib import ExitStack

import concourse.bass as bass
import concourse.mybir as mybir
from concourse.bass_utils import run_bass_kernel_spmd

F32 = mybir.dt.float32
F16 = mybir.dt.float16
I32 = mybir.dt.int32
AF = mybir.ActivationFunctionType
ALU = mybir.AluOpType
AX = mybir.AxisListType

B_SZ, T, TH, H = 4, 2048, 1024, 4096
DIN, DH, ND, KC = 8192, 4096, 32, 32
NST, R = 16, 256
EPS_ATTN, EPS_LN, RMS_EPS = 1e-6, 1e-5, 1.1920929e-07
SINK_ITERS, SINK_TAU = 50, 0.1
TS = 512
TSP = TS + 1

ENGS = ("sync", "vector", "scalar", "tensor", "gpsimd")
UPTO = 99
FFN_STAGE = 99
DEBUG = False


class Sem:
    def __init__(self, nc, name):
        self.h = nc.ctx.enter_context(nc.semaphore(name))
        self.n = 0


class Prog:
    def __init__(self):
        self.ops = {e: [] for e in ENGS}

    def emit(self, eng, fn, waits=(), inc=None, inc_k=1):
        w = tuple((s.h, v) for (s, v) in waits)

        def run(e, fn=fn, w=w, inc=inc, inc_k=inc_k):
            for sh, v in w:
                e.wait_ge(sh, v)
            ins = fn(e)
            if inc is not None:
                ins.then_inc(inc.h, inc_k)
        self.ops[eng].append(run)
        if inc is not None:
            inc.n += inc_k

    def wait(self, eng, *sems_vals):
        w = tuple((s.h, v) for (s, v) in sems_vals)

        def run(e, w=w):
            for sh, v in w:
                e.wait_ge(sh, v)
        self.ops[eng].append(run)



def _cp(e, out, in_):
    if hasattr(e, "tensor_copy"):
        return e.tensor_copy(out, in_)
    return e.copy(out, in_)

def build_nc():
    nc = bass.Bass("TRN2", target_bir_lowering=False)
    ctx = ExitStack()
    E = ctx.enter_context

    def din(name, shape, dt=F32):
        return nc.dram_tensor(name, list(shape), dt, kind="ExternalInput").ap()

    x_b = din("x_b", [T, H])
    x_mine = din("x_mine", [TH, H])
    wxz = din("wxz", [2 * ND, 128, KC, 128])
    wxpT = din("wxpT", [128, ND, R + 2 * NST])
    wdtT = din("wdtT", [128, 2, DH])
    woT = din("woT", [KC, 128, 2 * ND, 128])
    convw = din("convw", [128, ND * 4])
    convb = din("convb", [128, ND])
    dparam = din("dparam", [128, ND])
    dtbias = din("dtbias", [128, ND])
    alog = din("alog", [128, ND * NST])
    dtlnw = din("dtlnw", [128, 2])
    blnw = din("blnw", [16, 1])
    clnw = din("clnw", [16, 1])
    wresT = din("wresT", [128, KC, 24])
    wbr = din("wbr", [128, 8, 1024])
    hbases = din("hbases", [1, 24])
    gate = din("gate", [1, 1])
    eye4 = din("eye4", [1, 16])
    ident = din("ident", [128, 128], F16)
    ones128c = din("ones128c", [128, 1])
    ones16c = din("ones16c", [16, 1])
    ones1r = din("ones1r", [1, 128])
    oh16 = din("oh16", [16, 16 * 128], F16)
    flagj = din("flagj", [1, 2], I32)

    out = nc.dram_tensor("out", [TH, H], F32, kind="ExternalOutput").ap()
    dbg = {}
    if DEBUG:
        for nm, shape, dt in [("d_xc", [DH, T], F16), ("d_zs", [DH, T], F16),
                              ("d_ssm0", [R + 2 * NST, TH], F32),
                              ("d_ssm1", [R + 2 * NST, TH], F32),
                              ("d_yag0", [DIN, TH], F16), ("d_yag1", [DIN, TH], F16),
                              ("d_sd", [H, TH], F16), ("d_sgT", [TH, H], F16),
                              ("d_hres", [16, TH], F16), ("d_hpre", [16, TH], F16),
                              ("d_hpo", [4, TH], F16), ("d_xpre", [128, 8 * TH], F16),
                              ("d_ybr", [128, 8 * TH], F16), ("d_rs2", [128, 8], F32),
                              ("d_lg", [128, 192], F32), ("d_uu", [128, 32], F32),
                              ("d_vv", [128, 32], F32), ("d_h16", [128, 128], F16),
                              ("d_hpp", [128, 64], F16), ("d_zt", [128, 128], F32),
                              ("d_m1", [128, 32], F32), ("d_s1", [128, 32], F32),
                              ("d_e1", [128, 128], F32), ("d_sb", [128, 48], F32),
                              ("d_sp1", [128, 32], F32), ("d_sp2", [128, 32], F32),
                              ("d_smx", [128, 8], F32),
                              ("d_chi", [128, 512], F16), ("d_clo", [128, 512], F16),
                              ("d_chi0", [128, 512], F16), ("d_clo0", [128, 512], F16),
                              ("d_chi1", [128, 512], F16), ("d_clo1", [128, 512], F16),
                              ("d_da0", [128, 8 * 514], F32), ("d_dbx0", [128, 8 * 514], F16),
                              ("d_h0", [128, 8 * 514], F32),
                              ("d_bb", [128, 8192], F16), ("d_cb", [128, 8192], F16),
                              ("d_u", [128, 512], F16), ("d_da", [128, 8 * 514], F32),
                              ("d_h", [128, 8 * 514], F32), ("d_dbx", [128, 8 * 514], F16)]:
            dbg[nm] = nc.dram_tensor(nm, shape, dt, kind="ExternalOutput").ap()

    xc_d = nc.dram_tensor("xc_d", [DH, T], F16).ap()
    zs_d = nc.dram_tensor("zs_d", [DH, T], F16).ap()
    ssm_in = [nc.dram_tensor(f"ssm_in{t}", [R + 2 * NST, TH], F32).ap() for t in range(2)]
    ssm_out = [nc.dram_tensor(f"ssm_out{t}", [R + 2 * NST, TH], F32).ap() for t in range(2)]
    yag_in = [nc.dram_tensor(f"yag_in{t}", [DH, TH], F16).ap() for t in range(2)]
    yag_out = [nc.dram_tensor(f"yag_out{t}", [DIN, TH], F16).ap() for t in range(2)]
    s_d = nc.dram_tensor("s_d", [H, TH], F16).ap()
    sgT_d = nc.dram_tensor("sgT_d", [TH, H], F16).ap()

    RG = [[0, 1], [2, 3], [4, 5], [6, 7]]

    def sb(name, shape, dt):
        return E(nc.sbuf_tensor(name, list(shape), dt))

    R0 = sb("R0", [128, 33024], F16)
    v_xn = R0[:, 0:KC * TH]
    NH, LEN = 8, TS + 2
    v_da = R0.bitcast(F32)[:, 0:NH * LEN]
    v_dbx = R0[:, 8224:8224 + NH * LEN]
    v_h = R0.bitcast(F32)[:, 6168:6168 + NH * LEN]
    v_p = R0[:, 20560:20560 + NH * LEN]
    v_yres = R0[:, 0:2 * ND * 512]
    v_xnm = R0[:, 0:KC * TH]

    R2 = sb("R2", [128, 24576], F16)
    v_wxp = R2[:, 0:ND * (R + 2 * NST)]
    v_ssm = R2.bitcast(F32)[:, 4800:4800 + 3 * TH]
    v_xt = R2.bitcast(F32)[:, 7872:7872 + 4096]
    v_fin = v_xt
    v_bb = R2[:, 0:NST * TS]
    v_cb = R2[:, NST * TS:2 * NST * TS]
    v_wdt = R2[:, 2 * NST * TS:2 * NST * TS + 2 * DH]
    v_wo = [R2[:, i * (2 * ND * 128):(i + 1) * (2 * ND * 128)] for i in range(2)]
    v_xpre = R2[:, 0:8 * TH]
    v_ybr = R2[:, 8 * TH:16 * TH]
    v_wbr = R2[:, 16 * TH:16 * TH + 8 * 1024]

    R3 = sb("R3", [128, 8192], F16)
    v_w16 = [R3[:, i * (KC * 128):(i + 1) * (KC * 128)] for i in range(2)]
    v_hb = R3[:, 0:5 * TH]
    v_sev = R3[:, 5 * TH:5 * TH + 512]
    v_sgc = R3[:, 6 * TH:7 * TH]


    R5 = sb("R5", [128, 8192], F16)
    v_sq = R5[:, 0:H]
    v_dt2 = R5.bitcast(F32)[:, 0:2 * TH]
    v_nsq = R5.bitcast(F32)[:, 2 * TH:4 * TH]
    v_zt = R5.bitcast(F32)[:, 0:128]
    v_e1 = R5.bitcast(F32)[:, 128:256]
    hpp16 = R5[:, 256:320]
    v_uu = R5.bitcast(F32)[:, 256:288]
    v_vv = R5.bitcast(F32)[:, 288:320]
    v_m1 = R5.bitcast(F32)[:, 320:352]
    v_s1 = R5.bitcast(F32)[:, 352:384]
    v_eyeb = R5.bitcast(F32)[:, 384:400]
    v_baseb = R5.bitcast(F32)[:, 400:424]
    v_gcol = R5.bitcast(F32)[:, 424:425]
    v_omg = R5.bitcast(F32)[:, 425:426]
    v_mx = R5.bitcast(F32)[:, 432:440]
    v_lg = R5.bitcast(F32)[:, 512:704]
    v_hres16 = R5[:, 1704:1832]
    v_lgst16 = R5[:, 4096:4096 + TH]


    R7 = sb("R7", [128, 11992], F16)
    v_dtv = R7[:, 0:TS]
    v_u = R7[:, TS:2 * TS]
    v_sld = R7[:, 10936:10936 + TH]
    v_xcr = [R7[:, 1024 + i * TS:1024 + (i + 1) * TS] for i in range(2)]
    v_tev = R7[:, 1024:1024 + 512]
    v_zsr = [R7[:, 2048 + i * TS:2048 + (i + 1) * TS] for i in range(2)]
    v_ys = R7.bitcast(F32)[:, 1536:1536 + TS]
    v_yf = R7[:, 4096:4096 + TS]
    v_yf2 = R7[:, 4608:4608 + TH]
    v_chi = R7[:, 5632:5632 + ND * NST]
    v_clo = R7[:, 6144:6144 + ND * NST]
    v_spt = R7.bitcast(F32)[:, 3328:3328 + 512]
    v_spt2 = R7.bitcast(F32)[:, 3840:3840 + 512]
    v_xraw = R7[:, 8704:8704 + TH + 8]
    v_xcw = R7[:, 9736:9736 + TH]
    v_ctail = R7[:, 10760:10760 + ND * 4]
    v_stat = R7.bitcast(F32)[:, 5444:5460]
    v_rs2 = R7.bitcast(F32)[:, 5460:5468]
    v_lo32 = R7.bitcast(F32)[:, 5980:5996]
    v_sp1 = R7.bitcast(F32)[:, 3072:3104]
    v_sp2 = R7.bitcast(F32)[:, 3104:3136]
    v_smx = R7.bitcast(F32)[:, 3136:3144]

    R8 = sb("R8", [128, 8192], F16)
    v_braw = R8.bitcast(F32)[:, 0:TH]
    v_craw = R8.bitcast(F32)[:, TH:2 * TH]
    v_r1 = R8.bitcast(F32)[:, 2 * TH:3 * TH]
    v_rb = R8[:, 6 * TH:7 * TH]
    v_rb16 = R8[:, 7 * TH:8 * TH]

    _dtrn = sb("s_dtrn", [128, 2 * TH], F16)
    s_dtrn = [_dtrn, _dtrn]
    s_bn = sb("s_bn", [16, TH], F16)
    s_cn = sb("s_cn", [16, TH], F16)
    s_hpo = sb("s_hpo", [4, TH], F16)
    s_convw = sb("s_convw", [128, ND * 4], F32)
    s_convb = sb("s_convb", [128, ND], F32)
    s_dpar = sb("s_dpar", [128, ND], F32)
    s_dtb = sb("s_dtb", [128, ND], F32)
    s_aneg = sb("s_aneg", [128, ND * NST], F32)
    s_dtln = sb("s_dtln", [128, 2], F32)
    s_bln = sb("s_bln", [16, 1], F32)
    s_cln = sb("s_cln", [16, 1], F32)
    s_wres = sb("s_wres", [128, KC * 24], F16)
    s_id = sb("s_id", [128, 128], F16)
    s_o128c = sb("s_o128c", [128, 1], F32)
    s_o16c = sb("s_o16c", [16, 1], F32)
    s_o1r = sb("s_o1r", [1, 128], F32)
    s_oh16 = sb("s_oh16", [16, 16 * 128], F16)
    s_small = sb("s_small", [1, 64], F32)
    s_flag = sb("s_flag", [1, 2], I32)

    p_x = E(nc.psum_tensor("p_x", [128, 512], F32))
    p_z = E(nc.psum_tensor("p_z", [128, 512], F32))
    p_ssm = E(nc.psum_tensor("p_ssm", [128, 3 * 512], F32))
    p_tp = E(nc.psum_tensor("p_tp", [128, 1024], F16))
    p_dt = E(nc.psum_tensor("p_dt", [128, 512], F32))
    p_bc = E(nc.psum_tensor("p_bc", [128, 512], F32))
    p_o = [p_x, p_z]

    sm = {k: Sem(nc, k) for k in [
        "ldc", "setup", "xld", "lnv", "lna", "tpd", "xnf",
        "wld", "pxz", "evx", "cvd", "sil", "xcd", "psd", "ssa", "ssmd", "cc",
        "nrm", "bcb", "dtm", "spd", "dad", "dbd", "scn", "yrd", "xcr",
        "wod", "yrl", "omm", "oev",
        "ffw", "ffv", "ffa", "ffp", "fgd", "outd", "pl"]}

    P = Prog()

    def dma(dst, src):
        return lambda e: e.dma_start(out=dst, in_=src)

    ld = sm["ldc"]
    for dst, src in [
        (s_convw[:], convw), (s_convb[:], convb), (s_dpar[:], dparam),
        (s_dtb[:], dtbias), (s_aneg[:], alog), (s_dtln[:], dtlnw),
        (s_bln[:], blnw), (s_cln[:], clnw), (s_id[:], ident),
        (s_o128c[:], ones128c), (s_o16c[:], ones16c), (s_o1r[:], ones1r),
        (s_oh16[:], oh16), (s_flag[:], flagj),
        (s_small[0:1, 0:24], hbases), (s_small[0:1, 24:25], gate),
        (s_small[0:1, 32:48], eye4),
    ]:
        P.emit("sync", dma(dst, src), inc=ld, inc_k=16)
    P.emit("gpsimd", dma(v_wxp.rearrange("p (n e) -> p n e", n=ND), wxpT), inc=ld, inc_k=16)
    P.emit("gpsimd", dma(s_wres[:].rearrange("p (n e) -> p n e", n=KC), wresT), inc=ld, inc_k=16)
    NCONST = ld.n

    P.wait("scalar", (ld, NCONST))
    P.emit("scalar", lambda e: e.activation(s_aneg[:], s_aneg[:], AF.Exp), inc=sm["setup"])
    P.wait("vector", (ld, NCONST))
    P.emit("vector", lambda e: e.tensor_scalar_mul(s_aneg[:], s_aneg[:], -1.0),
           waits=[(sm["setup"], 1)], inc=sm["setup"])
    P.emit("vector", lambda e: e.memset(v_chi[:], 0.0), inc=sm["setup"])
    P.emit("vector", lambda e: e.memset(v_clo[:], 0.0), inc=sm["setup"])
    P.emit("vector", lambda e: e.memset(v_ctail[:], 0.0), inc=sm["setup"])
    SETUP_N = sm["setup"].n
    P.wait("tensor", (ld, NCONST))
    P.wait("gpsimd", (ld, NCONST))
    P.wait("sync", (ld, NCONST))

    def phase_LN(th, src_dram, ffn=False):
        if ffn:
            sv, sa, sp = sm["ffv"], sm["ffa"], sm["ffp"]
        else:
            sv, sa, sp = sm["lnv"], sm["lna"], sm["tpd"]
        r0_guard = (sm["pxz"], sm["pxz"].n) if not ffn else (sm["omm"], sm["omm"].n)
        for i in range(8):
            if ffn:
                srcap = src_dram[i * 128:(i + 1) * 128, :]
            else:
                srcap = src_dram[th * TH + i * 128:th * TH + (i + 1) * 128, :]
            w = [(sm["xnf"], sm["xnf"].n), (sm["fgd"], sm["fgd"].n)] if (i > 0 or th > 0 or ffn) else []
            P.emit("sync", dma(v_xt, srcap), waits=w, inc=sm["xld"], inc_k=16)
            xv = sm["xld"].n
            P.emit("vector", lambda e: e.tensor_scalar(
                v_sq, v_xt, 1.0, 0.0, ALU.mult, ALU.add, accum_out=v_stat[:, 0:1]),
                waits=[(sm["xld"], xv), (sa, sa.n)], inc=sv)
            P.emit("scalar", lambda e: e.activation(
                v_sq, v_xt, AF.Square, accum_out=v_stat[:, 1:2]),
                waits=[(sm["xld"], xv), (sv, sv.n)], inc=sa)

            def stats(e):
                e.tensor_scalar_mul(v_stat[:, 0:1], v_stat[:, 0:1], 1.0 / H)
                e.tensor_scalar_mul(v_stat[:, 1:2], v_stat[:, 1:2], 1.0 / H)
                e.tensor_tensor(v_stat[:, 8:9], v_stat[:, 0:1], v_stat[:, 0:1], ALU.mult)
                e.tensor_sub(v_stat[:, 1:2], v_stat[:, 1:2], v_stat[:, 8:9])
                return e.tensor_scalar(v_stat[:, 9:10], v_stat[:, 1:2], 1.0, EPS_LN,
                                       ALU.mult, ALU.add)
            P.emit("vector", stats, waits=[(sv, sv.n), (sa, sa.n)], inc=sv)
            P.emit("scalar", lambda e: e.activation(v_stat[:, 10:11], v_stat[:, 9:10], AF.Sqrt),
                   waits=[(sv, sv.n)], inc=sa)

            def fin_a(e, ffn=ffn):
                ins = e.reciprocal(v_stat[:, 11:12], v_stat[:, 10:11])
                if ffn:
                    e.tensor_tensor(v_stat[:, 12:13], v_stat[:, 1:2], v_stat[:, 11:12], ALU.mult)
                    e.tensor_tensor(v_stat[:, 12:13], v_stat[:, 12:13], v_stat[:, 11:12], ALU.mult)
                    ins = e.tensor_scalar(v_stat[:, 12:13], v_stat[:, 12:13], 1.0, RMS_EPS,
                                          ALU.mult, ALU.add)
                return ins
            P.emit("vector", fin_a, waits=[(sa, sa.n)], inc=sv)
            P.emit("vector", lambda e: e.tensor_scalar(
                v_sq, v_xt, v_stat[:, 0:1], v_stat[:, 11:12], ALU.subtract, ALU.mult),
                waits=[(sv, sv.n)], inc=sv)
            if ffn:
                P.emit("scalar", lambda e: e.activation(v_stat[:, 13:14], v_stat[:, 12:13],
                                                        AF.Sqrt), waits=[(sv, sv.n)], inc=sa)
                P.emit("vector", lambda e, i=i: e.reciprocal(v_rs2[:, i:i + 1], v_stat[:, 13:14]),
                       waits=[(sa, sa.n)], inc=sv)
            lv = sv.n
            for g in range(8):
                def tp(e, g=g):
                    for jj in range(4):
                        kk = g * 4 + jj
                        ins = e.transpose(p_tp[:, jj * 128:(jj + 1) * 128],
                                          v_sq[:, kk * 128:(kk + 1) * 128], s_id[:])
                    return ins
                P.emit("tensor", tp, waits=[(sv, lv), (sm["xnf"], sm["xnf"].n)], inc=sp)
                dst = (v_xnm if ffn else v_xn).rearrange("p (k t) -> p k t", k=KC)[
                    :, g * 4:(g + 1) * 4, i * 128:(i + 1) * 128]
                P.emit("scalar", lambda e, dst=dst: _cp(e, 
                    dst, p_tp[:, 0:512].rearrange("p (k t) -> p k t", k=4)),
                    waits=[(sp, sp.n), r0_guard], inc=sm["xnf"])

    pxz_after_d = {}

    def phase_A(th):
        P.emit("vector", lambda e: e.memset(v_ssm, 0.0),
               waits=[(sm["ssmd"], sm["ssmd"].n)], inc=sm["ssa"])
        ssm_clear = sm["ssa"].n
        xn_ready = sm["xnf"].n
        for d in range(ND):
            wb = v_w16[d % 2]
            key = (th, d - 2) if d >= 2 else ((th - 1, ND - 2 + d) if th > 0 else None)
            wfree = [(sm["pxz"], pxz_after_d[key])] if key in pxz_after_d else []
            P.emit("gpsimd", dma(wb.rearrange("p (k c) -> p k c", k=KC), wxz[d]),
                   waits=wfree + [(sm["xnf"], xn_ready)], inc=sm["wld"], inc_k=16)
            wv = sm["wld"].n
            for ts in range(2):
                def mmx(e, ts=ts, wb=wb):
                    for k in range(KC):
                        ins = e.matmul(p_x[:], wb[:, k * 128:(k + 1) * 128],
                                       v_xn[:, k * TH + ts * 512:k * TH + (ts + 1) * 512],
                                       start=(k == 0), stop=(k == KC - 1))
                    return ins
                P.emit("tensor", mmx, waits=[(sm["wld"], wv), (sm["evx"], sm["evx"].n)],
                       inc=sm["pxz"])
                P.emit("scalar", lambda e, ts=ts: _cp(e, 
                    v_xraw[:, 3 + ts * 512:3 + (ts + 1) * 512], p_x[:]),
                    waits=[(sm["pxz"], sm["pxz"].n), (sm["xcd"], sm["xcd"].n),
                           (sm["cvd"], sm["cvd"].n)], inc=sm["evx"])
            evv = sm["evx"].n
            if th == 0:
                P.emit("vector", lambda e: e.memset(v_xraw[:, 0:3], 0.0),
                       waits=[(sm["evx"], evv), (sm["setup"], SETUP_N)], inc=sm["cvd"])
            else:
                P.emit("vector", lambda e, d=d: _cp(e, 
                    v_xraw[:, 0:3], v_ctail[:, d * 4:d * 4 + 3]),
                    waits=[(sm["evx"], evv)], inc=sm["cvd"])
            P.emit("vector", lambda e, d=d: _cp(e, 
                v_ctail[:, d * 4:d * 4 + 3], v_xraw[:, TH:TH + 3]), inc=sm["cvd"])

            def conv(e, d=d):
                e.tensor_scalar(v_xcw, v_xraw[:, 3:3 + TH],
                                s_convw[:, d * 4 + 3:d * 4 + 4],
                                s_convb[:, d:d + 1], ALU.mult, ALU.add)
                for k in range(3):
                    ins = e.scalar_tensor_tensor(
                        v_xcw, v_xraw[:, k:k + TH], s_convw[:, d * 4 + k:d * 4 + k + 1],
                        v_xcw, ALU.mult, ALU.add)
                return ins
            P.emit("vector", conv, waits=[(sm["xcd"], sm["xcd"].n),
                                          (sm["cvd"], sm["cvd"].n)], inc=sm["cvd"])
            cvv = sm["cvd"].n
            P.emit("scalar", lambda e: e.activation(v_xcw, v_xcw, AF.Silu),
                   waits=[(sm["cvd"], cvv)], inc=sm["sil"])
            silv = sm["sil"].n
            P.emit("sync", lambda e, d=d, th=th: e.dma_start(
                out=xc_d[d * 128:(d + 1) * 128, th * TH:(th + 1) * TH], in_=v_xcw),
                waits=[(sm["sil"], silv)], inc=sm["xcd"], inc_k=16)
            for ts in range(2):
                def mmssm(e, d=d, ts=ts):
                    sl = v_xcw[:, ts * 512:(ts + 1) * 512]
                    e.matmul(p_ssm[:, 0:512], v_wxp[:, d * 288:d * 288 + 128], sl,
                             start=True, stop=True)
                    e.matmul(p_ssm[:, 512:1024], v_wxp[:, d * 288 + 128:d * 288 + 256], sl,
                             start=True, stop=True)
                    return e.matmul(p_ssm[0:32, 1024:1536],
                                    v_wxp[:, d * 288 + 256:d * 288 + 288],
                                    sl, start=True, stop=True)
                P.emit("tensor", mmssm, waits=[(sm["sil"], silv), (sm["ssa"], sm["ssa"].n)],
                       inc=sm["psd"])

                def ssmacc(e, ts=ts):
                    dst = v_ssm.rearrange("p (g t) -> p g t", g=3)[:, :, ts * 512:(ts + 1) * 512]
                    src = p_ssm[:].rearrange("p (g t) -> p g t", g=3)
                    return e.scalar_tensor_tensor(dst, src, 1.0, dst, ALU.mult, ALU.add)
                P.emit("vector", ssmacc, waits=[(sm["psd"], sm["psd"].n),
                                                (sm["ssa"], ssm_clear)], inc=sm["ssa"])
            P.emit("gpsimd", dma(wb.rearrange("p (k c) -> p k c", k=KC), wxz[ND + d]),
                   waits=[(sm["pxz"], sm["pxz"].n)], inc=sm["wld"], inc_k=16)
            wv2 = sm["wld"].n
            for ts in range(2):
                def mmz(e, ts=ts, wb=wb):
                    for k in range(KC):
                        ins = e.matmul(p_z[:], wb[:, k * 128:(k + 1) * 128],
                                       v_xn[:, k * TH + ts * 512:k * TH + (ts + 1) * 512],
                                       start=(k == 0), stop=(k == KC - 1))
                    return ins
                P.emit("tensor", mmz, waits=[(sm["wld"], wv2), (sm["sil"], sm["sil"].n)],
                       inc=sm["pxz"])
                P.emit("scalar", lambda e, ts=ts: e.activation(
                    v_xraw[:, 3 + ts * 512:3 + (ts + 1) * 512], p_z[:], AF.Silu),
                    waits=[(sm["pxz"], sm["pxz"].n), (sm["cvd"], cvv),
                           (sm["xcd"], sm["xcd"].n)], inc=sm["sil"])
            pxz_after_d[(th, d)] = sm["pxz"].n
            P.emit("sync", lambda e, d=d, th=th: e.dma_start(
                out=zs_d[d * 128:(d + 1) * 128, th * TH:(th + 1) * TH],
                in_=v_xraw[:, 3:3 + TH]),
                waits=[(sm["sil"], sm["sil"].n)], inc=sm["xcd"], inc_k=16)
        P.wait("sync", (sm["ssa"], sm["ssa"].n))
        P.emit("sync", lambda e, th=th: e.dma_start(out=ssm_in[th][0:128, :], in_=v_ssm[:, 0:TH]),
               inc=sm["ssmd"], inc_k=16)
        P.emit("sync", lambda e, th=th: e.dma_start(out=ssm_in[th][128:256, :],
                                                    in_=v_ssm[:, TH:2 * TH]),
               inc=sm["ssmd"], inc_k=16)
        P.emit("sync", lambda e, th=th: e.dma_start(out=ssm_in[th][256:288, :],
                                                    in_=v_ssm[0:32, 2 * TH:3 * TH]),
               inc=sm["ssmd"], inc_k=16)
        P.emit("gpsimd", lambda e, th=th: e.collective_compute(
            "AllReduce", ALU.add, replica_groups=RG,
            ins=[ssm_in[th].opt()], outs=[ssm_out[th].opt()]),
            waits=[(sm["ssmd"], sm["ssmd"].n)], inc=sm["cc"])

    def phase_N(th, ccv):
        nr = sm["nrm"]
        P.emit("sync", lambda e, th=th: e.dma_start(
            out=v_dt2[:, 0:TH], in_=ssm_out[th][0:128, :]),
            waits=[(sm["cc"], ccv), (sm["xnf"], sm["xnf"].n)], inc=nr, inc_k=16)
        P.emit("sync", lambda e, th=th: e.dma_start(
            out=v_dt2[:, TH:2 * TH], in_=ssm_out[th][128:256, :]),
            waits=[(sm["cc"], ccv)], inc=nr, inc_k=16)
        P.emit("sync", lambda e, th=th: e.dma_start(out=v_braw[0:16, :],
                                                    in_=ssm_out[th][256:272, :]),
               waits=[(sm["cc"], ccv)], inc=nr, inc_k=16)
        P.emit("sync", lambda e, th=th: e.dma_start(out=v_craw[0:16, :],
                                                    in_=ssm_out[th][272:288, :]),
               waits=[(sm["cc"], ccv)], inc=nr, inc_k=16)
        base = nr.n
        P.emit("scalar", lambda e: e.activation(v_nsq[:, 0:2 * TH], v_dt2, AF.Square),
               waits=[(nr, base)], inc=nr)

        def msum(e):
            last = None
            for c in range(2):
                pp = p_dt if c == 0 else p_bc
                for k in range(2):
                    last = e.matmul(pp[0:1, :], s_o128c[:],
                                    v_nsq[:, k * TH + c * 512:k * TH + (c + 1) * 512],
                                    start=(k == 0), stop=(k == 1))
            return last
        P.emit("tensor", msum, waits=[(nr, nr.n)], inc=nr)

        def rfun(e):
            e.tensor_scalar(v_r1[0:1, 0:512], p_dt[0:1, :], 1.0 / R, EPS_ATTN, ALU.mult, ALU.add)
            return e.tensor_scalar(v_r1[0:1, 512:1024], p_bc[0:1, :], 1.0 / R, EPS_ATTN,
                                   ALU.mult, ALU.add)
        P.emit("vector", rfun, waits=[(nr, nr.n)], inc=nr)
        P.emit("scalar", lambda e: e.activation(v_r1[0:1, :], v_r1[0:1, :], AF.Sqrt),
               waits=[(nr, nr.n)], inc=nr)
        P.emit("vector", lambda e: e.reciprocal(v_r1[0:1, :], v_r1[0:1, :]),
               waits=[(nr, nr.n)], inc=nr)
        rv = nr.n
        for c in range(2):
            P.emit("tensor", lambda e, c=c: e.matmul(
                p_bc[:] if c else p_dt[:], s_o1r[:], v_r1[0:1, c * 512:(c + 1) * 512],
                start=True, stop=True), waits=[(nr, rv)], inc=nr)
            P.emit("scalar", lambda e, c=c: _cp(e, 
                v_rb[:, c * 512:(c + 1) * 512], p_bc[:] if c else p_dt[:]),
                waits=[(nr, nr.n)], inc=nr)
        bv = nr.n

        def dtrn(e, th=th):
            last = None
            for k in range(2):
                last = e.scalar_tensor_tensor(
                    s_dtrn[th][:, k * TH:(k + 1) * TH], v_dt2[:, k * TH:(k + 1) * TH],
                    s_dtln[:, k:k + 1], v_rb, ALU.mult, ALU.mult)
            return last
        P.emit("vector", dtrn, waits=[(nr, bv), (sm["dtm"], sm["dtm"].n)], inc=nr)
        for (raw, lnw, dstn) in ((v_braw, s_bln, s_bn), (v_craw, s_cln, s_cn)):
            P.emit("scalar", lambda e, raw=raw: e.activation(
                v_nsq[0:16, 0:TH], raw[0:16, :], AF.Square), waits=[(nr, nr.n)], inc=nr)

            def bsum(e):
                e.matmul(p_dt[0:1, :], s_o16c[:], v_nsq[0:16, 0:512], start=True, stop=True)
                return e.matmul(p_bc[0:1, :], s_o16c[:], v_nsq[0:16, 512:1024],
                                start=True, stop=True)
            P.emit("tensor", bsum, waits=[(nr, nr.n)], inc=nr)

            def brs(e):
                e.tensor_scalar(v_r1[0:1, 0:512], p_dt[0:1, :], 1.0 / NST, EPS_ATTN,
                                ALU.mult, ALU.add)
                return e.tensor_scalar(v_r1[0:1, 512:1024], p_bc[0:1, :], 1.0 / NST, EPS_ATTN,
                                       ALU.mult, ALU.add)
            P.emit("vector", brs, waits=[(nr, nr.n)], inc=nr)
            P.emit("scalar", lambda e: e.activation(v_r1[0:1, :], v_r1[0:1, :], AF.Sqrt),
                   waits=[(nr, nr.n)], inc=nr)
            P.emit("vector", lambda e: e.reciprocal(v_r1[0:1, :], v_r1[0:1, :]),
                   waits=[(nr, nr.n)], inc=nr)

            def bc16(e):
                e.matmul(p_dt[0:16, :], s_o1r[0:1, 0:16], v_r1[0:1, 0:512],
                         start=True, stop=True)
                return e.matmul(p_bc[0:16, :], s_o1r[0:1, 0:16], v_r1[0:1, 512:1024],
                                start=True, stop=True)
            P.emit("tensor", bc16, waits=[(nr, nr.n)], inc=nr)

            def bevac(e):
                _cp(e, v_rb16[0:16, 0:512], p_dt[0:16, :])
                return _cp(e, v_rb16[0:16, 512:1024], p_bc[0:16, :])
            P.emit("scalar", bevac, waits=[(nr, nr.n)], inc=nr)
            P.emit("vector", lambda e, raw=raw, lnw=lnw, dstn=dstn: e.scalar_tensor_tensor(
                dstn[:], raw[0:16, :], lnw[:], v_rb16[0:16, :], ALU.mult, ALU.mult),
                waits=[(nr, nr.n)], inc=nr)

    xcr_hist = {}
    xcr_marks = []
    cs_pend = [None]
    pend_y = [None]
    cs_slot = 0

    def phase_B(th):
        nonlocal cs_slot
        if th == 0:
            P.emit("gpsimd", dma(v_wdt.rearrange("p (n e) -> p n e", n=2), wdtT),
                   waits=[(sm["pxz"], sm["pxz"].n)], inc=sm["wld"], inc_k=16)
            P.emit("vector", lambda e: e.memset(
                v_da.rearrange("p (n t) -> p n t", n=NH)[:, :, 0:1], 0.0),
                waits=[(sm["pxz"], sm["pxz"].n), (sm["xnf"], sm["xnf"].n)], inc=sm["dbd"])
            P.emit("vector", lambda e: e.memset(
                v_da.rearrange("p (n t) -> p n t", n=NH)[:, :, 1:2], 1.0), inc=sm["dbd"])
        sep_done = sm["dbd"].n
        wdt_v = sm["wld"].n
        for ts in range(2):
            t0 = ts * TS
            for (src, dstb) in ((s_bn, v_bb), (s_cn, v_cb)):
                for n in range(NST):
                    def bcast(e, n=n, src=src, t0=t0):
                        return e.matmul(p_bc[:], s_oh16[:, n * 128:(n + 1) * 128],
                                        src[:, t0:t0 + TS], start=True, stop=True)
                    P.emit("tensor", bcast, waits=[(sm["nrm"], sm["nrm"].n),
                                                   (sm["bcb"], sm["bcb"].n)], inc=sm["bcb"])
                    P.emit("scalar", lambda e, n=n, dstb=dstb: _cp(e, 
                        dstb[:, n * TS:(n + 1) * TS], p_bc[:]),
                        waits=[(sm["bcb"], sm["bcb"].n), (sm["dbd"], sm["dbd"].n),
                               (sm["scn"], sm["scn"].n)], inc=sm["bcb"])
            bcv = sm["bcb"].n
            for d in range(ND):
                wpre = [(sm["scn"], xcr_marks[-1])] if xcr_marks else []
                P.emit("sync", lambda e, d=d, th=th, t0=t0: e.dma_start(
                    out=v_xcr[d % 2],
                    in_=xc_d[d * 128:(d + 1) * 128, th * TH + t0:th * TH + t0 + TS]),
                    waits=wpre, inc=sm["xcr"], inc_k=16)
                P.emit("sync", lambda e, d=d, th=th, t0=t0: e.dma_start(
                    out=v_zsr[d % 2],
                    in_=zs_d[d * 128:(d + 1) * 128, th * TH + t0:th * TH + t0 + TS]),
                    inc=sm["xcr"], inc_k=16)
                xcrv = sm["xcr"].n

                def mmdt(e, d=d, th=th, t0=t0):
                    for k in range(2):
                        ins = e.matmul(
                            p_dt[:], v_wdt[:, k * DH + d * 128:k * DH + (d + 1) * 128],
                            s_dtrn[th][:, k * TH + t0:k * TH + t0 + TS],
                            start=(k == 0), stop=(k == 1))
                    return ins
                P.emit("tensor", mmdt, waits=[(sm["wld"], wdt_v), (sm["nrm"], sm["nrm"].n),
                                              (sm["spd"], sm["spd"].n), (sm["bcb"], bcv)],
                       inc=sm["dtm"])
                dtv_ = sm["dtm"].n

                def sp(e, d=d):
                    e.activation(v_spt[:, 0:512], p_dt[:], AF.Abs, bias=s_dtb[:, d:d + 1])
                    e.activation(v_spt[:, 0:512], v_spt[:, 0:512], AF.Exp, scale=-1.0)
                    e.activation(v_spt[:, 0:512], v_spt[:, 0:512], AF.Ln, bias=1.0)
                    return e.activation(v_spt2[:, 0:512], p_dt[:], AF.Relu,
                                        bias=s_dtb[:, d:d + 1])
                P.emit("scalar", sp, waits=[(sm["dtm"], dtv_), (sm["dad"], sm["dad"].n)],
                       inc=sm["spd"])
                P.emit("vector", lambda e: e.tensor_add(v_dtv, v_spt2[:, 0:512], v_spt[:, 0:512]),
                       waits=[(sm["spd"], sm["spd"].n)], inc=sm["spd"])
                spv = sm["spd"].n
                P.emit("vector", lambda e, d=d: e.tensor_mul(v_u, v_dtv, v_xcr[d % 2]),
                       waits=[(sm["spd"], spv), (sm["xcr"], xcrv)], inc=sm["dbd"])

                def drain(last=False):
                    if pend_y[0] is None:
                        return
                    pd, pnh, plv = pend_y[0]
                    pend_y[0] = None

                    def yredB(e, pd=pd, pnh=pnh):
                        p3 = v_p.rearrange("p (n t) -> p n t", n=NH)
                        e.tensor_add(p3[:, 0:2, 2:LEN], p3[:, 0:2, 2:LEN], p3[:, 2:4, 2:LEN])
                        e.tensor_add(p3[:, 0:1, 2:LEN], p3[:, 0:1, 2:LEN], p3[:, 1:2, 2:LEN])
                        if pnh == 0:
                            return e.tensor_copy(v_ys, p3[:, 0, 2:LEN])
                        return e.tensor_add(v_ys, v_ys, p3[:, 0, 2:LEN])
                    P.emit("vector", yredB, waits=[(sm["pl"], plv)], inc=sm["scn"])
                    if pnh == 1:
                        def yfin(e, pd=pd):
                            e.scalar_tensor_tensor(v_ys, v_xcr[pd % 2], s_dpar[:, pd:pd + 1],
                                                   v_ys, ALU.mult, ALU.add)
                            return e.tensor_mul(v_yf, v_ys, v_zsr[pd % 2])
                        P.emit("vector", yfin, waits=[(sm["yrd"], sm["yrd"].n)], inc=sm["scn"])
                        xcr_marks.append(sm["scn"].n)
                        P.emit("sync", lambda e, pd=pd, th=th, t0=t0: e.dma_start(
                            out=yag_in[th][pd * 128:(pd + 1) * 128, t0:t0 + TS], in_=v_yf),
                            waits=[(sm["scn"], sm["scn"].n)], inc=sm["yrd"], inc_k=16)

                for nh in range(2):
                    def mkda(e, d=d, nh=nh):
                        for nn in range(NH):
                            n = nh * NH + nn
                            ins = e.activation(
                                v_da.rearrange("p (n t) -> p n t", n=NH)[:, nn, 2:LEN],
                                v_dtv, AF.Exp, scale=s_aneg[:, d * NST + n:d * NST + n + 1])
                        return ins
                    P.emit("scalar", mkda, waits=[(sm["spd"], spv), (sm["scn"], sm["scn"].n),
                                                  (sm["setup"], SETUP_N)], inc=sm["dad"])

                    def mkdbx(e, d=d, nh=nh):
                        dst = v_dbx.rearrange("p (n t) -> p n t", n=NH)[:, :, 2:LEN]
                        ub = v_u.unsqueeze(1).broadcast_to([128, NH, TS])
                        bb = v_bb.rearrange("p (n t) -> p n t", n=NST)[:, nh * NH:(nh + 1) * NH, :]
                        e.tensor_tensor(dst, ub, bb, ALU.mult)
                        e.tensor_copy(
                            v_dbx.rearrange("p (n t) -> p n t", n=NH)[:, :, 0],
                            v_chi[:, d * NST + nh * NH:d * NST + nh * NH + NH])
                        return e.tensor_copy(
                            v_dbx.rearrange("p (n t) -> p n t", n=NH)[:, :, 1],
                            v_clo[:, d * NST + nh * NH:d * NST + nh * NH + NH])
                    P.emit("vector", mkdbx, waits=[(sm["bcb"], bcv), (sm["scn"], sm["scn"].n),
                                                   (sm["setup"], SETUP_N),
                                                   (sm["dbd"], sm["dbd"].n)], inc=sm["dbd"])
                    P.emit("vector", lambda e: e.tensor_tensor_scan(
                        v_h, v_da, v_dbx, 0.0, ALU.mult, ALU.add),
                        waits=[(sm["dad"], sm["dad"].n), (sm["dbd"], sep_done),
                               (sm["dbd"], sm["dbd"].n)], inc=sm["scn"])

                    def csaveA(e, d=d, nh=nh, pend=cs_pend[0]):
                        sl = slice(d * NST + nh * NH, d * NST + nh * NH + NH)
                        hl = v_h.rearrange("p (n t) -> p n t", n=NH)[:, :, LEN - 1]
                        ins = e.tensor_copy(v_chi[:, sl], hl)
                        if pend is not None:
                            psl, pslot = pend
                            ins = e.tensor_copy(v_clo[:, psl], v_lo32[:, pslot * NH:(pslot + 1) * NH])
                        return ins
                    P.emit("vector", csaveA, waits=[(sm["scn"], sm["scn"].n)], inc=sm["scn"])
                    cs_slot = (cs_slot + 1) % 2

                    def csaveB(e, d=d, nh=nh, slot=cs_slot):
                        sl = slice(d * NST + nh * NH, d * NST + nh * NH + NH)
                        hl = v_h.rearrange("p (n t) -> p n t", n=NH)[:, :, LEN - 1]
                        return e.tensor_tensor(v_lo32[:, slot * NH:(slot + 1) * NH],
                                               hl, v_chi[:, sl], ALU.subtract)
                    P.emit("vector", csaveB, waits=[(sm["scn"], sm["scn"].n)], inc=sm["scn"])
                    cs_pend[0] = (slice(d * NST + nh * NH, d * NST + nh * NH + NH), cs_slot)

                    drain()

                    def ymulP(e, nh=nh):
                        hv = v_h.rearrange("p (n t) -> p n t", n=NH)[:, :, 2:LEN]
                        p3 = v_p.rearrange("p (n t) -> p n t", n=NH)
                        cb = v_cb.rearrange("p (n t) -> p n t", n=NST)[:, nh * NH:(nh + 1) * NH, :]
                        e.tensor_tensor(p3[:, :, 2:LEN], hv, cb, ALU.mult)
                        return e.tensor_add(p3[:, 0:4, 2:LEN], p3[:, 0:4, 2:LEN],
                                            p3[:, 4:8, 2:LEN])
                    P.emit("gpsimd", ymulP, waits=[(sm["scn"], sm["scn"].n)], inc=sm["pl"])
                    pend_y[0] = (d, nh, sm["pl"].n)
            drain(last=True)
            def csflush(e, pend=cs_pend[0]):
                psl, pslot = pend
                return e.tensor_copy(v_clo[:, psl], v_lo32[:, pslot * NH:(pslot + 1) * NH])
            P.emit("vector", csflush, waits=[(sm["scn"], sm["scn"].n)], inc=sm["scn"])
            cs_pend[0] = None
            if DEBUG and th == 0:
                mids = [(f"d_chi{ts}", v_chi), (f"d_clo{ts}", v_clo)]
                if ts == 0:
                    mids += [("d_da0", v_da), ("d_dbx0", v_dbx), ("d_h0", v_h)]
                for k, (nm, srcT) in enumerate(mids):
                    P.emit("gpsimd", lambda e, nm=nm, srcT=srcT: e.dma_start(
                        out=dbg[nm], in_=srcT),
                        waits=[(sm["scn"], sm["scn"].n)] if k == 0 else [],
                        inc=sm["outd"], inc_k=16)
                P.emit("vector", lambda e: e.tensor_copy(v_dtv, v_dtv),
                       waits=[(sm["outd"], sm["outd"].n)], inc=sm["scn"])
        P.emit("gpsimd", lambda e, th=th: e.collective_compute(
            "AllGather", ALU.bypass, replica_groups=RG,
            ins=[yag_in[th].opt()], outs=[yag_out[th].opt()]),
            waits=[(sm["yrd"], sm["yrd"].n)], inc=sm["cc"])

    omm_hist = {}

    def phase_C(ccv):
        for tb in range(2):
            def ldy(e, tb=tb):
                v = e.value_load(s_flag[0:1, 0:1])
                for half in range(2):
                    src = yag_out[half][:, tb * 512:(tb + 1) * 512].rearrange(
                        "(k p) t -> p k t", p=128)
                    e.dma_start(out=v_yres[:].rearrange("p (k t) -> p k t", k=2 * ND),
                                in_=src, cond=(v == half)).then_inc(sm["yrl"].h, 16)
            w = [(sm["cc"], ccv), (sm["scn"], sm["scn"].n)]
            if tb > 0:
                w.append((sm["omm"], omm_hist[(tb - 1, KC - 1)]))
            wt = tuple((s.h, v) for (s, v) in w)

            def ldy_run(e, ldy=ldy, wt=wt):
                for sh, v in wt:
                    e.wait_ge(sh, v)
                ldy(e)
            P.ops["sync"].append(ldy_run)
            sm["yrl"].n += 16 * 2
            ylv = sm["yrl"].n
            for hch in range(KC):
                wo = v_wo[hch % 2]
                key = (tb, hch - 2) if hch >= 2 else ((tb - 1, KC - 2 + hch) if tb > 0 else None)
                wfree = [(sm["omm"], omm_hist[key])] if key in omm_hist else [
                    (sm["scn"], sm["scn"].n)]
                P.emit("gpsimd", dma(wo.rearrange("p (k c) -> p k c", k=2 * ND), woT[hch]),
                       waits=wfree, inc=sm["wod"], inc_k=16)
                wv = sm["wod"].n

                def mmo(e, wo=wo, hch=hch):
                    pp = p_o[hch % 2]
                    for k in range(2 * ND):
                        ins = e.matmul(pp[:], wo[:, k * 128:(k + 1) * 128],
                                       v_yres[:, k * 512:(k + 1) * 512],
                                       start=(k == 0), stop=(k == 2 * ND - 1))
                    return ins
                P.emit("tensor", mmo, waits=[(sm["wod"], wv), (sm["yrl"], ylv),
                                             (sm["oev"], sm["oev"].n)], inc=sm["omm"])
                omm_hist[(tb, hch)] = sm["omm"].n
                P.emit("scalar", lambda e, hch=hch: _cp(e, v_sev, p_o[hch % 2][:]),
                       waits=[(sm["omm"], sm["omm"].n), (sm["oev"], sm["oev"].n)],
                       inc=sm["oev"])
                P.emit("sync", lambda e, hch=hch, tb=tb: e.dma_start(
                    out=s_d[hch * 128:(hch + 1) * 128, tb * 512:(tb + 1) * 512], in_=v_sev),
                    waits=[(sm["oev"], sm["oev"].n)], inc=sm["oev"], inc_k=16)

    def phase_FFN():
        sv, sa, sp = sm["ffv"], sm["ffa"], sm["ffp"]
        phase_LN(0, x_mine, ffn=True)
        xnv = sm["xnf"].n
        if FFN_STAGE < 2:
            return
        for c in range(2):
            def mml(e, c=c):
                for k in range(KC):
                    ins = e.matmul(p_dt[0:24, :], s_wres[:, k * 24:(k + 1) * 24],
                                   v_xnm[:, k * TH + c * 512:k * TH + (c + 1) * 512],
                                   start=(k == 0), stop=(k == KC - 1))
                return ins
            P.emit("tensor", mml, waits=[(sm["xnf"], xnv), (sa, sa.n),
                                         (sm["nrm"], sm["nrm"].n)], inc=sp)
            P.emit("scalar", lambda e, c=c: _cp(e, 
                v_lgst16[0:24, c * 512:(c + 1) * 512], p_dt[0:24, :]),
                waits=[(sp, sp.n)], inc=sa)
        lgv = sa.n
        for i in range(8):
            P.emit("tensor", lambda e, i=i: e.transpose(
                p_tp[:, 0:24], v_lgst16[0:24, i * 128:(i + 1) * 128], s_id[0:24, 0:24]),
                waits=[(sa, lgv), (sa, sa.n)], inc=sp)
            P.emit("scalar", lambda e, i=i: _cp(e, 
                v_lg[:, i * 24:(i + 1) * 24], p_tp[:, 0:24]),
                waits=[(sp, sp.n)], inc=sa)
        if FFN_STAGE < 3:
            return
        P.emit("scalar", lambda e: e.activation(s_small[0:1, 25:26], s_small[0:1, 24:25],
                                                AF.Sigmoid), inc=sa)

        for (dst, srcsl) in ((v_baseb[:, 0:24], s_small[0:1, 0:24]),
                             (v_eyeb[:, 0:16], s_small[0:1, 32:48]),
                             (v_gcol[:, 0:1], s_small[0:1, 25:26])):
            P.emit("tensor", lambda e, srcsl=srcsl: e.matmul(
                p_bc[0:128, 0:srcsl.shape[1]], s_o1r[:], srcsl, start=True, stop=True),
                waits=[(sa, sa.n)], inc=sp)
            P.emit("scalar", lambda e, dst=dst, srcsl=srcsl: _cp(
                e, dst, p_bc[0:128, 0:srcsl.shape[1]]),
                waits=[(sp, sp.n)], inc=sa)
        bsv = sa.n

        def mkz(e):
            lg3 = v_lg.rearrange("p (i e) -> p i e", i=8)
            for i in range(8):
                e.tensor_scalar_mul(lg3[:, i, :], lg3[:, i, :], v_rs2[:, i:i + 1])
            bb = v_baseb[:, 0:24].unsqueeze(1).broadcast_to([128, 8, 24])
            e.tensor_tensor(lg3, lg3, bb, ALU.add)
            return e.tensor_scalar_mul(
                v_zt.rearrange("p (i g) -> p i g", i=8), lg3[:, :, 0:16], 1.0 / SINK_TAU)
        P.emit("vector", mkz, waits=[(sa, bsv), (sv, sv.n)], inc=sv)
        if FFN_STAGE < 4:
            return
        P.emit("vector", lambda e: e.memset(v_uu[:, 0:32], 0.0), inc=sv)
        P.emit("vector", lambda e: e.memset(v_vv[:, 0:32], 0.0), inc=sv)
        z4 = v_zt.rearrange("p (i j k) -> p i j k", i=8, j=4)
        e4 = v_e1.rearrange("p (i j k) -> p i j k", i=8, j=4)
        e4T = v_e1.rearrange("p (i j k) -> p i k j", i=8, j=4)
        u4 = v_uu[:, 0:32].rearrange("p (i j) -> p i j", i=8)
        v4 = v_vv[:, 0:32].rearrange("p (i j) -> p i j", i=8)
        m4 = v_m1[:, 0:32].rearrange("p (i j) -> p i j", i=8)
        s4 = v_s1[:, 0:32].rearrange("p (i j) -> p i j", i=8)
        SINK_HALF = globals().get('SINK_HALF', False)
        for it in range(globals().get('SINK_IT_OVR', SINK_ITERS)):
            def ustep_a(e):
                vb = v4.unsqueeze(2).broadcast_to([128, 8, 4, 4])
                e.tensor_tensor(e4, z4, vb, ALU.add)
                return e.reduce_max(m4, e4, axis=AX.X)
            P.emit("vector", ustep_a, waits=[(sa, sa.n), (sv, sv.n)], inc=sv)

            def ustep_b(e):
                mb = m4.unsqueeze(3).broadcast_to([128, 8, 4, 4])
                return e.tensor_tensor(e4, e4, mb, ALU.subtract)
            P.emit("vector", ustep_b, waits=[(sv, sv.n)], inc=sv)
            P.emit("scalar", lambda e: e.activation(v_e1[:, 0:128], v_e1[:, 0:128], AF.Exp),
                   waits=[(sv, sv.n)], inc=sa)
            P.emit("vector", lambda e: e.reduce_sum(s4, e4, axis=AX.X),
                   waits=[(sa, sa.n)], inc=sv)
            P.emit("scalar", lambda e: e.activation(v_s1[:, 0:32], v_s1[:, 0:32], AF.Ln),
                   waits=[(sv, sv.n)], inc=sa)
            P.emit("vector", lambda e: e.scalar_tensor_tensor(
                v_uu[:, 0:32], v_m1[:, 0:32], -1.0, v_s1[:, 0:32], ALU.mult, ALU.subtract),
                waits=[(sa, sa.n)], inc=sv)

            if SINK_HALF:
                continue

            def vstep_a(e):
                ub = u4.unsqueeze(3).broadcast_to([128, 8, 4, 4])
                e.tensor_tensor(e4, z4, ub, ALU.add)
                return e.reduce_max(m4, e4T, axis=AX.X)
            P.emit("vector", vstep_a, waits=[(sv, sv.n)], inc=sv)

            def vstep_b(e):
                mb = m4.unsqueeze(3).broadcast_to([128, 8, 4, 4])
                return e.tensor_tensor(e4T, e4T, mb, ALU.subtract)
            P.emit("vector", vstep_b, waits=[(sv, sv.n)], inc=sv)
            P.emit("scalar", lambda e: e.activation(v_e1[:, 0:128], v_e1[:, 0:128], AF.Exp),
                   waits=[(sv, sv.n)], inc=sa)
            P.emit("vector", lambda e: e.reduce_sum(s4, e4T, axis=AX.X),
                   waits=[(sa, sa.n)], inc=sv)
            P.emit("scalar", lambda e: e.activation(v_s1[:, 0:32], v_s1[:, 0:32], AF.Ln),
                   waits=[(sv, sv.n)], inc=sa)
            P.emit("vector", lambda e: e.scalar_tensor_tensor(
                v_vv[:, 0:32], v_m1[:, 0:32], -1.0, v_s1[:, 0:32], ALU.mult, ALU.subtract),
                waits=[(sa, sa.n)], inc=sv)

        if FFN_STAGE < 5:
            return

        def hfin(e):
            ub = u4.unsqueeze(3).broadcast_to([128, 8, 4, 4])
            vb = v4.unsqueeze(2).broadcast_to([128, 8, 4, 4])
            e.tensor_tensor(e4, z4, ub, ALU.add)
            return e.tensor_tensor(e4, e4, vb, ALU.add)
        P.emit("vector", hfin, waits=[(sv, sv.n)], inc=sv)
        P.emit("scalar", lambda e: e.activation(v_e1[:, 0:128], v_e1[:, 0:128], AF.Exp),
               waits=[(sv, sv.n)], inc=sa)

        def blend1(e):
            e.tensor_scalar_mul(v_e1[:, 0:128], v_e1[:, 0:128], v_gcol[:, 0:1])
            return e.tensor_scalar(v_omg[:, 0:1], v_gcol[:, 0:1], -1.0, 1.0, ALU.mult, ALU.add)
        P.emit("vector", blend1, waits=[(sa, sa.n)], inc=sv)

        def blend2(e):
            return e.scalar_tensor_tensor(
                v_hres16[:, 0:128].rearrange("p (i e) -> p i e", i=8),
                v_eyeb[:, 0:16].unsqueeze(1).broadcast_to([128, 8, 16]),
                v_omg[:, 0:1],
                v_e1[:, 0:128].rearrange("p (i e) -> p i e", i=8), ALU.mult, ALU.add)
        P.emit("vector", blend2, waits=[(sv, sv.n)], inc=sv)

        lg3s = v_lg.rearrange("p (i e) -> p i e", i=8)
        for (c0, off) in ((16, 0), (20, 32)):
            P.emit("vector", lambda e, c0=c0: e.tensor_copy(
                v_sp1[:, 0:32].rearrange("p (i j) -> p i j", i=8),
                lg3s[:, :, c0:c0 + 4]),
                waits=[(sv, sv.n), (sa, sa.n)], inc=sv)
            P.emit("vector", lambda e: e.reduce_max(
                v_smx[:, 0:8], v_sp1[:, 0:32].rearrange("p (i j) -> p i j", i=8),
                axis=AX.X), waits=[(sv, sv.n)], inc=sv)
            P.emit("vector", lambda e: e.tensor_tensor(
                v_sp1[:, 0:32].rearrange("p (i j) -> p i j", i=8),
                v_sp1[:, 0:32].rearrange("p (i j) -> p i j", i=8),
                v_smx[:, 0:8].unsqueeze(2).broadcast_to([128, 8, 4]),
                ALU.subtract), waits=[(sv, sv.n)], inc=sv)
            P.emit("scalar", lambda e: e.activation(v_sp2[:, 0:32], v_sp1[:, 0:32], AF.Exp),
                   waits=[(sv, sv.n)], inc=sa)

            P.emit("vector", lambda e: e.reduce_sum(
                v_smx[:, 0:8], v_sp2[:, 0:32].rearrange("p (i j) -> p i j", i=8),
                axis=AX.X), waits=[(sa, sa.n)], inc=sv)
            P.emit("vector", lambda e: e.reciprocal(v_smx[:, 0:8], v_smx[:, 0:8]),
                   waits=[(sv, sv.n)], inc=sv)
            P.emit("vector", lambda e, off=off: e.tensor_tensor(
                hpp16[:, off:off + 32].rearrange("p (i j) -> p i j", i=8),
                v_sp2[:, 0:32].rearrange("p (i j) -> p i j", i=8),
                v_smx[:, 0:8].unsqueeze(2).broadcast_to([128, 8, 4]),
                ALU.mult), waits=[(sv, sv.n)], inc=sv)
        hv = sv.n
        if FFN_STAGE < 6:
            return
        for i in range(8):
            def tph(e, i=i):
                e.transpose(p_tp[0:16, 0:128], v_hres16[:, i * 16:(i + 1) * 16], s_id[:])
                e.transpose(p_tp[0:4, 128:256], hpp16[:, i * 4:(i + 1) * 4], s_id[:])
                return e.transpose(p_tp[0:4, 256:384], hpp16[:, 32 + i * 4:32 + (i + 1) * 4],
                                   s_id[:])
            P.emit("tensor", tph, waits=[(sv, hv), (sa, sa.n), (sm["scn"], sm["scn"].n)],
                   inc=sp)

            def evh(e, i=i):
                _cp(e, s_bn[:, i * 128:(i + 1) * 128], p_tp[0:16, 0:128])
                _cp(e, s_cn[0:4, i * 128:(i + 1) * 128], p_tp[0:4, 128:256])
                return _cp(e, s_hpo[0:4, i * 128:(i + 1) * 128], p_tp[0:4, 256:384])
            P.emit("scalar", evh, waits=[(sp, sp.n)], inc=sa)
        hrv = sa.n
        P.emit("gpsimd", dma(v_wbr.rearrange("p (n e) -> p n e", n=8), wbr),
               waits=[(sm["omm"], sm["omm"].n), (sm["xnf"], sm["xnf"].n),
                      (sm["tpd"], sm["tpd"].n), (sm["ffp"], sm["ffp"].n)],
               inc=sm["ffw"], inc_k=16)

        def emit_bcast_row(src, r, dst_off, extra=(), kdim=16):
            def mm2(e, src=src, r=r, kdim=kdim):
                e.matmul(p_bc[:], s_oh16[0:kdim, r * 128:(r + 1) * 128],
                         src[0:kdim, 0:512], start=True, stop=True)
                return e.matmul(p_dt[:], s_oh16[0:kdim, r * 128:(r + 1) * 128],
                                src[0:kdim, 512:1024], start=True, stop=True)
            P.emit("tensor", mm2, waits=[(sa, hrv), (sa, sa.n)] + list(extra), inc=sp)

            def ev2(e, dst_off=dst_off):
                _cp(e, v_hb[:, dst_off:dst_off + 512], p_bc[:])
                return _cp(e, v_hb[:, dst_off + 512:dst_off + TH], p_dt[:])
            P.emit("scalar", ev2, waits=[(sp, sp.n), (sv, sv.n)], inc=sa)

        if FFN_STAGE < 7:
            return
        for j in range(4):
            emit_bcast_row(s_cn, j, j * TH)
        bpv = sa.n
        if FFN_STAGE < 7.5:
            return
        for ci in range(8):
            def xpre(e, ci=ci):
                acc = v_xpre[:, ci * TH:(ci + 1) * TH]
                e.tensor_tensor(acc, v_xnm[:, ci * TH:(ci + 1) * TH],
                                v_hb[:, 0:TH], ALU.mult)
                for j in range(1, 4):
                    e.tensor_tensor(v_yf2, v_xnm[:, (j * 8 + ci) * TH:(j * 8 + ci + 1) * TH],
                                    v_hb[:, j * TH:(j + 1) * TH], ALU.mult)
                    ins = e.tensor_add(acc, acc, v_yf2)
                return ins
            P.emit("vector", xpre, waits=[(sa, bpv), (sm["wod"], sm["wod"].n)], inc=sv)
        xpv = sv.n
        if FFN_STAGE < 8:
            return
        for co in range(8):
            for tc in range(2):
                def mmbr(e, co=co, tc=tc):
                    pp = p_o[0]
                    for k in range(8):
                        ins = e.matmul(pp[:],
                                       v_wbr[:, k * 1024 + co * 128:k * 1024 + (co + 1) * 128],
                                       v_xpre[:, k * TH + tc * 512:k * TH + (tc + 1) * 512],
                                       start=(k == 0), stop=(k == 7))
                    return ins
                P.emit("tensor", mmbr, waits=[(sv, xpv), (sm["ffw"], sm["ffw"].n),
                                              (sa, sa.n), (sm["oev"], sm["oev"].n)], inc=sp)
                P.emit("scalar", lambda e, co=co, tc=tc: _cp(e, 
                    v_ybr[:, co * TH + tc * 512:co * TH + (tc + 1) * 512],
                    p_o[0][:]), waits=[(sp, sp.n)], inc=sa)
        ybv = sa.n
        if FFN_STAGE < 9:
            return
        for i in range(4):
            for j in range(4):
                emit_bcast_row(s_bn, i * 4 + j, j * TH, extra=[(sv, sv.n), (sa, sa.n)])
            emit_bcast_row(s_hpo, i, 4 * TH, extra=[(sa, sa.n)], kdim=4)
            brv = sa.n
            for ci in range(8):
                fc = i * 8 + ci
                P.emit("sync", lambda e, fc=fc: e.dma_start(
                    out=v_sld, in_=s_d[fc * 128:(fc + 1) * 128, :]),
                    waits=[(sm["oev"], sm["oev"].n), (sm["fgd"], sm["fgd"].n)],
                    inc=sm["fgd"], inc_k=16)
                sldv = sm["fgd"].n

                def mix(e, ci=ci):
                    acc = v_sgc
                    e.tensor_tensor(acc, v_xnm[:, ci * TH:(ci + 1) * TH],
                                    v_hb[:, 0:TH], ALU.mult)
                    for j in range(1, 4):
                        e.tensor_tensor(v_yf2, v_xnm[:, (j * 8 + ci) * TH:(j * 8 + ci + 1) * TH],
                                        v_hb[:, j * TH:(j + 1) * TH], ALU.mult)
                        e.tensor_add(acc, acc, v_yf2)
                    e.tensor_tensor(v_yf2, v_ybr[:, ci * TH:(ci + 1) * TH],
                                    v_hb[:, 4 * TH:5 * TH], ALU.mult)
                    return e.tensor_add(acc, acc, v_yf2)
                P.emit("vector", mix, waits=[(sa, brv), (sm["ffp"], sm["ffp"].n)], inc=sv)
                P.emit("scalar", lambda e: e.activation(v_sgc, v_sgc, AF.Silu),
                       waits=[(sv, sv.n)], inc=sa)
                P.emit("vector", lambda e: e.tensor_mul(v_sgc, v_sgc, v_sld),
                       waits=[(sa, sa.n), (sm["fgd"], sldv)], inc=sv)
                sgv = sv.n
                for g in range(2):
                    def tps(e, g=g):
                        for jj in range(4):
                            tt = g * 4 + jj
                            ins = e.transpose(p_tp[:, jj * 128:(jj + 1) * 128],
                                              v_sgc[:, tt * 128:(tt + 1) * 128], s_id[:])
                        return ins
                    P.emit("tensor", tps, waits=[(sv, sgv), (sa, sa.n),
                                                 (sm["fgd"], sm["fgd"].n)], inc=sp)
                    P.emit("scalar", lambda e: _cp(e, v_tev[:, 0:512], p_tp[:, 0:512]),
                           waits=[(sp, sp.n), (sm["fgd"], sm["fgd"].n)], inc=sa)
                    P.emit("sync", lambda e, fc=fc, g=g: e.dma_start(
                        out=sgT_d[g * 512:(g + 1) * 512, fc * 128:(fc + 1) * 128]
                        .rearrange("(a p) c -> p a c", p=128),
                        in_=v_tev[:, 0:512].rearrange("p (a c) -> p a c", a=4)),
                        waits=[(sa, sa.n)], inc=sm["fgd"], inc_k=16)

    def phase_FIN():
        fgv = sm["fgd"].n
        for i in range(8):
            P.emit("sync", lambda e, i=i: e.dma_start(
                out=v_fin, in_=x_mine[i * 128:(i + 1) * 128, :]),
                waits=[(sm["fgd"], fgv), (sm["outd"], sm["outd"].n)],
                inc=sm["outd"], inc_k=16)
            P.emit("sync", lambda e, i=i: e.dma_start(
                out=v_sq, in_=sgT_d[i * 128:(i + 1) * 128, :]),
                waits=[(sm["ffv"], sm["ffv"].n), (sm["ffa"], sm["ffa"].n)],
                inc=sm["outd"], inc_k=16)
            fv = sm["outd"].n
            P.emit("vector", lambda e: e.tensor_add(v_fin, v_fin, v_sq),
                   waits=[(sm["outd"], fv)], inc=sm["fgd"])
            P.emit("sync", lambda e, i=i: e.dma_start(
                out=out[i * 128:(i + 1) * 128, :], in_=v_fin),
                waits=[(sm["fgd"], sm["fgd"].n)], inc=sm["outd"], inc_k=16)
        P.wait("sync", (sm["outd"], sm["outd"].n))
        P.wait("gpsimd", (sm["outd"], sm["outd"].n))

    phase_LN(0, x_b)
    phase_A(0)
    cc_a0 = sm["cc"].n
    if UPTO >= 2:
        phase_LN(1, x_b)
        phase_A(1)
    cc_a1 = sm["cc"].n
    if UPTO >= 3:
        phase_N(0, cc_a0)
        phase_B(0)
    if UPTO >= 4:
        phase_N(1, cc_a1)
        phase_B(1)
    cc_y = sm["cc"].n
    if UPTO >= 5:
        phase_C(cc_y)
    if UPTO >= 6:
        phase_FFN()
    if UPTO >= 7:
        phase_FIN()
    else:
        allw = [(s, s.n) for s in sm.values()]
        for i in range(8):
            P.emit("sync", lambda e, i=i: e.dma_start(
                out=out[i * 128:(i + 1) * 128, :], in_=v_sq.bitcast(F32)[:, 0:H // 2]
                if False else R5.bitcast(F32)[:, 0:H]),
                waits=allw if i == 0 else [], inc=sm["outd"], inc_k=16)
        P.wait("sync", (sm["outd"], sm["outd"].n))
        P.wait("gpsimd", (sm["outd"], sm["outd"].n))

    if DEBUG:
        allw = [(s, s.n) for s in sm.values()]
        pairs = [("d_xc", xc_d), ("d_zs", zs_d), ("d_ssm0", ssm_out[0]),
                 ("d_ssm1", ssm_out[1]), ("d_yag0", yag_out[0]), ("d_yag1", yag_out[1]),
                 ("d_sd", s_d), ("d_sgT", sgT_d), ("d_hres", s_bn[:]),
                 ("d_hpre", s_cn[:]), ("d_hpo", s_hpo[:]), ("d_xpre", v_xpre),
                 ("d_ybr", v_ybr), ("d_rs2", v_rs2), ("d_lg", v_lg),
                 ("d_uu", v_uu[:, 0:32]), ("d_vv", v_vv[:, 0:32]),
                 ("d_h16", v_hres16[:, 0:128]), ("d_hpp", hpp16[:, 0:64]),
                 ("d_zt", v_zt), ("d_m1", v_m1[:, 0:32]), ("d_s1", v_s1[:, 0:32]),
                 ("d_e1", v_e1),
                 ("d_sb", R5.bitcast(F32)[:, 384:432]),
                 ("d_sp1", v_sp1[:, 0:32]), ("d_sp2", v_sp2[:, 0:32]),
                 ("d_smx", v_smx[:, 0:8]),
                 ("d_chi", v_chi), ("d_clo", v_clo),
                 ("d_bb", v_bb), ("d_cb", v_cb), ("d_u", v_u),
                 ("d_da", v_da), ("d_h", v_h), ("d_dbx", v_dbx)]
        for k, (nm, srcT) in enumerate(pairs):
            P.emit("gpsimd", lambda e, nm=nm, srcT=srcT: e.dma_start(
                out=dbg[nm], in_=srcT), waits=allw if k == 0 else [],
                inc=sm["outd"], inc_k=16)
        P.wait("gpsimd", (sm["outd"], sm["outd"].n))

    with nc.Block() as block:
        for eng in ENGS:
            def make(eng):
                def body(e):
                    for f in P.ops[eng]:
                        f(e)
                return body
            getattr(block, eng)(make(eng))

    ctx.close()
    return nc


_NC_CACHE = {}


def _stage(inputs, j):
    ip = inputs
    W = ip["L0_attn_in_proj_w"]
    Wx = W[:, j * DH:(j + 1) * DH]
    Wz = W[:, DIN + j * DH:DIN + (j + 1) * DH]

    def blk(Wp):
        return np.ascontiguousarray(Wp.reshape(KC, 128, ND, 128).transpose(2, 1, 0, 3))
    wxz = np.concatenate([blk(Wx), blk(Wz)], axis=0)
    wxpT = np.ascontiguousarray(
        ip["L0_attn_x_proj_w"].T[j * DH:(j + 1) * DH, :]
        .reshape(ND, 128, R + 2 * NST).transpose(1, 0, 2))
    wdtT = np.ascontiguousarray(
        ip["L0_attn_dt_proj_w"][j * DH:(j + 1) * DH, :].T
        .reshape(2, 128, DH).transpose(1, 0, 2))
    woT = np.ascontiguousarray(
        ip["L0_attn_out_proj_w"].T.reshape(2 * ND, 128, KC, 128).transpose(2, 1, 0, 3))

    def cols(v):
        return np.ascontiguousarray(v[j * DH:(j + 1) * DH].reshape(ND, 128).T)

    convw = np.ascontiguousarray(
        ip["L0_attn_conv_w"][j * DH:(j + 1) * DH].reshape(ND, 128, 4)
        .transpose(1, 0, 2).reshape(128, ND * 4))
    alog = np.ascontiguousarray(
        ip["L0_attn_A_log"][j * DH:(j + 1) * DH].reshape(ND, 128, NST)
        .transpose(1, 0, 2).reshape(128, ND * NST))
    wres_cat = np.concatenate([ip["L0_ffn_W_res"], ip["L0_ffn_W_pre"],
                               ip["L0_ffn_W_post"]], axis=0)
    wresT = np.ascontiguousarray(wres_cat.T.reshape(KC, 128, 24).transpose(1, 0, 2))
    wbr = np.ascontiguousarray(
        ip["L0_ffn_W_branch"].reshape(8, 128, 1024).transpose(1, 0, 2))
    oh = np.zeros((16, 16, 128), np.float16)
    for n in range(16):
        oh[n, n, :] = 1.0
    hb = np.concatenate([ip["L0_ffn_H_res_base"].ravel(),
                         ip["L0_ffn_H_pre_base"].ravel(),
                         ip["L0_ffn_H_post_base"].ravel()]).reshape(1, 24)
    return dict(
        wxz=wxz, wxpT=wxpT, wdtT=wdtT, woT=woT, convw=convw,
        convb=cols(ip["L0_attn_conv_b"]).copy(),
        dparam=cols(ip["L0_attn_D_param"]).copy(),
        dtbias=cols(ip["L0_attn_dt_bias"]).copy(),
        alog=alog,
        dtlnw=np.ascontiguousarray(ip["L0_attn_dt_ln_w"].reshape(2, 128).T),
        blnw=np.ascontiguousarray(ip["L0_attn_B_ln_w"].reshape(16, 1)),
        clnw=np.ascontiguousarray(ip["L0_attn_C_ln_w"].reshape(16, 1)),
        wresT=wresT, wbr=wbr,
        hbases=np.ascontiguousarray(hb, dtype=np.float32),
        gate=np.ascontiguousarray(ip["L0_ffn_gate"].reshape(1, 1)),
        eye4=np.ascontiguousarray(np.eye(4, dtype=np.float32).reshape(1, 16)),
        ident=np.eye(128, dtype=np.float16),
        ones128c=np.ones((128, 1), np.float32),
        ones16c=np.ones((16, 1), np.float32),
        ones1r=np.ones((1, 128), np.float32),
        oh16=np.ascontiguousarray(oh.reshape(16, 16 * 128)),
        flagj=np.array([[j, 0]], np.int32),
    )


def kernel(**inputs):
    inputs = {k: np.asarray(v) for k, v in inputs.items()}
    x = np.asarray(inputs["L0_attn_hidden_states"], dtype=np.float32)
    if "nc" not in _NC_CACHE:
        _NC_CACHE["nc"] = build_nc()
    nc = _NC_CACHE["nc"]
    staged = [_stage(inputs, j) for j in range(2)]
    in_maps = []
    for c in range(8):
        b, j = c // 2, c % 2
        m = dict(staged[j])
        m["x_b"] = np.ascontiguousarray(x[b])
        m["x_mine"] = np.ascontiguousarray(x[b, j * TH:(j + 1) * TH])
        in_maps.append(m)
    res = run_bass_kernel_spmd(nc, in_maps, core_ids=list(range(8)))
    outp = np.empty((B_SZ, T, H), np.float32)
    for c in range(8):
        b, j = c // 2, c % 2
        outp[b, j * TH:(j + 1) * TH] = np.asarray(res.results[c]["out"])
    if DEBUG:
        kernel.debug_results = res.results
    return outp


if __name__ == "__main__":
    nc = build_nc()
    print("built ok")


# version 17
# speedup vs baseline: 1.0481x; 1.0354x over previous
import numpy as np
from contextlib import ExitStack

import concourse.bass as bass
import concourse.mybir as mybir
from concourse.bass_utils import run_bass_kernel_spmd

F32 = mybir.dt.float32
F16 = mybir.dt.float16
I32 = mybir.dt.int32
AF = mybir.ActivationFunctionType
ALU = mybir.AluOpType
AX = mybir.AxisListType

B_SZ, T, TH, H = 4, 2048, 1024, 4096
DIN, DH, ND, KC = 8192, 4096, 32, 32
NST, R = 16, 256
EPS_ATTN, EPS_LN, RMS_EPS = 1e-6, 1e-5, 1.1920929e-07
SINK_ITERS, SINK_TAU = 50, 0.1
TS = 512
TSP = TS + 1

ENGS = ("sync", "vector", "scalar", "tensor", "gpsimd")
UPTO = 99
FFN_STAGE = 99
DEBUG = False


class Sem:
    def __init__(self, nc, name):
        self.h = nc.ctx.enter_context(nc.semaphore(name))
        self.n = 0


class Prog:
    def __init__(self):
        self.ops = {e: [] for e in ENGS}

    def emit(self, eng, fn, waits=(), inc=None, inc_k=1):
        w = tuple((s.h, v) for (s, v) in waits)

        def run(e, fn=fn, w=w, inc=inc, inc_k=inc_k):
            for sh, v in w:
                e.wait_ge(sh, v)
            ins = fn(e)
            if inc is not None:
                ins.then_inc(inc.h, inc_k)
        self.ops[eng].append(run)
        if inc is not None:
            inc.n += inc_k

    def wait(self, eng, *sems_vals):
        w = tuple((s.h, v) for (s, v) in sems_vals)

        def run(e, w=w):
            for sh, v in w:
                e.wait_ge(sh, v)
        self.ops[eng].append(run)



def _cp(e, out, in_):
    if hasattr(e, "tensor_copy"):
        return e.tensor_copy(out, in_)
    return e.copy(out, in_)

def build_nc():
    nc = bass.Bass("TRN2", target_bir_lowering=False)
    ctx = ExitStack()
    E = ctx.enter_context

    def din(name, shape, dt=F32):
        return nc.dram_tensor(name, list(shape), dt, kind="ExternalInput").ap()

    x_b = din("x_b", [T, H])
    x_mine = din("x_mine", [TH, H])
    wxz = din("wxz", [2 * ND, 128, KC, 128])
    wxpT = din("wxpT", [128, ND, R + 2 * NST])
    wdtT = din("wdtT", [128, 2, DH])
    woT = din("woT", [KC, 128, 2 * ND, 128])
    convw = din("convw", [128, ND * 4])
    convb = din("convb", [128, ND])
    dparam = din("dparam", [128, ND])
    dtbias = din("dtbias", [128, ND])
    alog = din("alog", [128, ND * NST])
    dtlnw = din("dtlnw", [128, 2])
    blnw = din("blnw", [16, 1])
    clnw = din("clnw", [16, 1])
    wresT = din("wresT", [128, KC, 24])
    wbr = din("wbr", [128, 8, 1024])
    hbases = din("hbases", [1, 24])
    gate = din("gate", [1, 1])
    eye4 = din("eye4", [1, 16])
    ident = din("ident", [128, 128], F16)
    ones128c = din("ones128c", [128, 1])
    ones16c = din("ones16c", [16, 1])
    ones1r = din("ones1r", [1, 128])
    oh16 = din("oh16", [16, 16 * 128], F16)
    flagj = din("flagj", [1, 2], I32)

    out = nc.dram_tensor("out", [TH, H], F32, kind="ExternalOutput").ap()
    dbg = {}
    if DEBUG:
        for nm, shape, dt in [("d_xc", [DH, T], F16), ("d_zs", [DH, T], F16),
                              ("d_ssm0", [R + 2 * NST, TH], F32),
                              ("d_ssm1", [R + 2 * NST, TH], F32),
                              ("d_yag0", [DIN, TH], F16), ("d_yag1", [DIN, TH], F16),
                              ("d_sd", [H, TH], F16), ("d_sgT", [TH, H], F16),
                              ("d_hres", [16, TH], F16), ("d_hpre", [16, TH], F16),
                              ("d_hpo", [4, TH], F16), ("d_xpre", [128, 8 * TH], F16),
                              ("d_ybr", [128, 8 * TH], F16), ("d_rs2", [128, 8], F32),
                              ("d_lg", [128, 192], F32), ("d_uu", [128, 32], F32),
                              ("d_vv", [128, 32], F32), ("d_h16", [128, 128], F16),
                              ("d_hpp", [128, 64], F16), ("d_zt", [128, 128], F32),
                              ("d_m1", [128, 32], F32), ("d_s1", [128, 32], F32),
                              ("d_e1", [128, 128], F32), ("d_sb", [128, 48], F32),
                              ("d_sp1", [128, 32], F32), ("d_sp2", [128, 32], F32),
                              ("d_smx", [128, 8], F32),
                              ("d_chi", [128, 512], F16), ("d_clo", [128, 512], F16),
                              ("d_chi0", [128, 512], F16), ("d_clo0", [128, 512], F16),
                              ("d_chi1", [128, 512], F16), ("d_clo1", [128, 512], F16),
                              ("d_da0", [128, 8 * 514], F32), ("d_dbx0", [128, 8 * 514], F16),
                              ("d_h0", [128, 8 * 514], F32),
                              ("d_bb", [128, 8192], F16), ("d_cb", [128, 8192], F16),
                              ("d_u", [128, 512], F16), ("d_da", [128, 8 * 514], F32),
                              ("d_h", [128, 8 * 514], F32), ("d_dbx", [128, 8 * 514], F16)]:
            dbg[nm] = nc.dram_tensor(nm, shape, dt, kind="ExternalOutput").ap()

    xc_d = nc.dram_tensor("xc_d", [DH, T], F16).ap()
    zs_d = nc.dram_tensor("zs_d", [DH, T], F16).ap()
    ssm_in = [nc.dram_tensor(f"ssm_in{t}", [R + 2 * NST, TH], F32).ap() for t in range(2)]
    ssm_out = [nc.dram_tensor(f"ssm_out{t}", [R + 2 * NST, TH], F32).ap() for t in range(2)]
    yag_in = [nc.dram_tensor(f"yag_in{t}", [DH, TH], F16).ap() for t in range(2)]
    yag_out = [nc.dram_tensor(f"yag_out{t}", [DIN, TH], F16).ap() for t in range(2)]
    s_d = nc.dram_tensor("s_d", [H, TH], F16).ap()
    sgT_d = nc.dram_tensor("sgT_d", [TH, H], F16).ap()

    RG = [[0, 1], [2, 3], [4, 5], [6, 7]]

    def sb(name, shape, dt):
        return E(nc.sbuf_tensor(name, list(shape), dt))

    R0 = sb("R0", [128, 33024], F16)
    v_xn = R0[:, 0:KC * TH]
    NH, LEN = 8, TS + 2
    v_da = R0.bitcast(F32)[:, 0:NH * LEN]
    v_dbx = R0[:, 8224:8224 + NH * LEN]
    v_h = R0.bitcast(F32)[:, 6168:6168 + NH * LEN]
    v_p = R0[:, 20560:20560 + NH * LEN]
    v_yres = R0[:, 0:2 * ND * 512]
    v_xnm = R0[:, 0:KC * TH]

    R2 = sb("R2", [128, 24576], F16)
    v_wxp = R2[:, 0:ND * (R + 2 * NST)]
    v_ssm = R2.bitcast(F32)[:, 4800:4800 + 3 * TH]
    v_xt = R2.bitcast(F32)[:, 7872:7872 + 4096]
    v_fin = v_xt
    v_bb = R2[:, 0:NST * TS]
    v_cb = R2[:, NST * TS:2 * NST * TS]
    v_wdt = R2[:, 2 * NST * TS:2 * NST * TS + 2 * DH]
    v_wo = [R2[:, i * (2 * ND * 128):(i + 1) * (2 * ND * 128)] for i in range(2)]
    v_xpre = R2[:, 0:8 * TH]
    v_ybr = R2[:, 8 * TH:16 * TH]
    v_wbr = R2[:, 16 * TH:16 * TH + 8 * 1024]

    R3 = sb("R3", [128, 8192], F16)
    v_w16 = [R3[:, i * (KC * 128):(i + 1) * (KC * 128)] for i in range(2)]
    v_hb = R3[:, 0:5 * TH]
    v_sev = R3[:, 5 * TH:5 * TH + 512]
    v_sgc = R3[:, 6 * TH:7 * TH]


    R5 = sb("R5", [128, 8192], F16)
    v_sq = R5[:, 0:H]
    v_dt2 = R5.bitcast(F32)[:, 0:2 * TH]
    v_nsq = R5.bitcast(F32)[:, 2 * TH:4 * TH]
    v_zt = R5.bitcast(F32)[:, 0:128]
    v_e1 = R5.bitcast(F32)[:, 128:256]
    hpp16 = R5[:, 256:320]
    v_uu = R5.bitcast(F32)[:, 256:288]
    v_vv = R5.bitcast(F32)[:, 288:320]
    v_m1 = R5.bitcast(F32)[:, 320:352]
    v_s1 = R5.bitcast(F32)[:, 352:384]
    v_eyeb = R5.bitcast(F32)[:, 384:400]
    v_baseb = R5.bitcast(F32)[:, 400:424]
    v_gcol = R5.bitcast(F32)[:, 424:425]
    v_omg = R5.bitcast(F32)[:, 425:426]
    v_mx = R5.bitcast(F32)[:, 432:440]
    v_lg = R5.bitcast(F32)[:, 512:704]
    v_hres16 = R5[:, 1704:1832]
    v_lgst16 = R5[:, 4096:4096 + TH]


    R7 = sb("R7", [128, 11992], F16)
    v_dtv = R7[:, 0:TS]
    v_u = R7[:, TS:2 * TS]
    v_sld = R7[:, 10936:10936 + TH]
    v_xcr = [R7[:, 1024 + i * TS:1024 + (i + 1) * TS] for i in range(2)]
    v_tev = R7[:, 1024:1024 + 512]
    v_zsr = [R7[:, 2048 + i * TS:2048 + (i + 1) * TS] for i in range(2)]
    v_ys = R7.bitcast(F32)[:, 1536:1536 + TS]
    v_yf = R7[:, 4096:4096 + TS]
    v_yf2 = R7[:, 4608:4608 + TH]
    v_chi = R7[:, 5632:5632 + ND * NST]
    v_clo = R7[:, 6144:6144 + ND * NST]
    v_spt = R7.bitcast(F32)[:, 3328:3328 + 512]
    v_spt2 = R7.bitcast(F32)[:, 3840:3840 + 512]
    v_xraw = R7[:, 8704:8704 + TH + 8]
    v_xcw = R7[:, 9736:9736 + TH]
    v_ctail = R7[:, 10760:10760 + ND * 4]
    v_stat = R7.bitcast(F32)[:, 5444:5460]
    v_rs2 = R7.bitcast(F32)[:, 5460:5468]
    v_lo32 = R7.bitcast(F32)[:, 5980:5996]
    v_sp1 = R7.bitcast(F32)[:, 3072:3104]
    v_sp2 = R7.bitcast(F32)[:, 3104:3136]
    v_smx = R7.bitcast(F32)[:, 3136:3144]

    R8 = sb("R8", [128, 8192], F16)
    v_braw = R8.bitcast(F32)[:, 0:TH]
    v_craw = R8.bitcast(F32)[:, TH:2 * TH]
    v_r1 = R8.bitcast(F32)[:, 2 * TH:3 * TH]
    v_rb = R8[:, 6 * TH:7 * TH]
    v_rb16 = R8[:, 7 * TH:8 * TH]

    _dtrn = sb("s_dtrn", [128, 2 * TH], F16)
    s_dtrn = [_dtrn, _dtrn]
    s_bn = sb("s_bn", [16, TH], F16)
    s_cn = sb("s_cn", [16, TH], F16)
    s_hpo = sb("s_hpo", [4, TH], F16)
    s_convw = sb("s_convw", [128, ND * 4], F32)
    s_convb = sb("s_convb", [128, ND], F32)
    s_dpar = sb("s_dpar", [128, ND], F32)
    s_dtb = sb("s_dtb", [128, ND], F32)
    s_aneg = sb("s_aneg", [128, ND * NST], F32)
    s_dtln = sb("s_dtln", [128, 2], F32)
    s_bln = sb("s_bln", [16, 1], F32)
    s_cln = sb("s_cln", [16, 1], F32)
    s_wres = sb("s_wres", [128, KC * 24], F16)
    s_id = sb("s_id", [128, 128], F16)
    s_o128c = sb("s_o128c", [128, 1], F32)
    s_o16c = sb("s_o16c", [16, 1], F32)
    s_o1r = sb("s_o1r", [1, 128], F32)
    s_oh16 = sb("s_oh16", [16, 16 * 128], F16)
    s_small = sb("s_small", [1, 64], F32)
    s_flag = sb("s_flag", [1, 2], I32)

    p_x = E(nc.psum_tensor("p_x", [128, 512], F32))
    p_z = E(nc.psum_tensor("p_z", [128, 512], F32))
    p_ssm = E(nc.psum_tensor("p_ssm", [128, 3 * 512], F32))
    p_tp = E(nc.psum_tensor("p_tp", [128, 1024], F16))
    p_dt = E(nc.psum_tensor("p_dt", [128, 512], F32))
    p_bc = E(nc.psum_tensor("p_bc", [128, 512], F32))
    p_o = [p_x, p_z]

    sm = {k: Sem(nc, k) for k in [
        "ldc", "setup", "xld", "lnv", "lna", "tpd", "xnf",
        "wld", "pxz", "evx", "cvd", "sil", "xcd", "psd", "ssa", "ssmd", "cc",
        "nrm", "bcb", "dtm", "spd", "dad", "dbd", "scn", "yrd", "xcr",
        "wod", "yrl", "omm", "oev",
        "ffw", "ffv", "ffa", "ffp", "fgd", "outd"]}

    P = Prog()

    def dma(dst, src):
        return lambda e: e.dma_start(out=dst, in_=src)

    ld = sm["ldc"]
    for dst, src in [
        (s_convw[:], convw), (s_convb[:], convb), (s_dpar[:], dparam),
        (s_dtb[:], dtbias), (s_aneg[:], alog), (s_dtln[:], dtlnw),
        (s_bln[:], blnw), (s_cln[:], clnw), (s_id[:], ident),
        (s_o128c[:], ones128c), (s_o16c[:], ones16c), (s_o1r[:], ones1r),
        (s_oh16[:], oh16), (s_flag[:], flagj),
        (s_small[0:1, 0:24], hbases), (s_small[0:1, 24:25], gate),
        (s_small[0:1, 32:48], eye4),
    ]:
        P.emit("sync", dma(dst, src), inc=ld, inc_k=16)
    P.emit("gpsimd", dma(v_wxp.rearrange("p (n e) -> p n e", n=ND), wxpT), inc=ld, inc_k=16)
    P.emit("gpsimd", dma(s_wres[:].rearrange("p (n e) -> p n e", n=KC), wresT), inc=ld, inc_k=16)
    NCONST = ld.n

    P.wait("scalar", (ld, NCONST))
    P.emit("scalar", lambda e: e.activation(s_aneg[:], s_aneg[:], AF.Exp), inc=sm["setup"])
    P.wait("vector", (ld, NCONST))
    P.emit("vector", lambda e: e.tensor_scalar_mul(s_aneg[:], s_aneg[:], -1.0),
           waits=[(sm["setup"], 1)], inc=sm["setup"])
    P.emit("vector", lambda e: e.memset(v_chi[:], 0.0), inc=sm["setup"])
    P.emit("vector", lambda e: e.memset(v_clo[:], 0.0), inc=sm["setup"])
    P.emit("vector", lambda e: e.memset(v_ctail[:], 0.0), inc=sm["setup"])
    SETUP_N = sm["setup"].n
    P.wait("tensor", (ld, NCONST))
    P.wait("gpsimd", (ld, NCONST))
    P.wait("sync", (ld, NCONST))

    def phase_LN(th, src_dram, ffn=False):
        if ffn:
            sv, sa, sp = sm["ffv"], sm["ffa"], sm["ffp"]
        else:
            sv, sa, sp = sm["lnv"], sm["lna"], sm["tpd"]
        r0_guard = (sm["pxz"], sm["pxz"].n) if not ffn else (sm["omm"], sm["omm"].n)
        for i in range(8):
            if ffn:
                srcap = src_dram[i * 128:(i + 1) * 128, :]
            else:
                srcap = src_dram[th * TH + i * 128:th * TH + (i + 1) * 128, :]
            w = [(sm["xnf"], sm["xnf"].n), (sm["fgd"], sm["fgd"].n)] if (i > 0 or th > 0 or ffn) else []
            P.emit("sync", dma(v_xt, srcap), waits=w, inc=sm["xld"], inc_k=16)
            xv = sm["xld"].n
            P.emit("vector", lambda e: e.tensor_scalar(
                v_sq, v_xt, 1.0, 0.0, ALU.mult, ALU.add, accum_out=v_stat[:, 0:1]),
                waits=[(sm["xld"], xv), (sa, sa.n)], inc=sv)
            P.emit("scalar", lambda e: e.activation(
                v_sq, v_xt, AF.Square, accum_out=v_stat[:, 1:2]),
                waits=[(sm["xld"], xv), (sv, sv.n)], inc=sa)

            def stats(e):
                e.tensor_scalar_mul(v_stat[:, 0:1], v_stat[:, 0:1], 1.0 / H)
                e.tensor_scalar_mul(v_stat[:, 1:2], v_stat[:, 1:2], 1.0 / H)
                e.tensor_tensor(v_stat[:, 8:9], v_stat[:, 0:1], v_stat[:, 0:1], ALU.mult)
                e.tensor_sub(v_stat[:, 1:2], v_stat[:, 1:2], v_stat[:, 8:9])
                return e.tensor_scalar(v_stat[:, 9:10], v_stat[:, 1:2], 1.0, EPS_LN,
                                       ALU.mult, ALU.add)
            P.emit("vector", stats, waits=[(sv, sv.n), (sa, sa.n)], inc=sv)
            P.emit("scalar", lambda e: e.activation(v_stat[:, 10:11], v_stat[:, 9:10], AF.Sqrt),
                   waits=[(sv, sv.n)], inc=sa)

            def fin_a(e, ffn=ffn):
                ins = e.reciprocal(v_stat[:, 11:12], v_stat[:, 10:11])
                if ffn:
                    e.tensor_tensor(v_stat[:, 12:13], v_stat[:, 1:2], v_stat[:, 11:12], ALU.mult)
                    e.tensor_tensor(v_stat[:, 12:13], v_stat[:, 12:13], v_stat[:, 11:12], ALU.mult)
                    ins = e.tensor_scalar(v_stat[:, 12:13], v_stat[:, 12:13], 1.0, RMS_EPS,
                                          ALU.mult, ALU.add)
                return ins
            P.emit("vector", fin_a, waits=[(sa, sa.n)], inc=sv)
            P.emit("vector", lambda e: e.tensor_scalar(
                v_sq, v_xt, v_stat[:, 0:1], v_stat[:, 11:12], ALU.subtract, ALU.mult),
                waits=[(sv, sv.n)], inc=sv)
            if ffn:
                P.emit("scalar", lambda e: e.activation(v_stat[:, 13:14], v_stat[:, 12:13],
                                                        AF.Sqrt), waits=[(sv, sv.n)], inc=sa)
                P.emit("vector", lambda e, i=i: e.reciprocal(v_rs2[:, i:i + 1], v_stat[:, 13:14]),
                       waits=[(sa, sa.n)], inc=sv)
            lv = sv.n
            for g in range(8):
                def tp(e, g=g):
                    for jj in range(4):
                        kk = g * 4 + jj
                        ins = e.transpose(p_tp[:, jj * 128:(jj + 1) * 128],
                                          v_sq[:, kk * 128:(kk + 1) * 128], s_id[:])
                    return ins
                P.emit("tensor", tp, waits=[(sv, lv), (sm["xnf"], sm["xnf"].n)], inc=sp)
                dst = (v_xnm if ffn else v_xn).rearrange("p (k t) -> p k t", k=KC)[
                    :, g * 4:(g + 1) * 4, i * 128:(i + 1) * 128]
                P.emit("scalar", lambda e, dst=dst: _cp(e, 
                    dst, p_tp[:, 0:512].rearrange("p (k t) -> p k t", k=4)),
                    waits=[(sp, sp.n), r0_guard], inc=sm["xnf"])

    pxz_after_d = {}

    def phase_A(th):
        P.emit("vector", lambda e: e.memset(v_ssm, 0.0),
               waits=[(sm["ssmd"], sm["ssmd"].n)], inc=sm["ssa"])
        ssm_clear = sm["ssa"].n
        xn_ready = sm["xnf"].n
        for d in range(ND):
            wb = v_w16[d % 2]
            key = (th, d - 2) if d >= 2 else ((th - 1, ND - 2 + d) if th > 0 else None)
            wfree = [(sm["pxz"], pxz_after_d[key])] if key in pxz_after_d else []
            P.emit("gpsimd", dma(wb.rearrange("p (k c) -> p k c", k=KC), wxz[d]),
                   waits=wfree + [(sm["xnf"], xn_ready)], inc=sm["wld"], inc_k=16)
            wv = sm["wld"].n
            for ts in range(2):
                def mmx(e, ts=ts, wb=wb):
                    for k in range(KC):
                        ins = e.matmul(p_x[:], wb[:, k * 128:(k + 1) * 128],
                                       v_xn[:, k * TH + ts * 512:k * TH + (ts + 1) * 512],
                                       start=(k == 0), stop=(k == KC - 1))
                    return ins
                P.emit("tensor", mmx, waits=[(sm["wld"], wv), (sm["evx"], sm["evx"].n)],
                       inc=sm["pxz"])
                P.emit("scalar", lambda e, ts=ts: _cp(e, 
                    v_xraw[:, 3 + ts * 512:3 + (ts + 1) * 512], p_x[:]),
                    waits=[(sm["pxz"], sm["pxz"].n), (sm["xcd"], sm["xcd"].n),
                           (sm["cvd"], sm["cvd"].n)], inc=sm["evx"])
            evv = sm["evx"].n
            if th == 0:
                P.emit("vector", lambda e: e.memset(v_xraw[:, 0:3], 0.0),
                       waits=[(sm["evx"], evv), (sm["setup"], SETUP_N)], inc=sm["cvd"])
            else:
                P.emit("vector", lambda e, d=d: _cp(e, 
                    v_xraw[:, 0:3], v_ctail[:, d * 4:d * 4 + 3]),
                    waits=[(sm["evx"], evv)], inc=sm["cvd"])
            P.emit("vector", lambda e, d=d: _cp(e, 
                v_ctail[:, d * 4:d * 4 + 3], v_xraw[:, TH:TH + 3]), inc=sm["cvd"])

            def conv(e, d=d):
                e.tensor_scalar(v_xcw, v_xraw[:, 3:3 + TH],
                                s_convw[:, d * 4 + 3:d * 4 + 4],
                                s_convb[:, d:d + 1], ALU.mult, ALU.add)
                for k in range(3):
                    ins = e.scalar_tensor_tensor(
                        v_xcw, v_xraw[:, k:k + TH], s_convw[:, d * 4 + k:d * 4 + k + 1],
                        v_xcw, ALU.mult, ALU.add)
                return ins
            P.emit("vector", conv, waits=[(sm["xcd"], sm["xcd"].n),
                                          (sm["cvd"], sm["cvd"].n)], inc=sm["cvd"])
            cvv = sm["cvd"].n
            P.emit("scalar", lambda e: e.activation(v_xcw, v_xcw, AF.Silu),
                   waits=[(sm["cvd"], cvv)], inc=sm["sil"])
            silv = sm["sil"].n
            P.emit("sync", lambda e, d=d, th=th: e.dma_start(
                out=xc_d[d * 128:(d + 1) * 128, th * TH:(th + 1) * TH], in_=v_xcw),
                waits=[(sm["sil"], silv)], inc=sm["xcd"], inc_k=16)
            for ts in range(2):
                def mmssm(e, d=d, ts=ts):
                    sl = v_xcw[:, ts * 512:(ts + 1) * 512]
                    e.matmul(p_ssm[:, 0:512], v_wxp[:, d * 288:d * 288 + 128], sl,
                             start=True, stop=True)
                    e.matmul(p_ssm[:, 512:1024], v_wxp[:, d * 288 + 128:d * 288 + 256], sl,
                             start=True, stop=True)
                    return e.matmul(p_ssm[0:32, 1024:1536],
                                    v_wxp[:, d * 288 + 256:d * 288 + 288],
                                    sl, start=True, stop=True)
                P.emit("tensor", mmssm, waits=[(sm["sil"], silv), (sm["ssa"], sm["ssa"].n)],
                       inc=sm["psd"])

                def ssmacc(e, ts=ts):
                    dst = v_ssm.rearrange("p (g t) -> p g t", g=3)[:, :, ts * 512:(ts + 1) * 512]
                    src = p_ssm[:].rearrange("p (g t) -> p g t", g=3)
                    return e.scalar_tensor_tensor(dst, src, 1.0, dst, ALU.mult, ALU.add)
                P.emit("vector", ssmacc, waits=[(sm["psd"], sm["psd"].n),
                                                (sm["ssa"], ssm_clear)], inc=sm["ssa"])
            P.emit("gpsimd", dma(wb.rearrange("p (k c) -> p k c", k=KC), wxz[ND + d]),
                   waits=[(sm["pxz"], sm["pxz"].n)], inc=sm["wld"], inc_k=16)
            wv2 = sm["wld"].n
            for ts in range(2):
                def mmz(e, ts=ts, wb=wb):
                    for k in range(KC):
                        ins = e.matmul(p_z[:], wb[:, k * 128:(k + 1) * 128],
                                       v_xn[:, k * TH + ts * 512:k * TH + (ts + 1) * 512],
                                       start=(k == 0), stop=(k == KC - 1))
                    return ins
                P.emit("tensor", mmz, waits=[(sm["wld"], wv2), (sm["sil"], sm["sil"].n)],
                       inc=sm["pxz"])
                P.emit("scalar", lambda e, ts=ts: e.activation(
                    v_xraw[:, 3 + ts * 512:3 + (ts + 1) * 512], p_z[:], AF.Silu),
                    waits=[(sm["pxz"], sm["pxz"].n), (sm["cvd"], cvv),
                           (sm["xcd"], sm["xcd"].n)], inc=sm["sil"])
            pxz_after_d[(th, d)] = sm["pxz"].n
            P.emit("sync", lambda e, d=d, th=th: e.dma_start(
                out=zs_d[d * 128:(d + 1) * 128, th * TH:(th + 1) * TH],
                in_=v_xraw[:, 3:3 + TH]),
                waits=[(sm["sil"], sm["sil"].n)], inc=sm["xcd"], inc_k=16)
        P.wait("sync", (sm["ssa"], sm["ssa"].n))
        P.emit("sync", lambda e, th=th: e.dma_start(out=ssm_in[th][0:128, :], in_=v_ssm[:, 0:TH]),
               inc=sm["ssmd"], inc_k=16)
        P.emit("sync", lambda e, th=th: e.dma_start(out=ssm_in[th][128:256, :],
                                                    in_=v_ssm[:, TH:2 * TH]),
               inc=sm["ssmd"], inc_k=16)
        P.emit("sync", lambda e, th=th: e.dma_start(out=ssm_in[th][256:288, :],
                                                    in_=v_ssm[0:32, 2 * TH:3 * TH]),
               inc=sm["ssmd"], inc_k=16)
        P.emit("gpsimd", lambda e, th=th: e.collective_compute(
            "AllReduce", ALU.add, replica_groups=RG,
            ins=[ssm_in[th].opt()], outs=[ssm_out[th].opt()]),
            waits=[(sm["ssmd"], sm["ssmd"].n)], inc=sm["cc"])

    def phase_N(th, ccv):
        nr = sm["nrm"]
        P.emit("sync", lambda e, th=th: e.dma_start(
            out=v_dt2[:, 0:TH], in_=ssm_out[th][0:128, :]),
            waits=[(sm["cc"], ccv), (sm["xnf"], sm["xnf"].n)], inc=nr, inc_k=16)
        P.emit("sync", lambda e, th=th: e.dma_start(
            out=v_dt2[:, TH:2 * TH], in_=ssm_out[th][128:256, :]),
            waits=[(sm["cc"], ccv)], inc=nr, inc_k=16)
        P.emit("sync", lambda e, th=th: e.dma_start(out=v_braw[0:16, :],
                                                    in_=ssm_out[th][256:272, :]),
               waits=[(sm["cc"], ccv)], inc=nr, inc_k=16)
        P.emit("sync", lambda e, th=th: e.dma_start(out=v_craw[0:16, :],
                                                    in_=ssm_out[th][272:288, :]),
               waits=[(sm["cc"], ccv)], inc=nr, inc_k=16)
        base = nr.n
        P.emit("scalar", lambda e: e.activation(v_nsq[:, 0:2 * TH], v_dt2, AF.Square),
               waits=[(nr, base)], inc=nr)

        def msum(e):
            last = None
            for c in range(2):
                pp = p_dt if c == 0 else p_bc
                for k in range(2):
                    last = e.matmul(pp[0:1, :], s_o128c[:],
                                    v_nsq[:, k * TH + c * 512:k * TH + (c + 1) * 512],
                                    start=(k == 0), stop=(k == 1))
            return last
        P.emit("tensor", msum, waits=[(nr, nr.n)], inc=nr)

        def rfun(e):
            e.tensor_scalar(v_r1[0:1, 0:512], p_dt[0:1, :], 1.0 / R, EPS_ATTN, ALU.mult, ALU.add)
            return e.tensor_scalar(v_r1[0:1, 512:1024], p_bc[0:1, :], 1.0 / R, EPS_ATTN,
                                   ALU.mult, ALU.add)
        P.emit("vector", rfun, waits=[(nr, nr.n)], inc=nr)
        P.emit("scalar", lambda e: e.activation(v_r1[0:1, :], v_r1[0:1, :], AF.Sqrt),
               waits=[(nr, nr.n)], inc=nr)
        P.emit("vector", lambda e: e.reciprocal(v_r1[0:1, :], v_r1[0:1, :]),
               waits=[(nr, nr.n)], inc=nr)
        rv = nr.n
        for c in range(2):
            P.emit("tensor", lambda e, c=c: e.matmul(
                p_bc[:] if c else p_dt[:], s_o1r[:], v_r1[0:1, c * 512:(c + 1) * 512],
                start=True, stop=True), waits=[(nr, rv)], inc=nr)
            P.emit("scalar", lambda e, c=c: _cp(e, 
                v_rb[:, c * 512:(c + 1) * 512], p_bc[:] if c else p_dt[:]),
                waits=[(nr, nr.n)], inc=nr)
        bv = nr.n

        def dtrn(e, th=th):
            last = None
            for k in range(2):
                last = e.scalar_tensor_tensor(
                    s_dtrn[th][:, k * TH:(k + 1) * TH], v_dt2[:, k * TH:(k + 1) * TH],
                    s_dtln[:, k:k + 1], v_rb, ALU.mult, ALU.mult)
            return last
        P.emit("vector", dtrn, waits=[(nr, bv), (sm["dtm"], sm["dtm"].n)], inc=nr)
        for (raw, lnw, dstn) in ((v_braw, s_bln, s_bn), (v_craw, s_cln, s_cn)):
            P.emit("scalar", lambda e, raw=raw: e.activation(
                v_nsq[0:16, 0:TH], raw[0:16, :], AF.Square), waits=[(nr, nr.n)], inc=nr)

            def bsum(e):
                e.matmul(p_dt[0:1, :], s_o16c[:], v_nsq[0:16, 0:512], start=True, stop=True)
                return e.matmul(p_bc[0:1, :], s_o16c[:], v_nsq[0:16, 512:1024],
                                start=True, stop=True)
            P.emit("tensor", bsum, waits=[(nr, nr.n)], inc=nr)

            def brs(e):
                e.tensor_scalar(v_r1[0:1, 0:512], p_dt[0:1, :], 1.0 / NST, EPS_ATTN,
                                ALU.mult, ALU.add)
                return e.tensor_scalar(v_r1[0:1, 512:1024], p_bc[0:1, :], 1.0 / NST, EPS_ATTN,
                                       ALU.mult, ALU.add)
            P.emit("vector", brs, waits=[(nr, nr.n)], inc=nr)
            P.emit("scalar", lambda e: e.activation(v_r1[0:1, :], v_r1[0:1, :], AF.Sqrt),
                   waits=[(nr, nr.n)], inc=nr)
            P.emit("vector", lambda e: e.reciprocal(v_r1[0:1, :], v_r1[0:1, :]),
                   waits=[(nr, nr.n)], inc=nr)

            def bc16(e):
                e.matmul(p_dt[0:16, :], s_o1r[0:1, 0:16], v_r1[0:1, 0:512],
                         start=True, stop=True)
                return e.matmul(p_bc[0:16, :], s_o1r[0:1, 0:16], v_r1[0:1, 512:1024],
                                start=True, stop=True)
            P.emit("tensor", bc16, waits=[(nr, nr.n)], inc=nr)

            def bevac(e):
                _cp(e, v_rb16[0:16, 0:512], p_dt[0:16, :])
                return _cp(e, v_rb16[0:16, 512:1024], p_bc[0:16, :])
            P.emit("scalar", bevac, waits=[(nr, nr.n)], inc=nr)
            P.emit("vector", lambda e, raw=raw, lnw=lnw, dstn=dstn: e.scalar_tensor_tensor(
                dstn[:], raw[0:16, :], lnw[:], v_rb16[0:16, :], ALU.mult, ALU.mult),
                waits=[(nr, nr.n)], inc=nr)

    xcr_hist = {}
    xcr_marks = []
    cs_pend = [None]
    cs_slot = 0

    def phase_B(th):
        nonlocal cs_slot
        if th == 0:
            P.emit("gpsimd", dma(v_wdt.rearrange("p (n e) -> p n e", n=2), wdtT),
                   waits=[(sm["pxz"], sm["pxz"].n)], inc=sm["wld"], inc_k=16)
            P.emit("vector", lambda e: e.memset(
                v_da.rearrange("p (n t) -> p n t", n=NH)[:, :, 0:1], 0.0),
                waits=[(sm["pxz"], sm["pxz"].n), (sm["xnf"], sm["xnf"].n)], inc=sm["dbd"])
            P.emit("vector", lambda e: e.memset(
                v_da.rearrange("p (n t) -> p n t", n=NH)[:, :, 1:2], 1.0), inc=sm["dbd"])
        sep_done = sm["dbd"].n
        wdt_v = sm["wld"].n
        for ts in range(2):
            t0 = ts * TS
            for (src, dstb) in ((s_bn, v_bb), (s_cn, v_cb)):
                for n in range(NST):
                    def bcast(e, n=n, src=src, t0=t0):
                        return e.matmul(p_bc[:], s_oh16[:, n * 128:(n + 1) * 128],
                                        src[:, t0:t0 + TS], start=True, stop=True)
                    P.emit("tensor", bcast, waits=[(sm["nrm"], sm["nrm"].n),
                                                   (sm["bcb"], sm["bcb"].n)], inc=sm["bcb"])
                    P.emit("scalar", lambda e, n=n, dstb=dstb: _cp(e, 
                        dstb[:, n * TS:(n + 1) * TS], p_bc[:]),
                        waits=[(sm["bcb"], sm["bcb"].n), (sm["dbd"], sm["dbd"].n),
                               (sm["scn"], sm["scn"].n)], inc=sm["bcb"])
            bcv = sm["bcb"].n
            for d in range(ND):
                wpre = [(sm["scn"], xcr_marks[-2])] if len(xcr_marks) >= 2 else []
                P.emit("sync", lambda e, d=d, th=th, t0=t0: e.dma_start(
                    out=v_xcr[d % 2],
                    in_=xc_d[d * 128:(d + 1) * 128, th * TH + t0:th * TH + t0 + TS]),
                    waits=wpre, inc=sm["xcr"], inc_k=16)
                P.emit("sync", lambda e, d=d, th=th, t0=t0: e.dma_start(
                    out=v_zsr[d % 2],
                    in_=zs_d[d * 128:(d + 1) * 128, th * TH + t0:th * TH + t0 + TS]),
                    inc=sm["xcr"], inc_k=16)
                xcrv = sm["xcr"].n

                def mmdt(e, d=d, th=th, t0=t0):
                    for k in range(2):
                        ins = e.matmul(
                            p_dt[:], v_wdt[:, k * DH + d * 128:k * DH + (d + 1) * 128],
                            s_dtrn[th][:, k * TH + t0:k * TH + t0 + TS],
                            start=(k == 0), stop=(k == 1))
                    return ins
                P.emit("tensor", mmdt, waits=[(sm["wld"], wdt_v), (sm["nrm"], sm["nrm"].n),
                                              (sm["spd"], sm["spd"].n), (sm["bcb"], bcv)],
                       inc=sm["dtm"])
                dtv_ = sm["dtm"].n

                def sp(e, d=d):
                    e.activation(v_spt[:, 0:512], p_dt[:], AF.Abs, bias=s_dtb[:, d:d + 1])
                    e.activation(v_spt[:, 0:512], v_spt[:, 0:512], AF.Exp, scale=-1.0)
                    e.activation(v_spt[:, 0:512], v_spt[:, 0:512], AF.Ln, bias=1.0)
                    return e.activation(v_spt2[:, 0:512], p_dt[:], AF.Relu,
                                        bias=s_dtb[:, d:d + 1])
                P.emit("scalar", sp, waits=[(sm["dtm"], dtv_), (sm["dad"], sm["dad"].n)],
                       inc=sm["spd"])
                P.emit("vector", lambda e: e.tensor_add(v_dtv, v_spt2[:, 0:512], v_spt[:, 0:512]),
                       waits=[(sm["spd"], sm["spd"].n)], inc=sm["spd"])
                spv = sm["spd"].n
                P.emit("vector", lambda e, d=d: e.tensor_mul(v_u, v_dtv, v_xcr[d % 2]),
                       waits=[(sm["spd"], spv), (sm["xcr"], xcrv)], inc=sm["dbd"])

                for nh in range(2):
                    def mkda(e, d=d, nh=nh):
                        for nn in range(NH):
                            n = nh * NH + nn
                            ins = e.activation(
                                v_da.rearrange("p (n t) -> p n t", n=NH)[:, nn, 2:LEN],
                                v_dtv, AF.Exp, scale=s_aneg[:, d * NST + n:d * NST + n + 1])
                        return ins
                    P.emit("scalar", mkda, waits=[(sm["spd"], spv), (sm["scn"], sm["scn"].n),
                                                  (sm["setup"], SETUP_N)], inc=sm["dad"])

                    def mkdbx(e, d=d, nh=nh):
                        dst = v_dbx.rearrange("p (n t) -> p n t", n=NH)[:, :, 2:LEN]
                        ub = v_u.unsqueeze(1).broadcast_to([128, NH, TS])
                        bb = v_bb.rearrange("p (n t) -> p n t", n=NST)[:, nh * NH:(nh + 1) * NH, :]
                        e.tensor_tensor(dst, ub, bb, ALU.mult)
                        e.tensor_copy(
                            v_dbx.rearrange("p (n t) -> p n t", n=NH)[:, :, 0],
                            v_chi[:, d * NST + nh * NH:d * NST + nh * NH + NH])
                        return e.tensor_copy(
                            v_dbx.rearrange("p (n t) -> p n t", n=NH)[:, :, 1],
                            v_clo[:, d * NST + nh * NH:d * NST + nh * NH + NH])
                    P.emit("vector", mkdbx, waits=[(sm["bcb"], bcv), (sm["scn"], sm["scn"].n),
                                                   (sm["setup"], SETUP_N),
                                                   (sm["dbd"], sm["dbd"].n)], inc=sm["dbd"])
                    P.emit("vector", lambda e: e.tensor_tensor_scan(
                        v_h, v_da, v_dbx, 0.0, ALU.mult, ALU.add),
                        waits=[(sm["dad"], sm["dad"].n), (sm["dbd"], sep_done),
                               (sm["dbd"], sm["dbd"].n)], inc=sm["scn"])

                    def csaveA(e, d=d, nh=nh, pend=cs_pend[0]):
                        sl = slice(d * NST + nh * NH, d * NST + nh * NH + NH)
                        hl = v_h.rearrange("p (n t) -> p n t", n=NH)[:, :, LEN - 1]
                        ins = e.tensor_copy(v_chi[:, sl], hl)
                        if pend is not None:
                            psl, pslot = pend
                            ins = e.tensor_copy(v_clo[:, psl], v_lo32[:, pslot * NH:(pslot + 1) * NH])
                        return ins
                    P.emit("vector", csaveA, waits=[(sm["scn"], sm["scn"].n)], inc=sm["scn"])
                    cs_slot = (cs_slot + 1) % 2

                    def csby(e, d=d, nh=nh, slot=cs_slot):
                        sl = slice(d * NST + nh * NH, d * NST + nh * NH + NH)
                        hl = v_h.rearrange("p (n t) -> p n t", n=NH)[:, :, LEN - 1]
                        e.tensor_tensor(v_lo32[:, slot * NH:(slot + 1) * NH],
                                        hl, v_chi[:, sl], ALU.subtract)
                        hv = v_h.rearrange("p (n t) -> p n t", n=NH)[:, :, 2:LEN]
                        p3 = v_p.rearrange("p (n t) -> p n t", n=NH)
                        cb = v_cb.rearrange("p (n t) -> p n t", n=NST)[:, nh * NH:(nh + 1) * NH, :]
                        e.tensor_tensor(p3[:, :, 2:LEN], hv, cb, ALU.mult)
                        e.tensor_add(p3[:, 0:4, 2:LEN], p3[:, 0:4, 2:LEN], p3[:, 4:8, 2:LEN])
                        e.tensor_add(p3[:, 0:2, 2:LEN], p3[:, 0:2, 2:LEN], p3[:, 2:4, 2:LEN])
                        e.tensor_add(p3[:, 0:1, 2:LEN], p3[:, 0:1, 2:LEN], p3[:, 1:2, 2:LEN])
                        if nh == 0:
                            return e.tensor_copy(v_ys, p3[:, 0, 2:LEN])
                        return e.tensor_add(v_ys, v_ys, p3[:, 0, 2:LEN])
                    P.emit("vector", csby, waits=[(sm["scn"], sm["scn"].n)], inc=sm["scn"])
                    cs_pend[0] = (slice(d * NST + nh * NH, d * NST + nh * NH + NH), cs_slot)

                def yfin(e, d=d):
                    e.scalar_tensor_tensor(v_ys, v_xcr[d % 2], s_dpar[:, d:d + 1],
                                           v_ys, ALU.mult, ALU.add)
                    return e.tensor_mul(v_yf, v_ys, v_zsr[d % 2])
                P.emit("vector", yfin, waits=[(sm["yrd"], sm["yrd"].n)], inc=sm["scn"])
                xcr_marks.append(sm["scn"].n)
                P.emit("sync", lambda e, d=d, th=th, t0=t0: e.dma_start(
                    out=yag_in[th][d * 128:(d + 1) * 128, t0:t0 + TS], in_=v_yf),
                    waits=[(sm["scn"], sm["scn"].n)], inc=sm["yrd"], inc_k=16)
            def csflush(e, pend=cs_pend[0]):
                psl, pslot = pend
                return e.tensor_copy(v_clo[:, psl], v_lo32[:, pslot * NH:(pslot + 1) * NH])
            P.emit("vector", csflush, waits=[(sm["scn"], sm["scn"].n)], inc=sm["scn"])
            cs_pend[0] = None
            if DEBUG and th == 0:
                mids = [(f"d_chi{ts}", v_chi), (f"d_clo{ts}", v_clo)]
                if ts == 0:
                    mids += [("d_da0", v_da), ("d_dbx0", v_dbx), ("d_h0", v_h)]
                for k, (nm, srcT) in enumerate(mids):
                    P.emit("gpsimd", lambda e, nm=nm, srcT=srcT: e.dma_start(
                        out=dbg[nm], in_=srcT),
                        waits=[(sm["scn"], sm["scn"].n)] if k == 0 else [],
                        inc=sm["outd"], inc_k=16)
                P.emit("vector", lambda e: e.tensor_copy(v_dtv, v_dtv),
                       waits=[(sm["outd"], sm["outd"].n)], inc=sm["scn"])
        P.emit("gpsimd", lambda e, th=th: e.collective_compute(
            "AllGather", ALU.bypass, replica_groups=RG,
            ins=[yag_in[th].opt()], outs=[yag_out[th].opt()]),
            waits=[(sm["yrd"], sm["yrd"].n)], inc=sm["cc"])

    omm_hist = {}

    def phase_C(ccv):
        for tb in range(2):
            def ldy(e, tb=tb):
                v = e.value_load(s_flag[0:1, 0:1])
                for half in range(2):
                    src = yag_out[half][:, tb * 512:(tb + 1) * 512].rearrange(
                        "(k p) t -> p k t", p=128)
                    e.dma_start(out=v_yres[:].rearrange("p (k t) -> p k t", k=2 * ND),
                                in_=src, cond=(v == half)).then_inc(sm["yrl"].h, 16)
            w = [(sm["cc"], ccv), (sm["scn"], sm["scn"].n)]
            if tb > 0:
                w.append((sm["omm"], omm_hist[(tb - 1, KC - 1)]))
            wt = tuple((s.h, v) for (s, v) in w)

            def ldy_run(e, ldy=ldy, wt=wt):
                for sh, v in wt:
                    e.wait_ge(sh, v)
                ldy(e)
            P.ops["sync"].append(ldy_run)
            sm["yrl"].n += 16 * 2
            ylv = sm["yrl"].n
            for hch in range(KC):
                wo = v_wo[hch % 2]
                key = (tb, hch - 2) if hch >= 2 else ((tb - 1, KC - 2 + hch) if tb > 0 else None)
                wfree = [(sm["omm"], omm_hist[key])] if key in omm_hist else [
                    (sm["scn"], sm["scn"].n)]
                P.emit("gpsimd", dma(wo.rearrange("p (k c) -> p k c", k=2 * ND), woT[hch]),
                       waits=wfree, inc=sm["wod"], inc_k=16)
                wv = sm["wod"].n

                def mmo(e, wo=wo, hch=hch):
                    pp = p_o[hch % 2]
                    for k in range(2 * ND):
                        ins = e.matmul(pp[:], wo[:, k * 128:(k + 1) * 128],
                                       v_yres[:, k * 512:(k + 1) * 512],
                                       start=(k == 0), stop=(k == 2 * ND - 1))
                    return ins
                P.emit("tensor", mmo, waits=[(sm["wod"], wv), (sm["yrl"], ylv),
                                             (sm["oev"], sm["oev"].n)], inc=sm["omm"])
                omm_hist[(tb, hch)] = sm["omm"].n
                P.emit("scalar", lambda e, hch=hch: _cp(e, v_sev, p_o[hch % 2][:]),
                       waits=[(sm["omm"], sm["omm"].n), (sm["oev"], sm["oev"].n)],
                       inc=sm["oev"])
                P.emit("sync", lambda e, hch=hch, tb=tb: e.dma_start(
                    out=s_d[hch * 128:(hch + 1) * 128, tb * 512:(tb + 1) * 512], in_=v_sev),
                    waits=[(sm["oev"], sm["oev"].n)], inc=sm["oev"], inc_k=16)

    def phase_FFN():
        sv, sa, sp = sm["ffv"], sm["ffa"], sm["ffp"]
        phase_LN(0, x_mine, ffn=True)
        xnv = sm["xnf"].n
        if FFN_STAGE < 2:
            return
        for c in range(2):
            def mml(e, c=c):
                for k in range(KC):
                    ins = e.matmul(p_dt[0:24, :], s_wres[:, k * 24:(k + 1) * 24],
                                   v_xnm[:, k * TH + c * 512:k * TH + (c + 1) * 512],
                                   start=(k == 0), stop=(k == KC - 1))
                return ins
            P.emit("tensor", mml, waits=[(sm["xnf"], xnv), (sa, sa.n),
                                         (sm["nrm"], sm["nrm"].n)], inc=sp)
            P.emit("scalar", lambda e, c=c: _cp(e, 
                v_lgst16[0:24, c * 512:(c + 1) * 512], p_dt[0:24, :]),
                waits=[(sp, sp.n)], inc=sa)
        lgv = sa.n
        for i in range(8):
            P.emit("tensor", lambda e, i=i: e.transpose(
                p_tp[:, 0:24], v_lgst16[0:24, i * 128:(i + 1) * 128], s_id[0:24, 0:24]),
                waits=[(sa, lgv), (sa, sa.n)], inc=sp)
            P.emit("scalar", lambda e, i=i: _cp(e, 
                v_lg[:, i * 24:(i + 1) * 24], p_tp[:, 0:24]),
                waits=[(sp, sp.n)], inc=sa)
        if FFN_STAGE < 3:
            return
        P.emit("scalar", lambda e: e.activation(s_small[0:1, 25:26], s_small[0:1, 24:25],
                                                AF.Sigmoid), inc=sa)

        for (dst, srcsl) in ((v_baseb[:, 0:24], s_small[0:1, 0:24]),
                             (v_eyeb[:, 0:16], s_small[0:1, 32:48]),
                             (v_gcol[:, 0:1], s_small[0:1, 25:26])):
            P.emit("tensor", lambda e, srcsl=srcsl: e.matmul(
                p_bc[0:128, 0:srcsl.shape[1]], s_o1r[:], srcsl, start=True, stop=True),
                waits=[(sa, sa.n)], inc=sp)
            P.emit("scalar", lambda e, dst=dst, srcsl=srcsl: _cp(
                e, dst, p_bc[0:128, 0:srcsl.shape[1]]),
                waits=[(sp, sp.n)], inc=sa)
        bsv = sa.n

        def mkz(e):
            lg3 = v_lg.rearrange("p (i e) -> p i e", i=8)
            for i in range(8):
                e.tensor_scalar_mul(lg3[:, i, :], lg3[:, i, :], v_rs2[:, i:i + 1])
            bb = v_baseb[:, 0:24].unsqueeze(1).broadcast_to([128, 8, 24])
            e.tensor_tensor(lg3, lg3, bb, ALU.add)
            return e.tensor_scalar_mul(
                v_zt.rearrange("p (i g) -> p i g", i=8), lg3[:, :, 0:16], 1.0 / SINK_TAU)
        P.emit("vector", mkz, waits=[(sa, bsv), (sv, sv.n)], inc=sv)
        if FFN_STAGE < 4:
            return
        P.emit("vector", lambda e: e.memset(v_uu[:, 0:32], 0.0), inc=sv)
        P.emit("vector", lambda e: e.memset(v_vv[:, 0:32], 0.0), inc=sv)
        z4 = v_zt.rearrange("p (i j k) -> p i j k", i=8, j=4)
        e4 = v_e1.rearrange("p (i j k) -> p i j k", i=8, j=4)
        e4T = v_e1.rearrange("p (i j k) -> p i k j", i=8, j=4)
        u4 = v_uu[:, 0:32].rearrange("p (i j) -> p i j", i=8)
        v4 = v_vv[:, 0:32].rearrange("p (i j) -> p i j", i=8)
        m4 = v_m1[:, 0:32].rearrange("p (i j) -> p i j", i=8)
        s4 = v_s1[:, 0:32].rearrange("p (i j) -> p i j", i=8)
        SINK_HALF = globals().get('SINK_HALF', False)
        for it in range(globals().get('SINK_IT_OVR', SINK_ITERS)):
            def ustep_a(e):
                vb = v4.unsqueeze(2).broadcast_to([128, 8, 4, 4])
                e.tensor_tensor(e4, z4, vb, ALU.add)
                return e.reduce_max(m4, e4, axis=AX.X)
            P.emit("vector", ustep_a, waits=[(sa, sa.n), (sv, sv.n)], inc=sv)

            def ustep_b(e):
                mb = m4.unsqueeze(3).broadcast_to([128, 8, 4, 4])
                return e.tensor_tensor(e4, e4, mb, ALU.subtract)
            P.emit("vector", ustep_b, waits=[(sv, sv.n)], inc=sv)
            P.emit("scalar", lambda e: e.activation(v_e1[:, 0:128], v_e1[:, 0:128], AF.Exp),
                   waits=[(sv, sv.n)], inc=sa)
            P.emit("vector", lambda e: e.reduce_sum(s4, e4, axis=AX.X),
                   waits=[(sa, sa.n)], inc=sv)
            P.emit("scalar", lambda e: e.activation(v_s1[:, 0:32], v_s1[:, 0:32], AF.Ln),
                   waits=[(sv, sv.n)], inc=sa)
            P.emit("vector", lambda e: e.scalar_tensor_tensor(
                v_uu[:, 0:32], v_m1[:, 0:32], -1.0, v_s1[:, 0:32], ALU.mult, ALU.subtract),
                waits=[(sa, sa.n)], inc=sv)

            if SINK_HALF:
                continue

            def vstep_a(e):
                ub = u4.unsqueeze(3).broadcast_to([128, 8, 4, 4])
                e.tensor_tensor(e4, z4, ub, ALU.add)
                return e.reduce_max(m4, e4T, axis=AX.X)
            P.emit("vector", vstep_a, waits=[(sv, sv.n)], inc=sv)

            def vstep_b(e):
                mb = m4.unsqueeze(3).broadcast_to([128, 8, 4, 4])
                return e.tensor_tensor(e4T, e4T, mb, ALU.subtract)
            P.emit("vector", vstep_b, waits=[(sv, sv.n)], inc=sv)
            P.emit("scalar", lambda e: e.activation(v_e1[:, 0:128], v_e1[:, 0:128], AF.Exp),
                   waits=[(sv, sv.n)], inc=sa)
            P.emit("vector", lambda e: e.reduce_sum(s4, e4T, axis=AX.X),
                   waits=[(sa, sa.n)], inc=sv)
            P.emit("scalar", lambda e: e.activation(v_s1[:, 0:32], v_s1[:, 0:32], AF.Ln),
                   waits=[(sv, sv.n)], inc=sa)
            P.emit("vector", lambda e: e.scalar_tensor_tensor(
                v_vv[:, 0:32], v_m1[:, 0:32], -1.0, v_s1[:, 0:32], ALU.mult, ALU.subtract),
                waits=[(sa, sa.n)], inc=sv)

        if FFN_STAGE < 5:
            return

        def hfin(e):
            ub = u4.unsqueeze(3).broadcast_to([128, 8, 4, 4])
            vb = v4.unsqueeze(2).broadcast_to([128, 8, 4, 4])
            e.tensor_tensor(e4, z4, ub, ALU.add)
            return e.tensor_tensor(e4, e4, vb, ALU.add)
        P.emit("vector", hfin, waits=[(sv, sv.n)], inc=sv)
        P.emit("scalar", lambda e: e.activation(v_e1[:, 0:128], v_e1[:, 0:128], AF.Exp),
               waits=[(sv, sv.n)], inc=sa)

        def blend1(e):
            e.tensor_scalar_mul(v_e1[:, 0:128], v_e1[:, 0:128], v_gcol[:, 0:1])
            return e.tensor_scalar(v_omg[:, 0:1], v_gcol[:, 0:1], -1.0, 1.0, ALU.mult, ALU.add)
        P.emit("vector", blend1, waits=[(sa, sa.n)], inc=sv)

        def blend2(e):
            return e.scalar_tensor_tensor(
                v_hres16[:, 0:128].rearrange("p (i e) -> p i e", i=8),
                v_eyeb[:, 0:16].unsqueeze(1).broadcast_to([128, 8, 16]),
                v_omg[:, 0:1],
                v_e1[:, 0:128].rearrange("p (i e) -> p i e", i=8), ALU.mult, ALU.add)
        P.emit("vector", blend2, waits=[(sv, sv.n)], inc=sv)

        lg3s = v_lg.rearrange("p (i e) -> p i e", i=8)
        for (c0, off) in ((16, 0), (20, 32)):
            P.emit("vector", lambda e, c0=c0: e.tensor_copy(
                v_sp1[:, 0:32].rearrange("p (i j) -> p i j", i=8),
                lg3s[:, :, c0:c0 + 4]),
                waits=[(sv, sv.n), (sa, sa.n)], inc=sv)
            P.emit("vector", lambda e: e.reduce_max(
                v_smx[:, 0:8], v_sp1[:, 0:32].rearrange("p (i j) -> p i j", i=8),
                axis=AX.X), waits=[(sv, sv.n)], inc=sv)
            P.emit("vector", lambda e: e.tensor_tensor(
                v_sp1[:, 0:32].rearrange("p (i j) -> p i j", i=8),
                v_sp1[:, 0:32].rearrange("p (i j) -> p i j", i=8),
                v_smx[:, 0:8].unsqueeze(2).broadcast_to([128, 8, 4]),
                ALU.subtract), waits=[(sv, sv.n)], inc=sv)
            P.emit("scalar", lambda e: e.activation(v_sp2[:, 0:32], v_sp1[:, 0:32], AF.Exp),
                   waits=[(sv, sv.n)], inc=sa)

            P.emit("vector", lambda e: e.reduce_sum(
                v_smx[:, 0:8], v_sp2[:, 0:32].rearrange("p (i j) -> p i j", i=8),
                axis=AX.X), waits=[(sa, sa.n)], inc=sv)
            P.emit("vector", lambda e: e.reciprocal(v_smx[:, 0:8], v_smx[:, 0:8]),
                   waits=[(sv, sv.n)], inc=sv)
            P.emit("vector", lambda e, off=off: e.tensor_tensor(
                hpp16[:, off:off + 32].rearrange("p (i j) -> p i j", i=8),
                v_sp2[:, 0:32].rearrange("p (i j) -> p i j", i=8),
                v_smx[:, 0:8].unsqueeze(2).broadcast_to([128, 8, 4]),
                ALU.mult), waits=[(sv, sv.n)], inc=sv)
        hv = sv.n
        if FFN_STAGE < 6:
            return
        for i in range(8):
            def tph(e, i=i):
                e.transpose(p_tp[0:16, 0:128], v_hres16[:, i * 16:(i + 1) * 16], s_id[:])
                e.transpose(p_tp[0:4, 128:256], hpp16[:, i * 4:(i + 1) * 4], s_id[:])
                return e.transpose(p_tp[0:4, 256:384], hpp16[:, 32 + i * 4:32 + (i + 1) * 4],
                                   s_id[:])
            P.emit("tensor", tph, waits=[(sv, hv), (sa, sa.n), (sm["scn"], sm["scn"].n)],
                   inc=sp)

            def evh(e, i=i):
                _cp(e, s_bn[:, i * 128:(i + 1) * 128], p_tp[0:16, 0:128])
                _cp(e, s_cn[0:4, i * 128:(i + 1) * 128], p_tp[0:4, 128:256])
                return _cp(e, s_hpo[0:4, i * 128:(i + 1) * 128], p_tp[0:4, 256:384])
            P.emit("scalar", evh, waits=[(sp, sp.n)], inc=sa)
        hrv = sa.n
        P.emit("gpsimd", dma(v_wbr.rearrange("p (n e) -> p n e", n=8), wbr),
               waits=[(sm["omm"], sm["omm"].n), (sm["xnf"], sm["xnf"].n),
                      (sm["tpd"], sm["tpd"].n), (sm["ffp"], sm["ffp"].n)],
               inc=sm["ffw"], inc_k=16)

        def emit_bcast_row(src, r, dst_off, extra=(), kdim=16):
            def mm2(e, src=src, r=r, kdim=kdim):
                e.matmul(p_bc[:], s_oh16[0:kdim, r * 128:(r + 1) * 128],
                         src[0:kdim, 0:512], start=True, stop=True)
                return e.matmul(p_dt[:], s_oh16[0:kdim, r * 128:(r + 1) * 128],
                                src[0:kdim, 512:1024], start=True, stop=True)
            P.emit("tensor", mm2, waits=[(sa, hrv), (sa, sa.n)] + list(extra), inc=sp)

            def ev2(e, dst_off=dst_off):
                _cp(e, v_hb[:, dst_off:dst_off + 512], p_bc[:])
                return _cp(e, v_hb[:, dst_off + 512:dst_off + TH], p_dt[:])
            P.emit("scalar", ev2, waits=[(sp, sp.n), (sv, sv.n)], inc=sa)

        if FFN_STAGE < 7:
            return
        for j in range(4):
            emit_bcast_row(s_cn, j, j * TH)
        bpv = sa.n
        if FFN_STAGE < 7.5:
            return
        for ci in range(8):
            def xpre(e, ci=ci):
                acc = v_xpre[:, ci * TH:(ci + 1) * TH]
                e.tensor_tensor(acc, v_xnm[:, ci * TH:(ci + 1) * TH],
                                v_hb[:, 0:TH], ALU.mult)
                for j in range(1, 4):
                    e.tensor_tensor(v_yf2, v_xnm[:, (j * 8 + ci) * TH:(j * 8 + ci + 1) * TH],
                                    v_hb[:, j * TH:(j + 1) * TH], ALU.mult)
                    ins = e.tensor_add(acc, acc, v_yf2)
                return ins
            P.emit("vector", xpre, waits=[(sa, bpv), (sm["wod"], sm["wod"].n)], inc=sv)
        xpv = sv.n
        if FFN_STAGE < 8:
            return
        for co in range(8):
            for tc in range(2):
                def mmbr(e, co=co, tc=tc):
                    pp = p_o[0]
                    for k in range(8):
                        ins = e.matmul(pp[:],
                                       v_wbr[:, k * 1024 + co * 128:k * 1024 + (co + 1) * 128],
                                       v_xpre[:, k * TH + tc * 512:k * TH + (tc + 1) * 512],
                                       start=(k == 0), stop=(k == 7))
                    return ins
                P.emit("tensor", mmbr, waits=[(sv, xpv), (sm["ffw"], sm["ffw"].n),
                                              (sa, sa.n), (sm["oev"], sm["oev"].n)], inc=sp)
                P.emit("scalar", lambda e, co=co, tc=tc: _cp(e, 
                    v_ybr[:, co * TH + tc * 512:co * TH + (tc + 1) * 512],
                    p_o[0][:]), waits=[(sp, sp.n)], inc=sa)
        ybv = sa.n
        if FFN_STAGE < 9:
            return
        for i in range(4):
            for j in range(4):
                emit_bcast_row(s_bn, i * 4 + j, j * TH, extra=[(sv, sv.n), (sa, sa.n)])
            emit_bcast_row(s_hpo, i, 4 * TH, extra=[(sa, sa.n)], kdim=4)
            brv = sa.n
            for ci in range(8):
                fc = i * 8 + ci
                P.emit("sync", lambda e, fc=fc: e.dma_start(
                    out=v_sld, in_=s_d[fc * 128:(fc + 1) * 128, :]),
                    waits=[(sm["oev"], sm["oev"].n), (sm["fgd"], sm["fgd"].n)],
                    inc=sm["fgd"], inc_k=16)
                sldv = sm["fgd"].n

                def mix(e, ci=ci):
                    acc = v_sgc
                    e.tensor_tensor(acc, v_xnm[:, ci * TH:(ci + 1) * TH],
                                    v_hb[:, 0:TH], ALU.mult)
                    for j in range(1, 4):
                        e.tensor_tensor(v_yf2, v_xnm[:, (j * 8 + ci) * TH:(j * 8 + ci + 1) * TH],
                                        v_hb[:, j * TH:(j + 1) * TH], ALU.mult)
                        e.tensor_add(acc, acc, v_yf2)
                    e.tensor_tensor(v_yf2, v_ybr[:, ci * TH:(ci + 1) * TH],
                                    v_hb[:, 4 * TH:5 * TH], ALU.mult)
                    return e.tensor_add(acc, acc, v_yf2)
                P.emit("vector", mix, waits=[(sa, brv), (sm["ffp"], sm["ffp"].n)], inc=sv)
                P.emit("scalar", lambda e: e.activation(v_sgc, v_sgc, AF.Silu),
                       waits=[(sv, sv.n)], inc=sa)
                P.emit("vector", lambda e: e.tensor_mul(v_sgc, v_sgc, v_sld),
                       waits=[(sa, sa.n), (sm["fgd"], sldv)], inc=sv)
                sgv = sv.n
                for g in range(2):
                    def tps(e, g=g):
                        for jj in range(4):
                            tt = g * 4 + jj
                            ins = e.transpose(p_tp[:, jj * 128:(jj + 1) * 128],
                                              v_sgc[:, tt * 128:(tt + 1) * 128], s_id[:])
                        return ins
                    P.emit("tensor", tps, waits=[(sv, sgv), (sa, sa.n),
                                                 (sm["fgd"], sm["fgd"].n)], inc=sp)
                    P.emit("scalar", lambda e: _cp(e, v_tev[:, 0:512], p_tp[:, 0:512]),
                           waits=[(sp, sp.n), (sm["fgd"], sm["fgd"].n)], inc=sa)
                    P.emit("sync", lambda e, fc=fc, g=g: e.dma_start(
                        out=sgT_d[g * 512:(g + 1) * 512, fc * 128:(fc + 1) * 128]
                        .rearrange("(a p) c -> p a c", p=128),
                        in_=v_tev[:, 0:512].rearrange("p (a c) -> p a c", a=4)),
                        waits=[(sa, sa.n)], inc=sm["fgd"], inc_k=16)

    def phase_FIN():
        fgv = sm["fgd"].n
        for i in range(8):
            P.emit("sync", lambda e, i=i: e.dma_start(
                out=v_fin, in_=x_mine[i * 128:(i + 1) * 128, :]),
                waits=[(sm["fgd"], fgv), (sm["outd"], sm["outd"].n)],
                inc=sm["outd"], inc_k=16)
            P.emit("sync", lambda e, i=i: e.dma_start(
                out=v_sq, in_=sgT_d[i * 128:(i + 1) * 128, :]),
                waits=[(sm["ffv"], sm["ffv"].n), (sm["ffa"], sm["ffa"].n)],
                inc=sm["outd"], inc_k=16)
            fv = sm["outd"].n
            P.emit("vector", lambda e: e.tensor_add(v_fin, v_fin, v_sq),
                   waits=[(sm["outd"], fv)], inc=sm["fgd"])
            P.emit("sync", lambda e, i=i: e.dma_start(
                out=out[i * 128:(i + 1) * 128, :], in_=v_fin),
                waits=[(sm["fgd"], sm["fgd"].n)], inc=sm["outd"], inc_k=16)
        P.wait("sync", (sm["outd"], sm["outd"].n))
        P.wait("gpsimd", (sm["outd"], sm["outd"].n))

    phase_LN(0, x_b)
    phase_A(0)
    cc_a0 = sm["cc"].n
    if UPTO >= 2:
        phase_LN(1, x_b)
        phase_A(1)
    cc_a1 = sm["cc"].n
    if UPTO >= 3:
        phase_N(0, cc_a0)
        phase_B(0)
    if UPTO >= 4:
        phase_N(1, cc_a1)
        phase_B(1)
    cc_y = sm["cc"].n
    if UPTO >= 5:
        phase_C(cc_y)
    if UPTO >= 6:
        phase_FFN()
    if UPTO >= 7:
        phase_FIN()
    else:
        allw = [(s, s.n) for s in sm.values()]
        for i in range(8):
            P.emit("sync", lambda e, i=i: e.dma_start(
                out=out[i * 128:(i + 1) * 128, :], in_=v_sq.bitcast(F32)[:, 0:H // 2]
                if False else R5.bitcast(F32)[:, 0:H]),
                waits=allw if i == 0 else [], inc=sm["outd"], inc_k=16)
        P.wait("sync", (sm["outd"], sm["outd"].n))
        P.wait("gpsimd", (sm["outd"], sm["outd"].n))

    if DEBUG:
        allw = [(s, s.n) for s in sm.values()]
        pairs = [("d_xc", xc_d), ("d_zs", zs_d), ("d_ssm0", ssm_out[0]),
                 ("d_ssm1", ssm_out[1]), ("d_yag0", yag_out[0]), ("d_yag1", yag_out[1]),
                 ("d_sd", s_d), ("d_sgT", sgT_d), ("d_hres", s_bn[:]),
                 ("d_hpre", s_cn[:]), ("d_hpo", s_hpo[:]), ("d_xpre", v_xpre),
                 ("d_ybr", v_ybr), ("d_rs2", v_rs2), ("d_lg", v_lg),
                 ("d_uu", v_uu[:, 0:32]), ("d_vv", v_vv[:, 0:32]),
                 ("d_h16", v_hres16[:, 0:128]), ("d_hpp", hpp16[:, 0:64]),
                 ("d_zt", v_zt), ("d_m1", v_m1[:, 0:32]), ("d_s1", v_s1[:, 0:32]),
                 ("d_e1", v_e1),
                 ("d_sb", R5.bitcast(F32)[:, 384:432]),
                 ("d_sp1", v_sp1[:, 0:32]), ("d_sp2", v_sp2[:, 0:32]),
                 ("d_smx", v_smx[:, 0:8]),
                 ("d_chi", v_chi), ("d_clo", v_clo),
                 ("d_bb", v_bb), ("d_cb", v_cb), ("d_u", v_u),
                 ("d_da", v_da), ("d_h", v_h), ("d_dbx", v_dbx)]
        for k, (nm, srcT) in enumerate(pairs):
            P.emit("gpsimd", lambda e, nm=nm, srcT=srcT: e.dma_start(
                out=dbg[nm], in_=srcT), waits=allw if k == 0 else [],
                inc=sm["outd"], inc_k=16)
        P.wait("gpsimd", (sm["outd"], sm["outd"].n))

    with nc.Block() as block:
        for eng in ENGS:
            def make(eng):
                def body(e):
                    for f in P.ops[eng]:
                        f(e)
                return body
            getattr(block, eng)(make(eng))

    ctx.close()
    return nc


_NC_CACHE = {}


def _stage(inputs, j):
    ip = inputs
    W = ip["L0_attn_in_proj_w"]
    Wx = W[:, j * DH:(j + 1) * DH]
    Wz = W[:, DIN + j * DH:DIN + (j + 1) * DH]

    def blk(Wp):
        return np.ascontiguousarray(Wp.reshape(KC, 128, ND, 128).transpose(2, 1, 0, 3))
    wxz = np.concatenate([blk(Wx), blk(Wz)], axis=0)
    wxpT = np.ascontiguousarray(
        ip["L0_attn_x_proj_w"].T[j * DH:(j + 1) * DH, :]
        .reshape(ND, 128, R + 2 * NST).transpose(1, 0, 2))
    wdtT = np.ascontiguousarray(
        ip["L0_attn_dt_proj_w"][j * DH:(j + 1) * DH, :].T
        .reshape(2, 128, DH).transpose(1, 0, 2))
    woT = np.ascontiguousarray(
        ip["L0_attn_out_proj_w"].T.reshape(2 * ND, 128, KC, 128).transpose(2, 1, 0, 3))

    def cols(v):
        return np.ascontiguousarray(v[j * DH:(j + 1) * DH].reshape(ND, 128).T)

    convw = np.ascontiguousarray(
        ip["L0_attn_conv_w"][j * DH:(j + 1) * DH].reshape(ND, 128, 4)
        .transpose(1, 0, 2).reshape(128, ND * 4))
    alog = np.ascontiguousarray(
        ip["L0_attn_A_log"][j * DH:(j + 1) * DH].reshape(ND, 128, NST)
        .transpose(1, 0, 2).reshape(128, ND * NST))
    wres_cat = np.concatenate([ip["L0_ffn_W_res"], ip["L0_ffn_W_pre"],
                               ip["L0_ffn_W_post"]], axis=0)
    wresT = np.ascontiguousarray(wres_cat.T.reshape(KC, 128, 24).transpose(1, 0, 2))
    wbr = np.ascontiguousarray(
        ip["L0_ffn_W_branch"].reshape(8, 128, 1024).transpose(1, 0, 2))
    oh = np.zeros((16, 16, 128), np.float16)
    for n in range(16):
        oh[n, n, :] = 1.0
    hb = np.concatenate([ip["L0_ffn_H_res_base"].ravel(),
                         ip["L0_ffn_H_pre_base"].ravel(),
                         ip["L0_ffn_H_post_base"].ravel()]).reshape(1, 24)
    return dict(
        wxz=wxz, wxpT=wxpT, wdtT=wdtT, woT=woT, convw=convw,
        convb=cols(ip["L0_attn_conv_b"]).copy(),
        dparam=cols(ip["L0_attn_D_param"]).copy(),
        dtbias=cols(ip["L0_attn_dt_bias"]).copy(),
        alog=alog,
        dtlnw=np.ascontiguousarray(ip["L0_attn_dt_ln_w"].reshape(2, 128).T),
        blnw=np.ascontiguousarray(ip["L0_attn_B_ln_w"].reshape(16, 1)),
        clnw=np.ascontiguousarray(ip["L0_attn_C_ln_w"].reshape(16, 1)),
        wresT=wresT, wbr=wbr,
        hbases=np.ascontiguousarray(hb, dtype=np.float32),
        gate=np.ascontiguousarray(ip["L0_ffn_gate"].reshape(1, 1)),
        eye4=np.ascontiguousarray(np.eye(4, dtype=np.float32).reshape(1, 16)),
        ident=np.eye(128, dtype=np.float16),
        ones128c=np.ones((128, 1), np.float32),
        ones16c=np.ones((16, 1), np.float32),
        ones1r=np.ones((1, 128), np.float32),
        oh16=np.ascontiguousarray(oh.reshape(16, 16 * 128)),
        flagj=np.array([[j, 0]], np.int32),
    )


def kernel(**inputs):
    inputs = {k: np.asarray(v) for k, v in inputs.items()}
    x = np.asarray(inputs["L0_attn_hidden_states"], dtype=np.float32)
    if "nc" not in _NC_CACHE:
        _NC_CACHE["nc"] = build_nc()
    nc = _NC_CACHE["nc"]
    staged = [_stage(inputs, j) for j in range(2)]
    in_maps = []
    for c in range(8):
        b, j = c // 2, c % 2
        m = dict(staged[j])
        m["x_b"] = np.ascontiguousarray(x[b])
        m["x_mine"] = np.ascontiguousarray(x[b, j * TH:(j + 1) * TH])
        in_maps.append(m)
    res = run_bass_kernel_spmd(nc, in_maps, core_ids=list(range(8)))
    outp = np.empty((B_SZ, T, H), np.float32)
    for c in range(8):
        b, j = c // 2, c % 2
        outp[b, j * TH:(j + 1) * TH] = np.asarray(res.results[c]["out"])
    if DEBUG:
        kernel.debug_results = res.results
    return outp


if __name__ == "__main__":
    nc = build_nc()
    print("built ok")


# version 18 (final)
# speedup vs baseline: 1.0579x; 1.0093x over previous
import numpy as np
from contextlib import ExitStack

import concourse.bass as bass
import concourse.mybir as mybir
from concourse.bass_utils import run_bass_kernel_spmd

F32 = mybir.dt.float32
F16 = mybir.dt.float16
I32 = mybir.dt.int32
AF = mybir.ActivationFunctionType
ALU = mybir.AluOpType
AX = mybir.AxisListType

B_SZ, T, TH, H = 4, 2048, 1024, 4096
DIN, DH, ND, KC = 8192, 4096, 32, 32
NST, R = 16, 256
EPS_ATTN, EPS_LN, RMS_EPS = 1e-6, 1e-5, 1.1920929e-07
SINK_ITERS, SINK_TAU = 50, 0.1
TS = 512
TSP = TS + 1

ENGS = ("sync", "vector", "scalar", "tensor", "gpsimd")
UPTO = 99
FFN_STAGE = 99
DEBUG = False


class Sem:
    def __init__(self, nc, name):
        self.h = nc.ctx.enter_context(nc.semaphore(name))
        self.n = 0


class Prog:
    def __init__(self):
        self.ops = {e: [] for e in ENGS}

    def emit(self, eng, fn, waits=(), inc=None, inc_k=1):
        w = tuple((s.h, v) for (s, v) in waits)

        def run(e, fn=fn, w=w, inc=inc, inc_k=inc_k):
            for sh, v in w:
                e.wait_ge(sh, v)
            ins = fn(e)
            if inc is not None:
                ins.then_inc(inc.h, inc_k)
        self.ops[eng].append(run)
        if inc is not None:
            inc.n += inc_k

    def wait(self, eng, *sems_vals):
        w = tuple((s.h, v) for (s, v) in sems_vals)

        def run(e, w=w):
            for sh, v in w:
                e.wait_ge(sh, v)
        self.ops[eng].append(run)



def _cp(e, out, in_):
    if hasattr(e, "tensor_copy"):
        return e.tensor_copy(out, in_)
    return e.copy(out, in_)

def build_nc():
    nc = bass.Bass("TRN2", target_bir_lowering=False)
    ctx = ExitStack()
    E = ctx.enter_context

    def din(name, shape, dt=F32):
        return nc.dram_tensor(name, list(shape), dt, kind="ExternalInput").ap()

    x_b = din("x_b", [T, H])
    x_mine = din("x_mine", [TH, H])
    wxz = din("wxz", [2 * ND, 128, KC, 128])
    wxpT = din("wxpT", [128, ND, R + 2 * NST])
    wdtT = din("wdtT", [128, 2, DH])
    woT = din("woT", [KC, 128, 2 * ND, 128])
    convw = din("convw", [128, ND * 4])
    convb = din("convb", [128, ND])
    dparam = din("dparam", [128, ND])
    dtbias = din("dtbias", [128, ND])
    alog = din("alog", [128, ND * NST])
    dtlnw = din("dtlnw", [128, 2])
    blnw = din("blnw", [16, 1])
    clnw = din("clnw", [16, 1])
    wresT = din("wresT", [128, KC, 24])
    wbr = din("wbr", [128, 8, 1024])
    hbases = din("hbases", [1, 24])
    gate = din("gate", [1, 1])
    eye4 = din("eye4", [1, 16])
    ident = din("ident", [128, 128], F16)
    ones128c = din("ones128c", [128, 1])
    ones16c = din("ones16c", [16, 1])
    ones1r = din("ones1r", [1, 128])
    oh16 = din("oh16", [16, 16 * 128], F16)
    flagj = din("flagj", [1, 2], I32)

    out = nc.dram_tensor("out", [TH, H], F32, kind="ExternalOutput").ap()
    dbg = {}
    if DEBUG:
        for nm, shape, dt in [("d_xc", [DH, T], F16), ("d_zs", [DH, T], F16),
                              ("d_ssm0", [R + 2 * NST, TH], F32),
                              ("d_ssm1", [R + 2 * NST, TH], F32),
                              ("d_yag0", [DIN, TH], F16), ("d_yag1", [DIN, TH], F16),
                              ("d_sd", [H, TH], F16), ("d_sgT", [TH, H], F16),
                              ("d_hres", [16, TH], F16), ("d_hpre", [16, TH], F16),
                              ("d_hpo", [4, TH], F16), ("d_xpre", [128, 8 * TH], F16),
                              ("d_ybr", [128, 8 * TH], F16), ("d_rs2", [128, 8], F32),
                              ("d_lg", [128, 192], F32), ("d_uu", [128, 32], F32),
                              ("d_vv", [128, 32], F32), ("d_h16", [128, 128], F16),
                              ("d_hpp", [128, 64], F16), ("d_zt", [128, 128], F32),
                              ("d_m1", [128, 32], F32), ("d_s1", [128, 32], F32),
                              ("d_e1", [128, 128], F32), ("d_sb", [128, 48], F32),
                              ("d_sp1", [128, 32], F32), ("d_sp2", [128, 32], F32),
                              ("d_smx", [128, 8], F32),
                              ("d_chi", [128, 512], F16), ("d_clo", [128, 512], F16),
                              ("d_chi0", [128, 512], F16), ("d_clo0", [128, 512], F16),
                              ("d_chi1", [128, 512], F16), ("d_clo1", [128, 512], F16),
                              ("d_da0", [128, 8 * 514], F32), ("d_dbx0", [128, 8 * 514], F16),
                              ("d_h0", [128, 8 * 514], F32),
                              ("d_bb", [128, 8192], F16), ("d_cb", [128, 8192], F16),
                              ("d_u", [128, 512], F16), ("d_da", [128, 8 * 514], F32),
                              ("d_h", [128, 8 * 514], F32), ("d_dbx", [128, 8 * 514], F16)]:
            dbg[nm] = nc.dram_tensor(nm, shape, dt, kind="ExternalOutput").ap()

    xc_d = nc.dram_tensor("xc_d", [DH, T], F16).ap()
    zs_d = nc.dram_tensor("zs_d", [DH, T], F16).ap()
    ssm_in = [nc.dram_tensor(f"ssm_in{t}", [R + 2 * NST, TH], F32).ap() for t in range(2)]
    ssm_out = [nc.dram_tensor(f"ssm_out{t}", [R + 2 * NST, TH], F32).ap() for t in range(2)]
    yag_in = [nc.dram_tensor(f"yag_in{t}", [DH, TH], F16).ap() for t in range(2)]
    yag_out = [nc.dram_tensor(f"yag_out{t}", [DIN, TH], F16).ap() for t in range(2)]
    s_d = nc.dram_tensor("s_d", [H, TH], F16).ap()
    sgT_d = nc.dram_tensor("sgT_d", [TH, H], F16).ap()

    RG = [[0, 1], [2, 3], [4, 5], [6, 7]]

    def sb(name, shape, dt):
        return E(nc.sbuf_tensor(name, list(shape), dt))

    R0 = sb("R0", [128, 33024], F16)
    v_xn = R0[:, 0:KC * TH]
    NH, LEN = 8, TS + 2
    v_da = R0.bitcast(F32)[:, 0:NH * LEN]
    v_dbx = R0[:, 8224:8224 + NH * LEN]
    v_h = R0.bitcast(F32)[:, 6168:6168 + NH * LEN]
    v_p = R0[:, 20560:20560 + NH * LEN]
    v_yres = R0[:, 0:2 * ND * 512]
    v_xnm = R0[:, 0:KC * TH]

    R2 = sb("R2", [128, 24576], F16)
    v_wxp = R2[:, 0:ND * (R + 2 * NST)]
    v_ssm = R2.bitcast(F32)[:, 4800:4800 + 3 * TH]
    v_xt = R2.bitcast(F32)[:, 7872:7872 + 4096]
    v_fin = v_xt
    v_bb = R2[:, 0:NST * TS]
    v_cb = R2[:, NST * TS:2 * NST * TS]
    v_wdt = R2[:, 2 * NST * TS:2 * NST * TS + 2 * DH]
    v_wo = [R2[:, i * (2 * ND * 128):(i + 1) * (2 * ND * 128)] for i in range(2)]
    v_xpre = R2[:, 0:8 * TH]
    v_ybr = R2[:, 8 * TH:16 * TH]
    v_wbr = R2[:, 16 * TH:16 * TH + 8 * 1024]

    R3 = sb("R3", [128, 8192], F16)
    v_w16 = [R3[:, i * (KC * 128):(i + 1) * (KC * 128)] for i in range(2)]
    v_hb = R3[:, 0:5 * TH]
    v_sev = R3[:, 5 * TH:5 * TH + 512]
    v_sgc = R3[:, 6 * TH:7 * TH]


    R5 = sb("R5", [128, 8192], F16)
    v_sq = R5[:, 0:H]
    v_dt2 = R5.bitcast(F32)[:, 0:2 * TH]
    v_nsq = R5.bitcast(F32)[:, 2 * TH:4 * TH]
    v_zt = R5.bitcast(F32)[:, 0:128]
    v_e1 = R5.bitcast(F32)[:, 128:256]
    hpp16 = R5[:, 256:320]
    v_uu = R5.bitcast(F32)[:, 256:288]
    v_vv = R5.bitcast(F32)[:, 288:320]
    v_m1 = R5.bitcast(F32)[:, 320:352]
    v_s1 = R5.bitcast(F32)[:, 352:384]
    v_eyeb = R5.bitcast(F32)[:, 384:400]
    v_baseb = R5.bitcast(F32)[:, 400:424]
    v_gcol = R5.bitcast(F32)[:, 424:425]
    v_omg = R5.bitcast(F32)[:, 425:426]
    v_mx = R5.bitcast(F32)[:, 432:440]
    v_lg = R5.bitcast(F32)[:, 512:704]
    v_hres16 = R5[:, 1704:1832]
    v_lgst16 = R5[:, 4096:4096 + TH]


    R7 = sb("R7", [128, 11992], F16)
    v_dtv = R7[:, 0:TS]
    v_u = R7[:, TS:2 * TS]
    v_sld = R7[:, 10936:10936 + TH]
    v_xcr = [R7[:, 1024 + i * TS:1024 + (i + 1) * TS] for i in range(2)]
    v_tev = R7[:, 1024:1024 + 512]
    v_zsr = [R7[:, 2048 + i * TS:2048 + (i + 1) * TS] for i in range(2)]
    v_ys = R7.bitcast(F32)[:, 1536:1536 + TS]
    v_yf = R7[:, 4096:4096 + TS]
    v_yf2 = R7[:, 4608:4608 + TH]
    v_chi = R7[:, 5632:5632 + ND * NST]
    v_clo = R7[:, 6144:6144 + ND * NST]
    v_spt = R7.bitcast(F32)[:, 3328:3328 + 512]
    v_spt2 = R7.bitcast(F32)[:, 3840:3840 + 512]
    v_xraw = R7[:, 8704:8704 + TH + 8]
    v_xcw = R7[:, 9736:9736 + TH]
    v_ctail = R7[:, 10760:10760 + ND * 4]
    v_stat = R7.bitcast(F32)[:, 5444:5460]
    v_rs2 = R7.bitcast(F32)[:, 5460:5468]
    v_lo32 = R7.bitcast(F32)[:, 5980:5996]
    v_sp1 = R7.bitcast(F32)[:, 3072:3104]
    v_sp2 = R7.bitcast(F32)[:, 3104:3136]
    v_smx = R7.bitcast(F32)[:, 3136:3144]

    R8 = sb("R8", [128, 8192], F16)
    v_braw = R8.bitcast(F32)[:, 0:TH]
    v_craw = R8.bitcast(F32)[:, TH:2 * TH]
    v_r1 = R8.bitcast(F32)[:, 2 * TH:3 * TH]
    v_rb = R8[:, 6 * TH:7 * TH]
    v_rb16 = R8[:, 7 * TH:8 * TH]

    _dtrn = sb("s_dtrn", [128, 2 * TH], F16)
    s_dtrn = [_dtrn, _dtrn]
    s_bn = sb("s_bn", [16, TH], F16)
    s_cn = sb("s_cn", [16, TH], F16)
    s_hpo = sb("s_hpo", [4, TH], F16)
    s_convw = sb("s_convw", [128, ND * 4], F32)
    s_convb = sb("s_convb", [128, ND], F32)
    s_dpar = sb("s_dpar", [128, ND], F32)
    s_dtb = sb("s_dtb", [128, ND], F32)
    s_aneg = sb("s_aneg", [128, ND * NST], F32)
    s_dtln = sb("s_dtln", [128, 2], F32)
    s_bln = sb("s_bln", [16, 1], F32)
    s_cln = sb("s_cln", [16, 1], F32)
    s_wres = sb("s_wres", [128, KC * 24], F16)
    s_id = sb("s_id", [128, 128], F16)
    s_o128c = sb("s_o128c", [128, 1], F32)
    s_o16c = sb("s_o16c", [16, 1], F32)
    s_o1r = sb("s_o1r", [1, 128], F32)
    s_oh16 = sb("s_oh16", [16, 16 * 128], F16)
    s_small = sb("s_small", [1, 64], F32)
    s_flag = sb("s_flag", [1, 2], I32)

    p_x = E(nc.psum_tensor("p_x", [128, 512], F32))
    p_z = E(nc.psum_tensor("p_z", [128, 512], F32))
    p_ssm = E(nc.psum_tensor("p_ssm", [128, 3 * 512], F32))
    p_tp = E(nc.psum_tensor("p_tp", [128, 1024], F16))
    p_dt = E(nc.psum_tensor("p_dt", [128, 512], F32))
    p_bc = E(nc.psum_tensor("p_bc", [128, 512], F32))
    p_o = [p_x, p_z]

    sm = {k: Sem(nc, k) for k in [
        "ldc", "setup", "xld", "lnv", "lna", "tpd", "xnf",
        "wld", "pxz", "evx", "cvd", "sil", "xcd", "psd", "ssa", "ssmd", "cc",
        "nrm", "bcb", "dtm", "spd", "dad", "dbd", "scn", "yrd", "xcr",
        "wod", "yrl", "omm", "oev",
        "ffw", "ffv", "ffa", "ffp", "fgd", "outd"]}

    P = Prog()

    def dma(dst, src):
        return lambda e: e.dma_start(out=dst, in_=src)

    ld = sm["ldc"]
    for dst, src in [
        (s_convw[:], convw), (s_convb[:], convb), (s_dpar[:], dparam),
        (s_dtb[:], dtbias), (s_aneg[:], alog), (s_dtln[:], dtlnw),
        (s_bln[:], blnw), (s_cln[:], clnw), (s_id[:], ident),
        (s_o128c[:], ones128c), (s_o16c[:], ones16c), (s_o1r[:], ones1r),
        (s_oh16[:], oh16), (s_flag[:], flagj),
        (s_small[0:1, 0:24], hbases), (s_small[0:1, 24:25], gate),
        (s_small[0:1, 32:48], eye4),
    ]:
        P.emit("sync", dma(dst, src), inc=ld, inc_k=16)
    P.emit("gpsimd", dma(v_wxp.rearrange("p (n e) -> p n e", n=ND), wxpT), inc=ld, inc_k=16)
    P.emit("gpsimd", dma(s_wres[:].rearrange("p (n e) -> p n e", n=KC), wresT), inc=ld, inc_k=16)
    NCONST = ld.n

    P.wait("scalar", (ld, NCONST))
    P.emit("scalar", lambda e: e.activation(s_aneg[:], s_aneg[:], AF.Exp), inc=sm["setup"])
    P.wait("vector", (ld, NCONST))
    P.emit("vector", lambda e: e.tensor_scalar_mul(s_aneg[:], s_aneg[:], -1.0),
           waits=[(sm["setup"], 1)], inc=sm["setup"])
    P.emit("vector", lambda e: e.memset(v_chi[:], 0.0), inc=sm["setup"])
    P.emit("vector", lambda e: e.memset(v_clo[:], 0.0), inc=sm["setup"])
    P.emit("vector", lambda e: e.memset(v_ctail[:], 0.0), inc=sm["setup"])
    SETUP_N = sm["setup"].n
    P.wait("tensor", (ld, NCONST))
    P.wait("gpsimd", (ld, NCONST))
    P.wait("sync", (ld, NCONST))

    def phase_LN(th, src_dram, ffn=False):
        if ffn:
            sv, sa, sp = sm["ffv"], sm["ffa"], sm["ffp"]
        else:
            sv, sa, sp = sm["lnv"], sm["lna"], sm["tpd"]
        r0_guard = (sm["pxz"], sm["pxz"].n) if not ffn else (sm["omm"], sm["omm"].n)
        for i in range(8):
            if ffn:
                srcap = src_dram[i * 128:(i + 1) * 128, :]
            else:
                srcap = src_dram[th * TH + i * 128:th * TH + (i + 1) * 128, :]
            w = [(sm["xnf"], sm["xnf"].n), (sm["fgd"], sm["fgd"].n)] if (i > 0 or th > 0 or ffn) else []
            P.emit("sync", dma(v_xt, srcap), waits=w, inc=sm["xld"], inc_k=16)
            xv = sm["xld"].n
            P.emit("vector", lambda e: e.tensor_scalar(
                v_sq, v_xt, 1.0, 0.0, ALU.mult, ALU.add, accum_out=v_stat[:, 0:1]),
                waits=[(sm["xld"], xv), (sa, sa.n)], inc=sv)
            P.emit("scalar", lambda e: e.activation(
                v_sq, v_xt, AF.Square, accum_out=v_stat[:, 1:2]),
                waits=[(sm["xld"], xv), (sv, sv.n)], inc=sa)

            def stats(e):
                e.tensor_scalar_mul(v_stat[:, 0:1], v_stat[:, 0:1], 1.0 / H)
                e.tensor_scalar_mul(v_stat[:, 1:2], v_stat[:, 1:2], 1.0 / H)
                e.tensor_tensor(v_stat[:, 8:9], v_stat[:, 0:1], v_stat[:, 0:1], ALU.mult)
                e.tensor_sub(v_stat[:, 1:2], v_stat[:, 1:2], v_stat[:, 8:9])
                return e.tensor_scalar(v_stat[:, 9:10], v_stat[:, 1:2], 1.0, EPS_LN,
                                       ALU.mult, ALU.add)
            P.emit("vector", stats, waits=[(sv, sv.n), (sa, sa.n)], inc=sv)
            P.emit("scalar", lambda e: e.activation(v_stat[:, 10:11], v_stat[:, 9:10], AF.Sqrt),
                   waits=[(sv, sv.n)], inc=sa)

            def fin_a(e, ffn=ffn):
                ins = e.reciprocal(v_stat[:, 11:12], v_stat[:, 10:11])
                if ffn:
                    e.tensor_tensor(v_stat[:, 12:13], v_stat[:, 1:2], v_stat[:, 11:12], ALU.mult)
                    e.tensor_tensor(v_stat[:, 12:13], v_stat[:, 12:13], v_stat[:, 11:12], ALU.mult)
                    ins = e.tensor_scalar(v_stat[:, 12:13], v_stat[:, 12:13], 1.0, RMS_EPS,
                                          ALU.mult, ALU.add)
                return ins
            P.emit("vector", fin_a, waits=[(sa, sa.n)], inc=sv)
            P.emit("vector", lambda e: e.tensor_scalar(
                v_sq, v_xt, v_stat[:, 0:1], v_stat[:, 11:12], ALU.subtract, ALU.mult),
                waits=[(sv, sv.n)], inc=sv)
            if ffn:
                P.emit("scalar", lambda e: e.activation(v_stat[:, 13:14], v_stat[:, 12:13],
                                                        AF.Sqrt), waits=[(sv, sv.n)], inc=sa)
                P.emit("vector", lambda e, i=i: e.reciprocal(v_rs2[:, i:i + 1], v_stat[:, 13:14]),
                       waits=[(sa, sa.n)], inc=sv)
            lv = sv.n
            for g in range(8):
                def tp(e, g=g):
                    for jj in range(4):
                        kk = g * 4 + jj
                        ins = e.transpose(p_tp[:, jj * 128:(jj + 1) * 128],
                                          v_sq[:, kk * 128:(kk + 1) * 128], s_id[:])
                    return ins
                P.emit("tensor", tp, waits=[(sv, lv), (sm["xnf"], sm["xnf"].n)], inc=sp)
                dst = (v_xnm if ffn else v_xn).rearrange("p (k t) -> p k t", k=KC)[
                    :, g * 4:(g + 1) * 4, i * 128:(i + 1) * 128]
                P.emit("scalar", lambda e, dst=dst: _cp(e, 
                    dst, p_tp[:, 0:512].rearrange("p (k t) -> p k t", k=4)),
                    waits=[(sp, sp.n), r0_guard], inc=sm["xnf"])

    pxz_after_d = {}

    def phase_A(th):
        P.emit("vector", lambda e: e.memset(v_ssm, 0.0),
               waits=[(sm["ssmd"], sm["ssmd"].n)], inc=sm["ssa"])
        ssm_clear = sm["ssa"].n
        xn_ready = sm["xnf"].n
        for d in range(ND):
            wb = v_w16[d % 2]
            key = (th, d - 2) if d >= 2 else ((th - 1, ND - 2 + d) if th > 0 else None)
            wfree = [(sm["pxz"], pxz_after_d[key])] if key in pxz_after_d else []
            P.emit("gpsimd", dma(wb.rearrange("p (k c) -> p k c", k=KC), wxz[d]),
                   waits=wfree + [(sm["xnf"], xn_ready)], inc=sm["wld"], inc_k=16)
            wv = sm["wld"].n
            for ts in range(2):
                def mmx(e, ts=ts, wb=wb):
                    for k in range(KC):
                        ins = e.matmul(p_x[:], wb[:, k * 128:(k + 1) * 128],
                                       v_xn[:, k * TH + ts * 512:k * TH + (ts + 1) * 512],
                                       start=(k == 0), stop=(k == KC - 1))
                    return ins
                P.emit("tensor", mmx, waits=[(sm["wld"], wv), (sm["evx"], sm["evx"].n)],
                       inc=sm["pxz"])
                P.emit("scalar", lambda e, ts=ts: _cp(e, 
                    v_xraw[:, 3 + ts * 512:3 + (ts + 1) * 512], p_x[:]),
                    waits=[(sm["pxz"], sm["pxz"].n), (sm["xcd"], sm["xcd"].n),
                           (sm["cvd"], sm["cvd"].n)], inc=sm["evx"])
            evv = sm["evx"].n
            if th == 0:
                P.emit("vector", lambda e: e.memset(v_xraw[:, 0:3], 0.0),
                       waits=[(sm["evx"], evv), (sm["setup"], SETUP_N)], inc=sm["cvd"])
            else:
                P.emit("vector", lambda e, d=d: _cp(e, 
                    v_xraw[:, 0:3], v_ctail[:, d * 4:d * 4 + 3]),
                    waits=[(sm["evx"], evv)], inc=sm["cvd"])
            P.emit("vector", lambda e, d=d: _cp(e, 
                v_ctail[:, d * 4:d * 4 + 3], v_xraw[:, TH:TH + 3]), inc=sm["cvd"])

            def conv(e, d=d):
                e.tensor_scalar(v_xcw, v_xraw[:, 3:3 + TH],
                                s_convw[:, d * 4 + 3:d * 4 + 4],
                                s_convb[:, d:d + 1], ALU.mult, ALU.add)
                for k in range(3):
                    ins = e.scalar_tensor_tensor(
                        v_xcw, v_xraw[:, k:k + TH], s_convw[:, d * 4 + k:d * 4 + k + 1],
                        v_xcw, ALU.mult, ALU.add)
                return ins
            P.emit("vector", conv, waits=[(sm["xcd"], sm["xcd"].n),
                                          (sm["cvd"], sm["cvd"].n)], inc=sm["cvd"])
            cvv = sm["cvd"].n
            P.emit("scalar", lambda e: e.activation(v_xcw, v_xcw, AF.Silu),
                   waits=[(sm["cvd"], cvv)], inc=sm["sil"])
            silv = sm["sil"].n
            P.emit("sync", lambda e, d=d, th=th: e.dma_start(
                out=xc_d[d * 128:(d + 1) * 128, th * TH:(th + 1) * TH], in_=v_xcw),
                waits=[(sm["sil"], silv)], inc=sm["xcd"], inc_k=16)
            for ts in range(2):
                def mmssm(e, d=d, ts=ts):
                    sl = v_xcw[:, ts * 512:(ts + 1) * 512]
                    e.matmul(p_ssm[:, 0:512], v_wxp[:, d * 288:d * 288 + 128], sl,
                             start=True, stop=True)
                    e.matmul(p_ssm[:, 512:1024], v_wxp[:, d * 288 + 128:d * 288 + 256], sl,
                             start=True, stop=True)
                    return e.matmul(p_ssm[0:32, 1024:1536],
                                    v_wxp[:, d * 288 + 256:d * 288 + 288],
                                    sl, start=True, stop=True)
                P.emit("tensor", mmssm, waits=[(sm["sil"], silv), (sm["ssa"], sm["ssa"].n)],
                       inc=sm["psd"])

                def ssmacc(e, ts=ts):
                    dst = v_ssm.rearrange("p (g t) -> p g t", g=3)[:, :, ts * 512:(ts + 1) * 512]
                    src = p_ssm[:].rearrange("p (g t) -> p g t", g=3)
                    return e.scalar_tensor_tensor(dst, src, 1.0, dst, ALU.mult, ALU.add)
                P.emit("vector", ssmacc, waits=[(sm["psd"], sm["psd"].n),
                                                (sm["ssa"], ssm_clear)], inc=sm["ssa"])
            P.emit("gpsimd", dma(wb.rearrange("p (k c) -> p k c", k=KC), wxz[ND + d]),
                   waits=[(sm["pxz"], sm["pxz"].n)], inc=sm["wld"], inc_k=16)
            wv2 = sm["wld"].n
            for ts in range(2):
                def mmz(e, ts=ts, wb=wb):
                    for k in range(KC):
                        ins = e.matmul(p_z[:], wb[:, k * 128:(k + 1) * 128],
                                       v_xn[:, k * TH + ts * 512:k * TH + (ts + 1) * 512],
                                       start=(k == 0), stop=(k == KC - 1))
                    return ins
                P.emit("tensor", mmz, waits=[(sm["wld"], wv2), (sm["sil"], sm["sil"].n)],
                       inc=sm["pxz"])
                P.emit("scalar", lambda e, ts=ts: e.activation(
                    v_xraw[:, 3 + ts * 512:3 + (ts + 1) * 512], p_z[:], AF.Silu),
                    waits=[(sm["pxz"], sm["pxz"].n), (sm["cvd"], cvv),
                           (sm["xcd"], sm["xcd"].n)], inc=sm["sil"])
            pxz_after_d[(th, d)] = sm["pxz"].n
            P.emit("sync", lambda e, d=d, th=th: e.dma_start(
                out=zs_d[d * 128:(d + 1) * 128, th * TH:(th + 1) * TH],
                in_=v_xraw[:, 3:3 + TH]),
                waits=[(sm["sil"], sm["sil"].n)], inc=sm["xcd"], inc_k=16)
        P.wait("sync", (sm["ssa"], sm["ssa"].n))
        P.emit("sync", lambda e, th=th: e.dma_start(out=ssm_in[th][0:128, :], in_=v_ssm[:, 0:TH]),
               inc=sm["ssmd"], inc_k=16)
        P.emit("sync", lambda e, th=th: e.dma_start(out=ssm_in[th][128:256, :],
                                                    in_=v_ssm[:, TH:2 * TH]),
               inc=sm["ssmd"], inc_k=16)
        P.emit("sync", lambda e, th=th: e.dma_start(out=ssm_in[th][256:288, :],
                                                    in_=v_ssm[0:32, 2 * TH:3 * TH]),
               inc=sm["ssmd"], inc_k=16)
        P.emit("gpsimd", lambda e, th=th: e.collective_compute(
            "AllReduce", ALU.add, replica_groups=RG,
            ins=[ssm_in[th].opt()], outs=[ssm_out[th].opt()]),
            waits=[(sm["ssmd"], sm["ssmd"].n)], inc=sm["cc"])

    def phase_N(th, ccv):
        nr = sm["nrm"]
        P.emit("sync", lambda e, th=th: e.dma_start(
            out=v_dt2[:, 0:TH], in_=ssm_out[th][0:128, :]),
            waits=[(sm["cc"], ccv), (sm["xnf"], sm["xnf"].n)], inc=nr, inc_k=16)
        P.emit("sync", lambda e, th=th: e.dma_start(
            out=v_dt2[:, TH:2 * TH], in_=ssm_out[th][128:256, :]),
            waits=[(sm["cc"], ccv)], inc=nr, inc_k=16)
        P.emit("sync", lambda e, th=th: e.dma_start(out=v_braw[0:16, :],
                                                    in_=ssm_out[th][256:272, :]),
               waits=[(sm["cc"], ccv)], inc=nr, inc_k=16)
        P.emit("sync", lambda e, th=th: e.dma_start(out=v_craw[0:16, :],
                                                    in_=ssm_out[th][272:288, :]),
               waits=[(sm["cc"], ccv)], inc=nr, inc_k=16)
        base = nr.n
        P.emit("scalar", lambda e: e.activation(v_nsq[:, 0:2 * TH], v_dt2, AF.Square),
               waits=[(nr, base)], inc=nr)

        def msum(e):
            last = None
            for c in range(2):
                pp = p_dt if c == 0 else p_bc
                for k in range(2):
                    last = e.matmul(pp[0:1, :], s_o128c[:],
                                    v_nsq[:, k * TH + c * 512:k * TH + (c + 1) * 512],
                                    start=(k == 0), stop=(k == 1))
            return last
        P.emit("tensor", msum, waits=[(nr, nr.n)], inc=nr)

        def rfun(e):
            e.tensor_scalar(v_r1[0:1, 0:512], p_dt[0:1, :], 1.0 / R, EPS_ATTN, ALU.mult, ALU.add)
            return e.tensor_scalar(v_r1[0:1, 512:1024], p_bc[0:1, :], 1.0 / R, EPS_ATTN,
                                   ALU.mult, ALU.add)
        P.emit("vector", rfun, waits=[(nr, nr.n)], inc=nr)
        P.emit("scalar", lambda e: e.activation(v_r1[0:1, :], v_r1[0:1, :], AF.Sqrt),
               waits=[(nr, nr.n)], inc=nr)
        P.emit("vector", lambda e: e.reciprocal(v_r1[0:1, :], v_r1[0:1, :]),
               waits=[(nr, nr.n)], inc=nr)
        rv = nr.n
        for c in range(2):
            P.emit("tensor", lambda e, c=c: e.matmul(
                p_bc[:] if c else p_dt[:], s_o1r[:], v_r1[0:1, c * 512:(c + 1) * 512],
                start=True, stop=True), waits=[(nr, rv)], inc=nr)
            P.emit("scalar", lambda e, c=c: _cp(e, 
                v_rb[:, c * 512:(c + 1) * 512], p_bc[:] if c else p_dt[:]),
                waits=[(nr, nr.n)], inc=nr)
        bv = nr.n

        def dtrn(e, th=th):
            last = None
            for k in range(2):
                last = e.scalar_tensor_tensor(
                    s_dtrn[th][:, k * TH:(k + 1) * TH], v_dt2[:, k * TH:(k + 1) * TH],
                    s_dtln[:, k:k + 1], v_rb, ALU.mult, ALU.mult)
            return last
        P.emit("vector", dtrn, waits=[(nr, bv), (sm["dtm"], sm["dtm"].n)], inc=nr)
        for (raw, lnw, dstn) in ((v_braw, s_bln, s_bn), (v_craw, s_cln, s_cn)):
            P.emit("scalar", lambda e, raw=raw: e.activation(
                v_nsq[0:16, 0:TH], raw[0:16, :], AF.Square), waits=[(nr, nr.n)], inc=nr)

            def bsum(e):
                e.matmul(p_dt[0:1, :], s_o16c[:], v_nsq[0:16, 0:512], start=True, stop=True)
                return e.matmul(p_bc[0:1, :], s_o16c[:], v_nsq[0:16, 512:1024],
                                start=True, stop=True)
            P.emit("tensor", bsum, waits=[(nr, nr.n)], inc=nr)

            def brs(e):
                e.tensor_scalar(v_r1[0:1, 0:512], p_dt[0:1, :], 1.0 / NST, EPS_ATTN,
                                ALU.mult, ALU.add)
                return e.tensor_scalar(v_r1[0:1, 512:1024], p_bc[0:1, :], 1.0 / NST, EPS_ATTN,
                                       ALU.mult, ALU.add)
            P.emit("vector", brs, waits=[(nr, nr.n)], inc=nr)
            P.emit("scalar", lambda e: e.activation(v_r1[0:1, :], v_r1[0:1, :], AF.Sqrt),
                   waits=[(nr, nr.n)], inc=nr)
            P.emit("vector", lambda e: e.reciprocal(v_r1[0:1, :], v_r1[0:1, :]),
                   waits=[(nr, nr.n)], inc=nr)

            def bc16(e):
                e.matmul(p_dt[0:16, :], s_o1r[0:1, 0:16], v_r1[0:1, 0:512],
                         start=True, stop=True)
                return e.matmul(p_bc[0:16, :], s_o1r[0:1, 0:16], v_r1[0:1, 512:1024],
                                start=True, stop=True)
            P.emit("tensor", bc16, waits=[(nr, nr.n)], inc=nr)

            def bevac(e):
                _cp(e, v_rb16[0:16, 0:512], p_dt[0:16, :])
                return _cp(e, v_rb16[0:16, 512:1024], p_bc[0:16, :])
            P.emit("scalar", bevac, waits=[(nr, nr.n)], inc=nr)
            P.emit("vector", lambda e, raw=raw, lnw=lnw, dstn=dstn: e.scalar_tensor_tensor(
                dstn[:], raw[0:16, :], lnw[:], v_rb16[0:16, :], ALU.mult, ALU.mult),
                waits=[(nr, nr.n)], inc=nr)

    xcr_hist = {}
    xcr_marks = []
    cs_pend = [None]
    scan_marks = []
    cs_slot = 0

    def phase_B(th):
        nonlocal cs_slot
        if th == 0:
            P.emit("gpsimd", dma(v_wdt.rearrange("p (n e) -> p n e", n=2), wdtT),
                   waits=[(sm["pxz"], sm["pxz"].n)], inc=sm["wld"], inc_k=16)
            P.emit("vector", lambda e: e.memset(
                v_da.rearrange("p (n t) -> p n t", n=NH)[:, :, 0:1], 0.0),
                waits=[(sm["pxz"], sm["pxz"].n), (sm["xnf"], sm["xnf"].n)], inc=sm["dbd"])
            P.emit("vector", lambda e: e.memset(
                v_da.rearrange("p (n t) -> p n t", n=NH)[:, :, 1:2], 1.0), inc=sm["dbd"])
        sep_done = sm["dbd"].n
        wdt_v = sm["wld"].n
        for ts in range(2):
            t0 = ts * TS
            for (src, dstb) in ((s_bn, v_bb), (s_cn, v_cb)):
                for n in range(NST):
                    def bcast(e, n=n, src=src, t0=t0):
                        return e.matmul(p_bc[:], s_oh16[:, n * 128:(n + 1) * 128],
                                        src[:, t0:t0 + TS], start=True, stop=True)
                    P.emit("tensor", bcast, waits=[(sm["nrm"], sm["nrm"].n),
                                                   (sm["bcb"], sm["bcb"].n)], inc=sm["bcb"])
                    P.emit("scalar", lambda e, n=n, dstb=dstb: _cp(e, 
                        dstb[:, n * TS:(n + 1) * TS], p_bc[:]),
                        waits=[(sm["bcb"], sm["bcb"].n), (sm["dbd"], sm["dbd"].n),
                               (sm["scn"], sm["scn"].n)], inc=sm["bcb"])
            bcv = sm["bcb"].n
            for d in range(ND):
                wpre = [(sm["scn"], xcr_marks[-2])] if len(xcr_marks) >= 2 else []
                P.emit("sync", lambda e, d=d, th=th, t0=t0: e.dma_start(
                    out=v_xcr[d % 2],
                    in_=xc_d[d * 128:(d + 1) * 128, th * TH + t0:th * TH + t0 + TS]),
                    waits=wpre, inc=sm["xcr"], inc_k=16)
                P.emit("sync", lambda e, d=d, th=th, t0=t0: e.dma_start(
                    out=v_zsr[d % 2],
                    in_=zs_d[d * 128:(d + 1) * 128, th * TH + t0:th * TH + t0 + TS]),
                    inc=sm["xcr"], inc_k=16)
                xcrv = sm["xcr"].n

                def mmdt(e, d=d, th=th, t0=t0):
                    for k in range(2):
                        ins = e.matmul(
                            p_dt[:], v_wdt[:, k * DH + d * 128:k * DH + (d + 1) * 128],
                            s_dtrn[th][:, k * TH + t0:k * TH + t0 + TS],
                            start=(k == 0), stop=(k == 1))
                    return ins
                P.emit("tensor", mmdt, waits=[(sm["wld"], wdt_v), (sm["nrm"], sm["nrm"].n),
                                              (sm["spd"], sm["spd"].n), (sm["bcb"], bcv)],
                       inc=sm["dtm"])
                dtv_ = sm["dtm"].n

                def sp(e, d=d):
                    e.activation(v_spt[:, 0:512], p_dt[:], AF.Abs, bias=s_dtb[:, d:d + 1])
                    e.activation(v_spt[:, 0:512], v_spt[:, 0:512], AF.Exp, scale=-1.0)
                    e.activation(v_spt[:, 0:512], v_spt[:, 0:512], AF.Ln, bias=1.0)
                    return e.activation(v_spt2[:, 0:512], p_dt[:], AF.Relu,
                                        bias=s_dtb[:, d:d + 1])
                P.emit("scalar", sp, waits=[(sm["dtm"], dtv_), (sm["dad"], sm["dad"].n)],
                       inc=sm["spd"])
                P.emit("vector", lambda e: e.tensor_add(v_dtv, v_spt2[:, 0:512], v_spt[:, 0:512]),
                       waits=[(sm["spd"], sm["spd"].n)], inc=sm["spd"])
                spv = sm["spd"].n
                P.emit("vector", lambda e, d=d: e.tensor_mul(v_u, v_dtv, v_xcr[d % 2]),
                       waits=[(sm["spd"], spv), (sm["xcr"], xcrv)], inc=sm["dbd"])

                for nh in range(2):
                    def mkda(e, d=d, nh=nh):
                        for nn in range(NH):
                            n = nh * NH + nn
                            ins = e.activation(
                                v_da.rearrange("p (n t) -> p n t", n=NH)[:, nn, 2:LEN],
                                v_dtv, AF.Exp, scale=s_aneg[:, d * NST + n:d * NST + n + 1])
                        return ins
                    wda = [(sm["scn"], scan_marks[-1])] if scan_marks else []
                    P.emit("scalar", mkda, waits=[(sm["spd"], spv),
                                                  (sm["setup"], SETUP_N)] + wda, inc=sm["dad"])

                    def mkdbx(e, d=d, nh=nh):
                        dst = v_dbx.rearrange("p (n t) -> p n t", n=NH)[:, :, 2:LEN]
                        ub = v_u.unsqueeze(1).broadcast_to([128, NH, TS])
                        bb = v_bb.rearrange("p (n t) -> p n t", n=NST)[:, nh * NH:(nh + 1) * NH, :]
                        e.tensor_tensor(dst, ub, bb, ALU.mult)
                        e.tensor_copy(
                            v_dbx.rearrange("p (n t) -> p n t", n=NH)[:, :, 0],
                            v_chi[:, d * NST + nh * NH:d * NST + nh * NH + NH])
                        return e.tensor_copy(
                            v_dbx.rearrange("p (n t) -> p n t", n=NH)[:, :, 1],
                            v_clo[:, d * NST + nh * NH:d * NST + nh * NH + NH])
                    P.emit("vector", mkdbx, waits=[(sm["bcb"], bcv), (sm["scn"], sm["scn"].n),
                                                   (sm["setup"], SETUP_N),
                                                   (sm["dbd"], sm["dbd"].n)], inc=sm["dbd"])
                    P.emit("vector", lambda e: e.tensor_tensor_scan(
                        v_h, v_da, v_dbx, 0.0, ALU.mult, ALU.add),
                        waits=[(sm["dad"], sm["dad"].n), (sm["dbd"], sep_done),
                               (sm["dbd"], sm["dbd"].n)], inc=sm["scn"])
                    scan_marks.append(sm["scn"].n)

                    def csaveA(e, d=d, nh=nh, pend=cs_pend[0]):
                        sl = slice(d * NST + nh * NH, d * NST + nh * NH + NH)
                        hl = v_h.rearrange("p (n t) -> p n t", n=NH)[:, :, LEN - 1]
                        ins = e.tensor_copy(v_chi[:, sl], hl)
                        if pend is not None:
                            psl, pslot = pend
                            ins = e.tensor_copy(v_clo[:, psl], v_lo32[:, pslot * NH:(pslot + 1) * NH])
                        return ins
                    P.emit("vector", csaveA, waits=[(sm["scn"], sm["scn"].n)], inc=sm["scn"])
                    cs_slot = (cs_slot + 1) % 2

                    def csby(e, d=d, nh=nh, slot=cs_slot):
                        sl = slice(d * NST + nh * NH, d * NST + nh * NH + NH)
                        hl = v_h.rearrange("p (n t) -> p n t", n=NH)[:, :, LEN - 1]
                        e.tensor_tensor(v_lo32[:, slot * NH:(slot + 1) * NH],
                                        hl, v_chi[:, sl], ALU.subtract)
                        hv = v_h.rearrange("p (n t) -> p n t", n=NH)[:, :, 2:LEN]
                        p3 = v_p.rearrange("p (n t) -> p n t", n=NH)
                        cb = v_cb.rearrange("p (n t) -> p n t", n=NST)[:, nh * NH:(nh + 1) * NH, :]
                        e.tensor_tensor(p3[:, :, 2:LEN], hv, cb, ALU.mult)
                        e.tensor_add(p3[:, 0:4, 2:LEN], p3[:, 0:4, 2:LEN], p3[:, 4:8, 2:LEN])
                        e.tensor_add(p3[:, 0:2, 2:LEN], p3[:, 0:2, 2:LEN], p3[:, 2:4, 2:LEN])
                        e.tensor_add(p3[:, 0:1, 2:LEN], p3[:, 0:1, 2:LEN], p3[:, 1:2, 2:LEN])
                        if nh == 0:
                            return e.tensor_copy(v_ys, p3[:, 0, 2:LEN])
                        e.tensor_add(v_ys, v_ys, p3[:, 0, 2:LEN])
                        e.scalar_tensor_tensor(v_ys, v_xcr[d % 2], s_dpar[:, d:d + 1],
                                               v_ys, ALU.mult, ALU.add)
                        return e.tensor_mul(v_yf, v_ys, v_zsr[d % 2])
                    P.emit("vector", csby,
                           waits=[(sm["scn"], sm["scn"].n)] +
                                 ([(sm["yrd"], sm["yrd"].n)] if nh == 1 else []),
                           inc=sm["scn"])
                    cs_pend[0] = (slice(d * NST + nh * NH, d * NST + nh * NH + NH), cs_slot)
                xcr_marks.append(sm["scn"].n)
                P.emit("sync", lambda e, d=d, th=th, t0=t0: e.dma_start(
                    out=yag_in[th][d * 128:(d + 1) * 128, t0:t0 + TS], in_=v_yf),
                    waits=[(sm["scn"], sm["scn"].n)], inc=sm["yrd"], inc_k=16)
            def csflush(e, pend=cs_pend[0]):
                psl, pslot = pend
                return e.tensor_copy(v_clo[:, psl], v_lo32[:, pslot * NH:(pslot + 1) * NH])
            P.emit("vector", csflush, waits=[(sm["scn"], sm["scn"].n)], inc=sm["scn"])
            cs_pend[0] = None
            if DEBUG and th == 0:
                mids = [(f"d_chi{ts}", v_chi), (f"d_clo{ts}", v_clo)]
                if ts == 0:
                    mids += [("d_da0", v_da), ("d_dbx0", v_dbx), ("d_h0", v_h)]
                for k, (nm, srcT) in enumerate(mids):
                    P.emit("gpsimd", lambda e, nm=nm, srcT=srcT: e.dma_start(
                        out=dbg[nm], in_=srcT),
                        waits=[(sm["scn"], sm["scn"].n)] if k == 0 else [],
                        inc=sm["outd"], inc_k=16)
                P.emit("vector", lambda e: e.tensor_copy(v_dtv, v_dtv),
                       waits=[(sm["outd"], sm["outd"].n)], inc=sm["scn"])
        P.emit("gpsimd", lambda e, th=th: e.collective_compute(
            "AllGather", ALU.bypass, replica_groups=RG,
            ins=[yag_in[th].opt()], outs=[yag_out[th].opt()]),
            waits=[(sm["yrd"], sm["yrd"].n)], inc=sm["cc"])

    omm_hist = {}

    def phase_C(ccv):
        for tb in range(2):
            def ldy(e, tb=tb):
                v = e.value_load(s_flag[0:1, 0:1])
                for half in range(2):
                    src = yag_out[half][:, tb * 512:(tb + 1) * 512].rearrange(
                        "(k p) t -> p k t", p=128)
                    e.dma_start(out=v_yres[:].rearrange("p (k t) -> p k t", k=2 * ND),
                                in_=src, cond=(v == half)).then_inc(sm["yrl"].h, 16)
            w = [(sm["cc"], ccv), (sm["scn"], sm["scn"].n)]
            if tb > 0:
                w.append((sm["omm"], omm_hist[(tb - 1, KC - 1)]))
            wt = tuple((s.h, v) for (s, v) in w)

            def ldy_run(e, ldy=ldy, wt=wt):
                for sh, v in wt:
                    e.wait_ge(sh, v)
                ldy(e)
            P.ops["sync"].append(ldy_run)
            sm["yrl"].n += 16 * 2
            ylv = sm["yrl"].n
            for hch in range(KC):
                wo = v_wo[hch % 2]
                key = (tb, hch - 2) if hch >= 2 else ((tb - 1, KC - 2 + hch) if tb > 0 else None)
                wfree = [(sm["omm"], omm_hist[key])] if key in omm_hist else [
                    (sm["scn"], sm["scn"].n)]
                P.emit("gpsimd", dma(wo.rearrange("p (k c) -> p k c", k=2 * ND), woT[hch]),
                       waits=wfree, inc=sm["wod"], inc_k=16)
                wv = sm["wod"].n

                def mmo(e, wo=wo, hch=hch):
                    pp = p_o[hch % 2]
                    for k in range(2 * ND):
                        ins = e.matmul(pp[:], wo[:, k * 128:(k + 1) * 128],
                                       v_yres[:, k * 512:(k + 1) * 512],
                                       start=(k == 0), stop=(k == 2 * ND - 1))
                    return ins
                P.emit("tensor", mmo, waits=[(sm["wod"], wv), (sm["yrl"], ylv),
                                             (sm["oev"], sm["oev"].n)], inc=sm["omm"])
                omm_hist[(tb, hch)] = sm["omm"].n
                P.emit("scalar", lambda e, hch=hch: _cp(e, v_sev, p_o[hch % 2][:]),
                       waits=[(sm["omm"], sm["omm"].n), (sm["oev"], sm["oev"].n)],
                       inc=sm["oev"])
                P.emit("sync", lambda e, hch=hch, tb=tb: e.dma_start(
                    out=s_d[hch * 128:(hch + 1) * 128, tb * 512:(tb + 1) * 512], in_=v_sev),
                    waits=[(sm["oev"], sm["oev"].n)], inc=sm["oev"], inc_k=16)

    def phase_FFN():
        sv, sa, sp = sm["ffv"], sm["ffa"], sm["ffp"]
        phase_LN(0, x_mine, ffn=True)
        xnv = sm["xnf"].n
        if FFN_STAGE < 2:
            return
        for c in range(2):
            def mml(e, c=c):
                for k in range(KC):
                    ins = e.matmul(p_dt[0:24, :], s_wres[:, k * 24:(k + 1) * 24],
                                   v_xnm[:, k * TH + c * 512:k * TH + (c + 1) * 512],
                                   start=(k == 0), stop=(k == KC - 1))
                return ins
            P.emit("tensor", mml, waits=[(sm["xnf"], xnv), (sa, sa.n),
                                         (sm["nrm"], sm["nrm"].n)], inc=sp)
            P.emit("scalar", lambda e, c=c: _cp(e, 
                v_lgst16[0:24, c * 512:(c + 1) * 512], p_dt[0:24, :]),
                waits=[(sp, sp.n)], inc=sa)
        lgv = sa.n
        for i in range(8):
            P.emit("tensor", lambda e, i=i: e.transpose(
                p_tp[:, 0:24], v_lgst16[0:24, i * 128:(i + 1) * 128], s_id[0:24, 0:24]),
                waits=[(sa, lgv), (sa, sa.n)], inc=sp)
            P.emit("scalar", lambda e, i=i: _cp(e, 
                v_lg[:, i * 24:(i + 1) * 24], p_tp[:, 0:24]),
                waits=[(sp, sp.n)], inc=sa)
        if FFN_STAGE < 3:
            return
        P.emit("scalar", lambda e: e.activation(s_small[0:1, 25:26], s_small[0:1, 24:25],
                                                AF.Sigmoid), inc=sa)

        for (dst, srcsl) in ((v_baseb[:, 0:24], s_small[0:1, 0:24]),
                             (v_eyeb[:, 0:16], s_small[0:1, 32:48]),
                             (v_gcol[:, 0:1], s_small[0:1, 25:26])):
            P.emit("tensor", lambda e, srcsl=srcsl: e.matmul(
                p_bc[0:128, 0:srcsl.shape[1]], s_o1r[:], srcsl, start=True, stop=True),
                waits=[(sa, sa.n)], inc=sp)
            P.emit("scalar", lambda e, dst=dst, srcsl=srcsl: _cp(
                e, dst, p_bc[0:128, 0:srcsl.shape[1]]),
                waits=[(sp, sp.n)], inc=sa)
        bsv = sa.n

        def mkz(e):
            lg3 = v_lg.rearrange("p (i e) -> p i e", i=8)
            for i in range(8):
                e.tensor_scalar_mul(lg3[:, i, :], lg3[:, i, :], v_rs2[:, i:i + 1])
            bb = v_baseb[:, 0:24].unsqueeze(1).broadcast_to([128, 8, 24])
            e.tensor_tensor(lg3, lg3, bb, ALU.add)
            return e.tensor_scalar_mul(
                v_zt.rearrange("p (i g) -> p i g", i=8), lg3[:, :, 0:16], 1.0 / SINK_TAU)
        P.emit("vector", mkz, waits=[(sa, bsv), (sv, sv.n)], inc=sv)
        if FFN_STAGE < 4:
            return
        P.emit("vector", lambda e: e.memset(v_uu[:, 0:32], 0.0), inc=sv)
        P.emit("vector", lambda e: e.memset(v_vv[:, 0:32], 0.0), inc=sv)
        z4 = v_zt.rearrange("p (i j k) -> p i j k", i=8, j=4)
        e4 = v_e1.rearrange("p (i j k) -> p i j k", i=8, j=4)
        e4T = v_e1.rearrange("p (i j k) -> p i k j", i=8, j=4)
        u4 = v_uu[:, 0:32].rearrange("p (i j) -> p i j", i=8)
        v4 = v_vv[:, 0:32].rearrange("p (i j) -> p i j", i=8)
        m4 = v_m1[:, 0:32].rearrange("p (i j) -> p i j", i=8)
        s4 = v_s1[:, 0:32].rearrange("p (i j) -> p i j", i=8)
        SINK_HALF = globals().get('SINK_HALF', False)
        for it in range(globals().get('SINK_IT_OVR', SINK_ITERS)):
            def ustep_a(e):
                vb = v4.unsqueeze(2).broadcast_to([128, 8, 4, 4])
                e.tensor_tensor(e4, z4, vb, ALU.add)
                return e.reduce_max(m4, e4, axis=AX.X)
            P.emit("vector", ustep_a, waits=[(sa, sa.n), (sv, sv.n)], inc=sv)

            def ustep_b(e):
                mb = m4.unsqueeze(3).broadcast_to([128, 8, 4, 4])
                return e.tensor_tensor(e4, e4, mb, ALU.subtract)
            P.emit("vector", ustep_b, waits=[(sv, sv.n)], inc=sv)
            P.emit("scalar", lambda e: e.activation(v_e1[:, 0:128], v_e1[:, 0:128], AF.Exp),
                   waits=[(sv, sv.n)], inc=sa)
            P.emit("vector", lambda e: e.reduce_sum(s4, e4, axis=AX.X),
                   waits=[(sa, sa.n)], inc=sv)
            P.emit("scalar", lambda e: e.activation(v_s1[:, 0:32], v_s1[:, 0:32], AF.Ln),
                   waits=[(sv, sv.n)], inc=sa)
            P.emit("vector", lambda e: e.scalar_tensor_tensor(
                v_uu[:, 0:32], v_m1[:, 0:32], -1.0, v_s1[:, 0:32], ALU.mult, ALU.subtract),
                waits=[(sa, sa.n)], inc=sv)

            if SINK_HALF:
                continue

            def vstep_a(e):
                ub = u4.unsqueeze(3).broadcast_to([128, 8, 4, 4])
                e.tensor_tensor(e4, z4, ub, ALU.add)
                return e.reduce_max(m4, e4T, axis=AX.X)
            P.emit("vector", vstep_a, waits=[(sv, sv.n)], inc=sv)

            def vstep_b(e):
                mb = m4.unsqueeze(3).broadcast_to([128, 8, 4, 4])
                return e.tensor_tensor(e4T, e4T, mb, ALU.subtract)
            P.emit("vector", vstep_b, waits=[(sv, sv.n)], inc=sv)
            P.emit("scalar", lambda e: e.activation(v_e1[:, 0:128], v_e1[:, 0:128], AF.Exp),
                   waits=[(sv, sv.n)], inc=sa)
            P.emit("vector", lambda e: e.reduce_sum(s4, e4T, axis=AX.X),
                   waits=[(sa, sa.n)], inc=sv)
            P.emit("scalar", lambda e: e.activation(v_s1[:, 0:32], v_s1[:, 0:32], AF.Ln),
                   waits=[(sv, sv.n)], inc=sa)
            P.emit("vector", lambda e: e.scalar_tensor_tensor(
                v_vv[:, 0:32], v_m1[:, 0:32], -1.0, v_s1[:, 0:32], ALU.mult, ALU.subtract),
                waits=[(sa, sa.n)], inc=sv)

        if FFN_STAGE < 5:
            return

        def hfin(e):
            ub = u4.unsqueeze(3).broadcast_to([128, 8, 4, 4])
            vb = v4.unsqueeze(2).broadcast_to([128, 8, 4, 4])
            e.tensor_tensor(e4, z4, ub, ALU.add)
            return e.tensor_tensor(e4, e4, vb, ALU.add)
        P.emit("vector", hfin, waits=[(sv, sv.n)], inc=sv)
        P.emit("scalar", lambda e: e.activation(v_e1[:, 0:128], v_e1[:, 0:128], AF.Exp),
               waits=[(sv, sv.n)], inc=sa)

        def blend1(e):
            e.tensor_scalar_mul(v_e1[:, 0:128], v_e1[:, 0:128], v_gcol[:, 0:1])
            return e.tensor_scalar(v_omg[:, 0:1], v_gcol[:, 0:1], -1.0, 1.0, ALU.mult, ALU.add)
        P.emit("vector", blend1, waits=[(sa, sa.n)], inc=sv)

        def blend2(e):
            return e.scalar_tensor_tensor(
                v_hres16[:, 0:128].rearrange("p (i e) -> p i e", i=8),
                v_eyeb[:, 0:16].unsqueeze(1).broadcast_to([128, 8, 16]),
                v_omg[:, 0:1],
                v_e1[:, 0:128].rearrange("p (i e) -> p i e", i=8), ALU.mult, ALU.add)
        P.emit("vector", blend2, waits=[(sv, sv.n)], inc=sv)

        lg3s = v_lg.rearrange("p (i e) -> p i e", i=8)
        for (c0, off) in ((16, 0), (20, 32)):
            P.emit("vector", lambda e, c0=c0: e.tensor_copy(
                v_sp1[:, 0:32].rearrange("p (i j) -> p i j", i=8),
                lg3s[:, :, c0:c0 + 4]),
                waits=[(sv, sv.n), (sa, sa.n)], inc=sv)
            P.emit("vector", lambda e: e.reduce_max(
                v_smx[:, 0:8], v_sp1[:, 0:32].rearrange("p (i j) -> p i j", i=8),
                axis=AX.X), waits=[(sv, sv.n)], inc=sv)
            P.emit("vector", lambda e: e.tensor_tensor(
                v_sp1[:, 0:32].rearrange("p (i j) -> p i j", i=8),
                v_sp1[:, 0:32].rearrange("p (i j) -> p i j", i=8),
                v_smx[:, 0:8].unsqueeze(2).broadcast_to([128, 8, 4]),
                ALU.subtract), waits=[(sv, sv.n)], inc=sv)
            P.emit("scalar", lambda e: e.activation(v_sp2[:, 0:32], v_sp1[:, 0:32], AF.Exp),
                   waits=[(sv, sv.n)], inc=sa)

            P.emit("vector", lambda e: e.reduce_sum(
                v_smx[:, 0:8], v_sp2[:, 0:32].rearrange("p (i j) -> p i j", i=8),
                axis=AX.X), waits=[(sa, sa.n)], inc=sv)
            P.emit("vector", lambda e: e.reciprocal(v_smx[:, 0:8], v_smx[:, 0:8]),
                   waits=[(sv, sv.n)], inc=sv)
            P.emit("vector", lambda e, off=off: e.tensor_tensor(
                hpp16[:, off:off + 32].rearrange("p (i j) -> p i j", i=8),
                v_sp2[:, 0:32].rearrange("p (i j) -> p i j", i=8),
                v_smx[:, 0:8].unsqueeze(2).broadcast_to([128, 8, 4]),
                ALU.mult), waits=[(sv, sv.n)], inc=sv)
        hv = sv.n
        if FFN_STAGE < 6:
            return
        for i in range(8):
            def tph(e, i=i):
                e.transpose(p_tp[0:16, 0:128], v_hres16[:, i * 16:(i + 1) * 16], s_id[:])
                e.transpose(p_tp[0:4, 128:256], hpp16[:, i * 4:(i + 1) * 4], s_id[:])
                return e.transpose(p_tp[0:4, 256:384], hpp16[:, 32 + i * 4:32 + (i + 1) * 4],
                                   s_id[:])
            P.emit("tensor", tph, waits=[(sv, hv), (sa, sa.n), (sm["scn"], sm["scn"].n)],
                   inc=sp)

            def evh(e, i=i):
                _cp(e, s_bn[:, i * 128:(i + 1) * 128], p_tp[0:16, 0:128])
                _cp(e, s_cn[0:4, i * 128:(i + 1) * 128], p_tp[0:4, 128:256])
                return _cp(e, s_hpo[0:4, i * 128:(i + 1) * 128], p_tp[0:4, 256:384])
            P.emit("scalar", evh, waits=[(sp, sp.n)], inc=sa)
        hrv = sa.n
        P.emit("gpsimd", dma(v_wbr.rearrange("p (n e) -> p n e", n=8), wbr),
               waits=[(sm["omm"], sm["omm"].n), (sm["xnf"], sm["xnf"].n),
                      (sm["tpd"], sm["tpd"].n), (sm["ffp"], sm["ffp"].n)],
               inc=sm["ffw"], inc_k=16)

        def emit_bcast_row(src, r, dst_off, extra=(), kdim=16):
            def mm2(e, src=src, r=r, kdim=kdim):
                e.matmul(p_bc[:], s_oh16[0:kdim, r * 128:(r + 1) * 128],
                         src[0:kdim, 0:512], start=True, stop=True)
                return e.matmul(p_dt[:], s_oh16[0:kdim, r * 128:(r + 1) * 128],
                                src[0:kdim, 512:1024], start=True, stop=True)
            P.emit("tensor", mm2, waits=[(sa, hrv), (sa, sa.n)] + list(extra), inc=sp)

            def ev2(e, dst_off=dst_off):
                _cp(e, v_hb[:, dst_off:dst_off + 512], p_bc[:])
                return _cp(e, v_hb[:, dst_off + 512:dst_off + TH], p_dt[:])
            P.emit("scalar", ev2, waits=[(sp, sp.n), (sv, sv.n)], inc=sa)

        if FFN_STAGE < 7:
            return
        for j in range(4):
            emit_bcast_row(s_cn, j, j * TH)
        bpv = sa.n
        if FFN_STAGE < 7.5:
            return
        for ci in range(8):
            def xpre(e, ci=ci):
                acc = v_xpre[:, ci * TH:(ci + 1) * TH]
                e.tensor_tensor(acc, v_xnm[:, ci * TH:(ci + 1) * TH],
                                v_hb[:, 0:TH], ALU.mult)
                for j in range(1, 4):
                    e.tensor_tensor(v_yf2, v_xnm[:, (j * 8 + ci) * TH:(j * 8 + ci + 1) * TH],
                                    v_hb[:, j * TH:(j + 1) * TH], ALU.mult)
                    ins = e.tensor_add(acc, acc, v_yf2)
                return ins
            P.emit("vector", xpre, waits=[(sa, bpv), (sm["wod"], sm["wod"].n)], inc=sv)
        xpv = sv.n
        if FFN_STAGE < 8:
            return
        for co in range(8):
            for tc in range(2):
                def mmbr(e, co=co, tc=tc):
                    pp = p_o[0]
                    for k in range(8):
                        ins = e.matmul(pp[:],
                                       v_wbr[:, k * 1024 + co * 128:k * 1024 + (co + 1) * 128],
                                       v_xpre[:, k * TH + tc * 512:k * TH + (tc + 1) * 512],
                                       start=(k == 0), stop=(k == 7))
                    return ins
                P.emit("tensor", mmbr, waits=[(sv, xpv), (sm["ffw"], sm["ffw"].n),
                                              (sa, sa.n), (sm["oev"], sm["oev"].n)], inc=sp)
                P.emit("scalar", lambda e, co=co, tc=tc: _cp(e, 
                    v_ybr[:, co * TH + tc * 512:co * TH + (tc + 1) * 512],
                    p_o[0][:]), waits=[(sp, sp.n)], inc=sa)
        ybv = sa.n
        if FFN_STAGE < 9:
            return
        for i in range(4):
            for j in range(4):
                emit_bcast_row(s_bn, i * 4 + j, j * TH, extra=[(sv, sv.n), (sa, sa.n)])
            emit_bcast_row(s_hpo, i, 4 * TH, extra=[(sa, sa.n)], kdim=4)
            brv = sa.n
            for ci in range(8):
                fc = i * 8 + ci
                P.emit("sync", lambda e, fc=fc: e.dma_start(
                    out=v_sld, in_=s_d[fc * 128:(fc + 1) * 128, :]),
                    waits=[(sm["oev"], sm["oev"].n), (sm["fgd"], sm["fgd"].n)],
                    inc=sm["fgd"], inc_k=16)
                sldv = sm["fgd"].n

                def mix(e, ci=ci):
                    acc = v_sgc
                    e.tensor_tensor(acc, v_xnm[:, ci * TH:(ci + 1) * TH],
                                    v_hb[:, 0:TH], ALU.mult)
                    for j in range(1, 4):
                        e.tensor_tensor(v_yf2, v_xnm[:, (j * 8 + ci) * TH:(j * 8 + ci + 1) * TH],
                                        v_hb[:, j * TH:(j + 1) * TH], ALU.mult)
                        e.tensor_add(acc, acc, v_yf2)
                    e.tensor_tensor(v_yf2, v_ybr[:, ci * TH:(ci + 1) * TH],
                                    v_hb[:, 4 * TH:5 * TH], ALU.mult)
                    return e.tensor_add(acc, acc, v_yf2)
                P.emit("vector", mix, waits=[(sa, brv), (sm["ffp"], sm["ffp"].n)], inc=sv)
                P.emit("scalar", lambda e: e.activation(v_sgc, v_sgc, AF.Silu),
                       waits=[(sv, sv.n)], inc=sa)
                P.emit("vector", lambda e: e.tensor_mul(v_sgc, v_sgc, v_sld),
                       waits=[(sa, sa.n), (sm["fgd"], sldv)], inc=sv)
                sgv = sv.n
                for g in range(2):
                    def tps(e, g=g):
                        for jj in range(4):
                            tt = g * 4 + jj
                            ins = e.transpose(p_tp[:, jj * 128:(jj + 1) * 128],
                                              v_sgc[:, tt * 128:(tt + 1) * 128], s_id[:])
                        return ins
                    P.emit("tensor", tps, waits=[(sv, sgv), (sa, sa.n),
                                                 (sm["fgd"], sm["fgd"].n)], inc=sp)
                    P.emit("scalar", lambda e: _cp(e, v_tev[:, 0:512], p_tp[:, 0:512]),
                           waits=[(sp, sp.n), (sm["fgd"], sm["fgd"].n)], inc=sa)
                    P.emit("sync", lambda e, fc=fc, g=g: e.dma_start(
                        out=sgT_d[g * 512:(g + 1) * 512, fc * 128:(fc + 1) * 128]
                        .rearrange("(a p) c -> p a c", p=128),
                        in_=v_tev[:, 0:512].rearrange("p (a c) -> p a c", a=4)),
                        waits=[(sa, sa.n)], inc=sm["fgd"], inc_k=16)

    def phase_FIN():
        fgv = sm["fgd"].n
        for i in range(8):
            P.emit("sync", lambda e, i=i: e.dma_start(
                out=v_fin, in_=x_mine[i * 128:(i + 1) * 128, :]),
                waits=[(sm["fgd"], fgv), (sm["outd"], sm["outd"].n)],
                inc=sm["outd"], inc_k=16)
            P.emit("sync", lambda e, i=i: e.dma_start(
                out=v_sq, in_=sgT_d[i * 128:(i + 1) * 128, :]),
                waits=[(sm["ffv"], sm["ffv"].n), (sm["ffa"], sm["ffa"].n)],
                inc=sm["outd"], inc_k=16)
            fv = sm["outd"].n
            P.emit("vector", lambda e: e.tensor_add(v_fin, v_fin, v_sq),
                   waits=[(sm["outd"], fv)], inc=sm["fgd"])
            P.emit("sync", lambda e, i=i: e.dma_start(
                out=out[i * 128:(i + 1) * 128, :], in_=v_fin),
                waits=[(sm["fgd"], sm["fgd"].n)], inc=sm["outd"], inc_k=16)
        P.wait("sync", (sm["outd"], sm["outd"].n))
        P.wait("gpsimd", (sm["outd"], sm["outd"].n))

    phase_LN(0, x_b)
    phase_A(0)
    cc_a0 = sm["cc"].n
    if UPTO >= 2:
        phase_LN(1, x_b)
        phase_A(1)
    cc_a1 = sm["cc"].n
    if UPTO >= 3:
        phase_N(0, cc_a0)
        phase_B(0)
    if UPTO >= 4:
        phase_N(1, cc_a1)
        phase_B(1)
    cc_y = sm["cc"].n
    if UPTO >= 5:
        phase_C(cc_y)
    if UPTO >= 6:
        phase_FFN()
    if UPTO >= 7:
        phase_FIN()
    else:
        allw = [(s, s.n) for s in sm.values()]
        for i in range(8):
            P.emit("sync", lambda e, i=i: e.dma_start(
                out=out[i * 128:(i + 1) * 128, :], in_=v_sq.bitcast(F32)[:, 0:H // 2]
                if False else R5.bitcast(F32)[:, 0:H]),
                waits=allw if i == 0 else [], inc=sm["outd"], inc_k=16)
        P.wait("sync", (sm["outd"], sm["outd"].n))
        P.wait("gpsimd", (sm["outd"], sm["outd"].n))

    if DEBUG:
        allw = [(s, s.n) for s in sm.values()]
        pairs = [("d_xc", xc_d), ("d_zs", zs_d), ("d_ssm0", ssm_out[0]),
                 ("d_ssm1", ssm_out[1]), ("d_yag0", yag_out[0]), ("d_yag1", yag_out[1]),
                 ("d_sd", s_d), ("d_sgT", sgT_d), ("d_hres", s_bn[:]),
                 ("d_hpre", s_cn[:]), ("d_hpo", s_hpo[:]), ("d_xpre", v_xpre),
                 ("d_ybr", v_ybr), ("d_rs2", v_rs2), ("d_lg", v_lg),
                 ("d_uu", v_uu[:, 0:32]), ("d_vv", v_vv[:, 0:32]),
                 ("d_h16", v_hres16[:, 0:128]), ("d_hpp", hpp16[:, 0:64]),
                 ("d_zt", v_zt), ("d_m1", v_m1[:, 0:32]), ("d_s1", v_s1[:, 0:32]),
                 ("d_e1", v_e1),
                 ("d_sb", R5.bitcast(F32)[:, 384:432]),
                 ("d_sp1", v_sp1[:, 0:32]), ("d_sp2", v_sp2[:, 0:32]),
                 ("d_smx", v_smx[:, 0:8]),
                 ("d_chi", v_chi), ("d_clo", v_clo),
                 ("d_bb", v_bb), ("d_cb", v_cb), ("d_u", v_u),
                 ("d_da", v_da), ("d_h", v_h), ("d_dbx", v_dbx)]
        for k, (nm, srcT) in enumerate(pairs):
            P.emit("gpsimd", lambda e, nm=nm, srcT=srcT: e.dma_start(
                out=dbg[nm], in_=srcT), waits=allw if k == 0 else [],
                inc=sm["outd"], inc_k=16)
        P.wait("gpsimd", (sm["outd"], sm["outd"].n))

    with nc.Block() as block:
        for eng in ENGS:
            def make(eng):
                def body(e):
                    for f in P.ops[eng]:
                        f(e)
                return body
            getattr(block, eng)(make(eng))

    ctx.close()
    return nc


_NC_CACHE = {}


def _stage(inputs, j):
    ip = inputs
    W = ip["L0_attn_in_proj_w"]
    Wx = W[:, j * DH:(j + 1) * DH]
    Wz = W[:, DIN + j * DH:DIN + (j + 1) * DH]

    def blk(Wp):
        return np.ascontiguousarray(Wp.reshape(KC, 128, ND, 128).transpose(2, 1, 0, 3))
    wxz = np.concatenate([blk(Wx), blk(Wz)], axis=0)
    wxpT = np.ascontiguousarray(
        ip["L0_attn_x_proj_w"].T[j * DH:(j + 1) * DH, :]
        .reshape(ND, 128, R + 2 * NST).transpose(1, 0, 2))
    wdtT = np.ascontiguousarray(
        ip["L0_attn_dt_proj_w"][j * DH:(j + 1) * DH, :].T
        .reshape(2, 128, DH).transpose(1, 0, 2))
    woT = np.ascontiguousarray(
        ip["L0_attn_out_proj_w"].T.reshape(2 * ND, 128, KC, 128).transpose(2, 1, 0, 3))

    def cols(v):
        return np.ascontiguousarray(v[j * DH:(j + 1) * DH].reshape(ND, 128).T)

    convw = np.ascontiguousarray(
        ip["L0_attn_conv_w"][j * DH:(j + 1) * DH].reshape(ND, 128, 4)
        .transpose(1, 0, 2).reshape(128, ND * 4))
    alog = np.ascontiguousarray(
        ip["L0_attn_A_log"][j * DH:(j + 1) * DH].reshape(ND, 128, NST)
        .transpose(1, 0, 2).reshape(128, ND * NST))
    wres_cat = np.concatenate([ip["L0_ffn_W_res"], ip["L0_ffn_W_pre"],
                               ip["L0_ffn_W_post"]], axis=0)
    wresT = np.ascontiguousarray(wres_cat.T.reshape(KC, 128, 24).transpose(1, 0, 2))
    wbr = np.ascontiguousarray(
        ip["L0_ffn_W_branch"].reshape(8, 128, 1024).transpose(1, 0, 2))
    oh = np.zeros((16, 16, 128), np.float16)
    for n in range(16):
        oh[n, n, :] = 1.0
    hb = np.concatenate([ip["L0_ffn_H_res_base"].ravel(),
                         ip["L0_ffn_H_pre_base"].ravel(),
                         ip["L0_ffn_H_post_base"].ravel()]).reshape(1, 24)
    return dict(
        wxz=wxz, wxpT=wxpT, wdtT=wdtT, woT=woT, convw=convw,
        convb=cols(ip["L0_attn_conv_b"]).copy(),
        dparam=cols(ip["L0_attn_D_param"]).copy(),
        dtbias=cols(ip["L0_attn_dt_bias"]).copy(),
        alog=alog,
        dtlnw=np.ascontiguousarray(ip["L0_attn_dt_ln_w"].reshape(2, 128).T),
        blnw=np.ascontiguousarray(ip["L0_attn_B_ln_w"].reshape(16, 1)),
        clnw=np.ascontiguousarray(ip["L0_attn_C_ln_w"].reshape(16, 1)),
        wresT=wresT, wbr=wbr,
        hbases=np.ascontiguousarray(hb, dtype=np.float32),
        gate=np.ascontiguousarray(ip["L0_ffn_gate"].reshape(1, 1)),
        eye4=np.ascontiguousarray(np.eye(4, dtype=np.float32).reshape(1, 16)),
        ident=np.eye(128, dtype=np.float16),
        ones128c=np.ones((128, 1), np.float32),
        ones16c=np.ones((16, 1), np.float32),
        ones1r=np.ones((1, 128), np.float32),
        oh16=np.ascontiguousarray(oh.reshape(16, 16 * 128)),
        flagj=np.array([[j, 0]], np.int32),
    )


def kernel(**inputs):
    inputs = {k: np.asarray(v) for k, v in inputs.items()}
    x = np.asarray(inputs["L0_attn_hidden_states"], dtype=np.float32)
    if "nc" not in _NC_CACHE:
        _NC_CACHE["nc"] = build_nc()
    nc = _NC_CACHE["nc"]
    staged = [_stage(inputs, j) for j in range(2)]
    in_maps = []
    for c in range(8):
        b, j = c // 2, c % 2
        m = dict(staged[j])
        m["x_b"] = np.ascontiguousarray(x[b])
        m["x_mine"] = np.ascontiguousarray(x[b, j * TH:(j + 1) * TH])
        in_maps.append(m)
    res = run_bass_kernel_spmd(nc, in_maps, core_ids=list(range(8)))
    outp = np.empty((B_SZ, T, H), np.float32)
    for c in range(8):
        b, j = c // 2, c % 2
        outp[b, j * TH:(j + 1) * TH] = np.asarray(res.results[c]["out"])
    if DEBUG:
        kernel.debug_results = res.results
    return outp


if __name__ == "__main__":
    nc = build_nc()
    print("built ok")
